# version 11
# baseline (speedup 1.0000x reference)
import numpy as np
import ml_dtypes
from contextlib import ExitStack

import concourse.bass as bass
import concourse.mybir as mybir
from concourse.bass_utils import run_bass_kernel_spmd

F32 = mybir.dt.float32
BF16 = mybir.dt.bfloat16
AF = mybir.ActivationFunctionType
ALU = mybir.AluOpType
AX = mybir.AxisListType

T = 2048
D = 1024
NT = 16
DC = 8
H = 16
DK = 64
CH = 64
NCH = T // CH
NE = 8
HID = 4096
HC = 32
CAP = 5120
NCORES = 8
NSTA = 15
NSTF = 11
EPS_LN = 1e-5
EPS_DEN = 1e-6


class SC:
    def __init__(self, handle):
        self.h = handle
        self.n = 0


class DQ:

    def __init__(self, sems):
        self.sems = sems
        self.k = len(sems)
        self.i = 0

    def tok(self, token):
        j = token // 16 - 1
        return (self.sems[j % self.k], 16 * (j // self.k + 1))

    def snapshot(self):
        return [(s, s.n) for s in self.sems]


class Build:

    def __init__(self, debug=False):
        self.nc = bass.Bass("TRN2", target_bir_lowering=False, debug=False)
        self.debug = debug
        self.ops = {k: [] for k in ("t", "v", "s", "g", "y")}
        self.auto = {}
        self.cnt = {"v": 0, "s": 0, "g": 0}
        self.stack = ExitStack()
        self.dbg = {}

    def op(self, eng, build_fn, waits=(), incs=()):
        my_idx = self.cnt[eng]
        self.cnt[eng] = my_idx + 1
        waits, incs = tuple(self.norm_waits(waits)), tuple(incs)
        a = self.auto[eng]

        def f(e, _w=waits, _i=incs, _k=my_idx, _fn=build_fn, _a=a):
            for sc, val in _w:
                if val > 0:
                    e.wait_ge(sc.h, val)
            if _k > 0:
                e.wait_ge(_a.h, _k)
            inst = _fn(e)
            inst.then_inc(_a.h, 1)
            for sc, k in _i:
                inst.then_inc(sc.h, k)

        self.ops[eng].append(f)

    def raw(self, eng, build_fn, waits=(), incs=()):
        waits, incs = tuple(self.norm_waits(waits)), tuple(incs)

        def f(e, _w=waits, _i=incs, _fn=build_fn):
            for sc, val in _w:
                if val > 0:
                    e.wait_ge(sc.h, val)
            inst = _fn(e)
            if inst is not None:
                for sc, k in _i:
                    inst.then_inc(sc.h, k)

        self.ops[eng].append(f)

    def wait(self, eng, sc, val=None):
        if isinstance(sc, DQ):
            assert val is not None
            if isinstance(val, list):
                for s, v in val:
                    self.wait(eng, s, v)
                return
            sc, val = sc.tok(val)
        v = sc.n if val is None else val
        if v > 0:
            def f(e, _s=sc, _v=v):
                e.wait_ge(_s.h, _v)
            self.ops[eng].append(f)

    def wait_snap(self, eng, snap):
        for s, v in snap:
            self.wait(eng, s, v)

    def mark(self, eng):
        return (self.auto[eng], self.cnt[eng])

    def norm_waits(self, waits):
        out = []
        for w in waits:
            if isinstance(w[0], DQ):
                out.append(w[0].tok(w[1]))
            else:
                out.append(w)
        return out

    def dma(self, eng, q, out, in_, accum=None, waits=(), indirect_out=None,
            indirect_in=None, bounds=None, elem_off=0):
        kw = {}
        if accum is not None:
            kw["accum_op"] = accum
        waits = tuple(self.norm_waits(waits))
        j = q.i
        q.i += 1
        s = q.sems[j % q.k]
        gate = s.n
        s.n += 16
        ind = (indirect_out is not None) or (indirect_in is not None)

        def f(e, _w=waits, _o=out, _i=in_, _kw=kw, _s=s, _g=gate,
              _io=indirect_out, _ii=indirect_in, _b=bounds, _eo=elem_off):
            for sc, val in _w:
                if val > 0:
                    e.wait_ge(sc.h, val)
            if _g > 0:
                e.wait_ge(_s.h, _g)
            if _io is not None or _ii is not None:
                ikw = {}
                if _eo:
                    ikw["element_offset"] = _eo
                if _b is not None:
                    ikw["bounds_check"] = _b
                    ikw["oob_is_err"] = False
                e.indirect_dma_start(
                    out=_o,
                    out_offset=(bass.IndirectOffsetOnAxis(_io, 0) if _io is not None else None),
                    in_=_i,
                    in_offset=(bass.IndirectOffsetOnAxis(_ii, 0) if _ii is not None else None),
                    **ikw).then_inc(_s.h, 16)
            else:
                e.dma_start(out=_o, in_=_i, **_kw).then_inc(_s.h, 16)

        self.ops[eng].append(f)
        return 16 * (j + 1)

    def scratch(self, name, shape, dtype):
        kind = "ExternalOutput" if self.debug else "Internal"
        t = self.nc.dram_tensor(name, list(shape), dtype, kind=kind)
        if self.debug:
            self.dbg[name] = t
        return t


def _bf(x):
    return np.asarray(x, dtype=ml_dtypes.bfloat16)


def make_constants():
    c = {}
    c["ident"] = _bf(np.eye(128, dtype=np.float32))
    c["lstrict"] = _bf(np.tril(np.ones((128, 128), np.float32), -1))
    c["ones_bf"] = _bf(np.ones((1, 128), np.float32))
    c["ones_f"] = np.ones((1, 128), np.float32)
    c["onescol_bf"] = _bf(np.ones((128, 1), np.float32))
    c["iota8"] = np.tile(np.arange(8, dtype=np.float32), (128, 1))
    c["iotac"] = np.arange(128, dtype=np.float32).reshape(128, 1)
    c["st512"] = (np.arange(16, dtype=np.float32) * 512).reshape(1, 16)
    inv = (1.0 / 10000.0 ** (np.arange(32, dtype=np.float32) / 32.0)).astype(np.float32)
    ang = np.arange(T, dtype=np.float32)[:, None] * inv[None, :]
    cosv = np.cos(ang).astype(np.float32)
    sinv = np.sin(ang).astype(np.float32)
    cosx = np.concatenate([cosv, cosv], axis=1)
    sinx = np.concatenate([-sinv, sinv], axis=1)
    c["cosx"] = np.ascontiguousarray(cosx.reshape(NT, 128, 64).transpose(1, 0, 2))
    c["sinx"] = np.ascontiguousarray(sinx.reshape(NT, 128, 64).transpose(1, 0, 2))
    return c


def build(debug=False, stop_after="all", p3_mode=3):
    B = Build(debug=debug)
    nc = B.nc
    E = B.stack.enter_context

    def din(name, shape, dtype=F32):
        return nc.dram_tensor(name, list(shape), dtype, kind="ExternalInput").ap()

    x_in = din("x", (T, D))
    Wq, bq = din("Wq", (D, D)), din("bq", (1, D))
    Wk, bk = din("Wk", (D, D)), din("bk", (1, D))
    Wv, bv = din("Wv", (D, D)), din("bv", (1, D))
    Wg, bg = din("Wg", (D, D)), din("bg", (1, D))
    Wo, bo = din("Wo", (D, D)), din("bo", (1, D))
    rW, rb = din("rW", (128, DC, NE)), din("rb", (1, NE))
    aW1p = din("aW1p", (NE, 128, DC * HID))
    aW2p = din("aW2p", (NE, 128, HC * D))
    ab1p = din("ab1p", (NE, 128, HID // 128))
    ab2p = din("ab2p", (NE, D))
    s1c, s2c = din("s1c", (128, DC)), din("s2c", (128, DC))
    gwT = din("gwT", (128, DC, NE))
    gbr, ebr = din("gbr", (1, NE)), din("ebr", (1, NE))
    fW1p = din("fW1p", (NE, 128, DC * HID))
    fW2p = din("fW2p", (NE, 128, HC * D))
    fb1p = din("fb1p", (NE, 128, HID // 128))
    fb2p = din("fb2p", (NE, D))
    ident_i = din("ident", (128, 128), BF16)
    lstrict_i = din("lstrict", (128, 128), BF16)
    ones_bf_i = din("ones_bf", (1, 128), BF16)
    ones_f_i = din("ones_f", (1, 128))
    onescol_i = din("onescol_bf", (128, 1), BF16)
    iota8_i = din("iota8", (128, NE))
    iotac_i = din("iotac", (128, 1))
    st512_i = din("st512", (1, 16))
    cosx_i = din("cosx", (128, NT, 64))
    sinx_i = din("sinx", (128, NT, 64))
    maskcore = din("maskcore", (NE, 1))

    OUT = nc.dram_tensor("out", [T, D], F32, kind="ExternalOutput").ap()
    OUT_T = OUT.rearrange("(t p) d -> t p d", p=128)
    X_T = x_in.rearrange("(t p) d -> t p d", p=128)

    s_qT = B.scratch("s_qT", (NT, 128, D), BF16)
    s_k = B.scratch("s_k", (NT, 128, D), BF16)
    s_v = B.scratch("s_v", (NT, 128, H * 65), BF16)
    s_g = B.scratch("s_g", (NT, 128, D), F32)
    s_ao = B.scratch("s_ao", (NT, 128, D), BF16)
    s_xnT = B.scratch("s_xnT", (NT, 128, D), BF16)
    s_xm = B.scratch("s_xm", (T, D), BF16)
    s_xn = B.scratch("s_xn", (T, D), BF16)
    s_atok = B.scratch("s_atok", (NSTA * 512, 1), mybir.dt.int32)
    s_ftok = B.scratch("s_ftok", (NSTF * 512, 1), mybir.dt.int32)
    s_ay = B.scratch("s_ay", (NSTA * 512, D), F32)
    s_fy = B.scratch("s_fy", (NSTF * 512, D), F32)
    cc_in = nc.dram_tensor("cc_in", [1, NE], F32)
    cc_out = nc.dram_tensor("cc_out", [NCORES, NE], F32)
    if debug:
        d_xn = B.scratch("d_xn", (T, D), F32)
        d_cnt = B.scratch("d_cnt", (T, NE), F32)
        d_ohf = B.scratch("d_ohf", (T, NE), F32)
        d_h1 = B.scratch("d_h1", (T, D), F32)
        d_rows = B.scratch("d_rows", (1, 128), F32)

    cT = E(nc.sbuf_tensor("cT", [128, DC * T], BF16))
    big1 = E(nc.sbuf_tensor("big1", [128, 16384], F32))
    big2 = E(nc.sbuf_tensor("big2", [128, 16384], F32))
    work1 = E(nc.sbuf_tensor("work1", [128, 8192], F32))
    work2 = E(nc.sbuf_tensor("work2", [128, 2048], F32))
    smalls = E(nc.sbuf_tensor("smalls", [128, 1184], F32))
    consts = E(nc.sbuf_tensor("consts", [128, 24], F32))
    cbf = E(nc.sbuf_tensor("cbf", [128, 400], BF16))
    onesr_bf = E(nc.sbuf_tensor("onesr_bf", [1, 128], BF16))
    onesr_f = E(nc.sbuf_tensor("onesr_f", [1, 128], F32))
    bcol = E(nc.sbuf_tensor("bcol", [128, NE * 32], F32))
    mcol = E(nc.sbuf_tensor("mcol", [NE, 1], F32))
    iotac = E(nc.sbuf_tensor("iotac_sb", [128, 1], F32))
    posA = E(nc.sbuf_tensor("posA", [128, 32], mybir.dt.int32))
    posF = E(nc.sbuf_tensor("posF", [128, 16], mybir.dt.int32))
    idxw = E(nc.sbuf_tensor("idxw", [128, 4], mybir.dt.int32))
    tik = E(nc.sbuf_tensor("tik", [128, 8], mybir.dt.int32))
    sct = E(nc.sbuf_tensor("sct", [128, 8], mybir.dt.int32))
    b1g = E(nc.sbuf_tensor("b1g", [128, 32], F32))
    etb = E(nc.sbuf_tensor("etb", [128, 16], F32))
    lbb = E(nc.sbuf_tensor("lbb", [128, NE], F32))
    ccsb = E(nc.sbuf_tensor("ccsb", [NCORES, NE], F32))


    PF = [E(nc.psum_tensor(f"pf{i}", [128, 512], F32)) for i in range(6)]
    PT = [E(nc.psum_tensor(f"pt{i}", [128, 1024], BF16)) for i in range(2)]

    B.auto["v"] = SC(E(nc.semaphore("auto_v")))
    B.auto["s"] = SC(E(nc.semaphore("auto_s")))
    B.auto["g"] = SC(E(nc.semaphore("auto_g")))
    pe = SC(E(nc.semaphore("pe")))
    wq = DQ([SC(E(nc.semaphore(f"wq{i}"))) for i in range(4)])
    aq = DQ([SC(E(nc.semaphore(f"aq{i}"))) for i in range(8)])
    oq = DQ([SC(E(nc.semaphore(f"oq{i}"))) for i in range(4)])
    iq = DQ([SC(E(nc.semaphore(f"iq{i}"))) for i in range(6)])
    rbA = E(nc.gpsimd.register("rbA"))
    rbF = E(nc.gpsimd.register("rbF"))
    B.raw("g", lambda g: g.reg_mov(rbA, NSTA * 512 - 1) and None)
    B.raw("g", lambda g: g.reg_mov(rbF, NSTF * 512 - 1) and None)
    stq = DQ([SC(E(nc.semaphore(f"stq{i}"))) for i in range(4)])
    cc = SC(E(nc.semaphore("cc")))

    def mm(out_ap, lhsT, rhs, start, stop, waits=(), incs=()):
        def f(t_, _o=out_ap, _l=lhsT, _r=rhs, _s=start, _p=stop):
            return t_.matmul(_o, _l, _r, start=_s, stop=_p, skip_group_check=True)
        B.raw("t", f, waits=waits, incs=incs)

    def pe_inc():
        pe.n += 1
        return (pe, pe.n)

    IDENT = cbf[:, 0:128]
    LSTRICT = cbf[:, 128:256]
    ONESCOL = cbf[:, 256:257]
    rWsb = cbf[:, 260:324]
    gWsb = cbf[:, 324:388]
    OHBF = cbf[:, 388:396]
    OHBF2 = smalls[:, 400:404].bitcast(BF16)

    def transp(out_ap, in_ap, waits=(), incs=()):
        def f(t_, _o=out_ap, _i=in_ap):
            return t_.transpose(_o, _i, IDENT)
        B.raw("t", f, waits=waits, incs=incs)

    class TT:

        def __init__(self):
            self.bank_free = [None, None]

        def run(self, src_tile, dst_fn, in_mark, extra_first_wait=(), scale_col=None):
            for dc in range(DC):
                bk = PT[dc % 2]
                w = []
                if dc == 0 and in_mark is not None:
                    w.append(in_mark)
                if self.bank_free[dc % 2] is not None:
                    w.append(self.bank_free[dc % 2])
                transp(bk[:, 0:128], src_tile[:, dc * 128:(dc + 1) * 128],
                       waits=w, incs=[(pe, 1)])
                m = pe_inc()
                B.wait("v", *m)
                ww = list(extra_first_wait) if dc == 0 else []
                if scale_col is None:
                    B.op("v", lambda v, _o=dst_fn(dc), _i=bk[:, 0:128]:
                         v.tensor_copy(_o, _i), waits=ww)
                else:
                    B.op("v", lambda v, _o=dst_fn(dc), _i=bk[:, 0:128],
                         _s=scale_col(dc):
                         v.tensor_scalar(_o, _i, _s, None, ALU.mult), waits=ww)
                self.bank_free[dc % 2] = B.mark("v")

    tpose = TT()

    B.dma("y", aq, onesr_bf[:], ones_bf_i)
    B.dma("y", aq, onesr_f[:], ones_f_i)
    B.dma("y", aq, mcol[:], maskcore)
    IOTA = consts[:, 0:8]
    S1C = consts[:, 8:16]
    S2C = consts[:, 16:24]
    B.dma("y", aq, IOTA, iota8_i)
    B.dma("y", aq, iotac[:], iotac_i)
    B.dma("y", aq, S1C, s1c)
    B.dma("y", aq, S2C, s2c)
    B.dma("y", aq, IDENT, ident_i)
    B.dma("y", aq, LSTRICT, lstrict_i)
    B.dma("y", aq, ONESCOL, onescol_i)
    BROW = smalls[0:1, 640:1152].bitcast(BF16)
    RRB = smalls[0:1, 1168:1172].bitcast(BF16)
    RGB = smalls[0:1, 1172:1176].bitcast(BF16)
    REB = smalls[0:1, 1176:1180].bitcast(BF16)
    for dst, src_ in ((RRB, rb), (RGB, gbr), (REB, ebr)):
        B.dma("g", wq, dst, src_)

    COS = work1[:, 0:NT * 64]
    SIN = work1[:, NT * 64:2 * NT * 64]
    B.dma("y", aq, COS, cosx_i.rearrange("p t c -> p (t c)"))
    B.dma("y", aq, SIN, sinx_i.rearrange("p t c -> p (t c)"))
    consts_tok = aq.snapshot()
    B.dma("g", wq, rWsb.rearrange("p (c e) -> p c e", e=NE), rW)
    B.dma("g", wq, gWsb.rearrange("p (c e) -> p c e", e=NE), gwT)
    wq_const_tok = wq.snapshot()

    stats = smalls[:, 0:32]
    logitsA = smalls[:, 32:160]
    cntC = smalls[:, 160:288]
    wgA = smalls[:, 288:416]
    ohF = smalls[:, 416:544]
    gcols = smalls[:, 544:608]
    scr8a = smalls[:, 608:616]
    scr8b = smalls[:, 616:624]
    scr8c = smalls[:, 624:632]
    thr_b = smalls[:, 632:640]
    run_row = smalls[0:1, 1152:1160]
    off_row = smalls[0:1, 1160:1168]
    ST512R = smalls[0:1, 288:304]
    krow = smalls[0:1, 304:312]
    prow = smalls[0:1, 312:320]
    lbase = smalls[0:1, 320:328]
    endr = smalls[0:1, 328:336]
    et_row = smalls[0:1, 336:352]
    et_row_tmp = smalls[0:1, 384:400]
    gcA = smalls[:, 352:384]
    B.dma("y", aq, ST512R, st512_i)
    B.op("v", lambda v: v.memset(posA[:], 0))
    zvm = B.mark("v")
    for tgt, nst in ((s_atok, NSTA), (s_ftok, NSTF)):
        cols = nst * 4
        done = 0
        while done < cols:
            cw = min(32, cols - done)
            B.dma("y", aq, tgt.ap()[done * 128:(done + cw) * 128].rearrange(
                "(p c) one -> p (c one)", p=128), posA[:, 0:cw], waits=[zvm])
            done += cw


    xn = big1[:]
    xstage = [work1[:, 2048 + i * 1024:2048 + (i + 1) * 1024] for i in range(2)]
    sq = work1[:, 4096:4096 + 1024]
    x_tok = [None] * NT
    cons_mark = {}
    B.wait_snap("v", consts_tok)
    B.wait_snap("s", consts_tok)
    eps_col = stats[:, 30:31]
    B.op("v", lambda v: v.memset(eps_col, EPS_LN))
    for t in range(NT):
        w = []
        if t >= 2:
            w.extend(cons_mark[t - 2])
        x_tok[t] = B.dma("y", aq, xstage[t % 2], X_T[t], waits=w)
        xt = xstage[t % 2]
        xnt = xn[:, t * 1024:(t + 1) * 1024]
        c_s1 = stats[:, 0:1]
        c_ss = stats[:, 1:2]
        c_m = stats[:, 2:3]
        c_r = stats[:, 3:4]
        B.wait("v", aq, x_tok[t])
        B.wait("s", aq, x_tok[t])
        prev_vm0 = B.mark("v")
        B.op("v", lambda v, _x=xt, _o=c_s1: v.reduce_sum(_o, _x, axis=AX.X))
        B.op("s", lambda s, _x=xt, _a=c_ss: s.activation(
            out=sq[:], in_=_x, func=AF.Square, accum_out=_a), waits=[prev_vm0])
        B.op("v", lambda v, _o=c_m, _i=c_s1: v.tensor_scalar(_o, _i, 1.0 / D, None, ALU.mult))
        sm = B.mark("s")
        B.wait("v", *sm)
        B.op("v", lambda v, _o=c_s1, _m=c_m: v.tensor_mul(_o, _m, _m))
        B.op("v", lambda v, _o=c_ss, _i=c_ss: v.tensor_scalar(_o, _i, 1.0 / D, None, ALU.mult))
        B.op("v", lambda v, _o=c_ss, _a=c_ss, _b=c_s1: v.tensor_sub(_o, _a, _b))
        vm = B.mark("v")
        B.wait("s", *vm)
        B.op("s", lambda s, _o=c_ss, _i=c_ss: s.activation(out=_o, in_=_i, func=AF.Sqrt, bias=eps_col))
        sm = B.mark("s")
        B.wait("v", *sm)
        B.op("v", lambda v, _o=c_r, _i=c_ss: v.reciprocal(_o, _i))
        B.op("v", lambda v, _o=xnt, _x=xt, _m=c_m, _r=c_r:
             v.tensor_scalar(_o, _x, _m, _r, ALU.subtract, ALU.mult))
        cons_mark[t] = [B.mark("v"), B.mark("s")]
    dbg_toks = []
    if debug:
        vm = B.mark("v")
        B.wait("y", *vm)
        dbg_toks.append((aq, B.dma("y", aq, d_xn.ap().rearrange("(t p) d -> p t d", p=128),
              xn.rearrange("p (t d) -> p t d", d=D))))

    if stop_after == "p0":
        return _finish_trunc(B, nc, [aq, wq])
    B.wait_snap("t", consts_tok)

    abf = [work1[:, 5120 + i * 512:5120 + (i + 1) * 512].bitcast(BF16) for i in range(2)]
    xnbf = [work1[:, 6144 + i * 512:6144 + (i + 1) * 512].bitcast(BF16) for i in range(2)]
    tsg = [work1[:, 7168 + i * 512:7168 + (i + 1) * 512].bitcast(BF16) for i in range(2)]
    xnT_spill = {}
    xn_sp = {}
    tpose_done_pe = {}
    for t in range(NT):
        xnt = xn[:, t * 1024:(t + 1) * 1024]
        c_ss = stats[:, 4:5]
        c_r1 = stats[:, 5:6]
        a_t = abf[t % 2]
        xn_t = xnbf[t % 2]
        ts_t = tsg[t % 2]
        prev_vm1 = B.mark("v")
        B.op("s", lambda s, _x=xnt, _a=c_ss: s.activation(
            out=sq[:], in_=_x, func=AF.Square, accum_out=_a), waits=[prev_vm1])
        B.op("s", lambda s, _o=c_ss, _i=c_ss: s.activation(out=_o, in_=_i, func=AF.Sqrt, scale=1.0 / D))
        sm = B.mark("s")
        B.wait("v", *sm)
        B.op("v", lambda v, _o=c_r1, _i=c_ss: v.reciprocal(_o, _i))
        wv = [(pe, tpose_done_pe[t - 2])] if t >= 2 else []
        B.op("v", lambda v, _o=a_t, _x=xnt, _r=c_r1:
             v.tensor_scalar(_o, _x, _r, None, ALU.mult), waits=wv)
        wxn = [(aq, xn_sp[t - 2])] if (t - 2) in xn_sp else []
        B.op("v", lambda v, _o=xn_t, _x=xnt: v.tensor_copy(_o, _x), waits=wxn)
        vm = B.mark("v")
        tpose.run(a_t, lambda dc, _t=t: cT[:, dc * T + _t * 128: dc * T + _t * 128 + 128], vm,
                  scale_col=lambda dc: S1C[:, dc:dc + 1])
        war = [(aq, xnT_spill[t - 2])] if t >= 2 else []
        tpose.run(xn_t, lambda dc, _ts=ts_t: _ts[:, dc * 128:(dc + 1) * 128], None,
                  extra_first_wait=war)
        tpose_done_pe[t] = pe.n
        vm2 = B.mark("v")
        B.wait("y", *vm2)
        xnT_spill[t] = B.dma("y", aq, s_xnT.ap()[t], ts_t)
        xn_sp[t] = B.dma("y", aq, s_xn.ap().rearrange("(t p) d -> t p d", p=128)[t], xn_t,
                         waits=[vm2])
    p1_vmark = B.mark("v")

    if stop_after == "p1":
        return _finish_trunc(B, nc, [aq, wq])
    wbuf = big2[:].bitcast(BF16)
    qkst = [work1[:, 2048 + i * 512:2048 + (i + 1) * 512].bitcast(BF16) for i in range(2)]
    tmp1 = work1[:, 3072:3584]
    tmp2 = work1[:, 3584:4096]
    tmp3 = work1[:, 4096:4608]
    v65 = [work1[:, 4608 + i * 520:4608 + (i + 1) * 520].bitcast(BF16) for i in range(2)]
    qTst = [work1[:, 5648 + i * 512:5648 + (i + 1) * 512].bitcast(BF16) for i in range(2)]
    g32 = [work2[:, i * 1024:(i + 1) * 1024] for i in range(2)]

    p1_aq = aq.snapshot()
    B.wait_snap("v", p1_aq)
    B.wait_snap("s", p1_aq)
    B.op("v", lambda v: v.memset(
        v65[0].rearrange("p (h c) -> p h c", c=65)[:, :, 64:65], 1.0))
    B.op("v", lambda v: v.memset(
        v65[1].rearrange("p (h c) -> p h c", c=65)[:, :, 64:65], 1.0))

    PROJS = [("q", Wq, bq, s_qT), ("k", Wk, bk, s_k), ("v", Wv, bv, s_v), ("g", Wg, bg, s_g)]
    spill_tok = {}
    stage_cons_pe = {}
    bank_cons = {0: p1_vmark, 1: p1_vmark}
    wslot_last_pe = {0: None, 1: None}
    for pi, (pname, Wsrc, brow, sdst) in enumerate(PROJS):
        wslot = wbuf[:, (pi % 2) * 8192:(pi % 2) * 8192 + 8192]
        wwar = [(pe, wslot_last_pe[pi % 2])] if wslot_last_pe[pi % 2] else []
        wtok = B.dma("g", wq, wslot.rearrange("p (c n) -> p c n", n=D),
                     Wsrc.rearrange("(c p) n -> p c n", p=128), waits=wwar)
        btok = B.dma("g", wq, BROW, brow, waits=[(pe, pe.n)])
        B.wait("t", wq, btok)
        B.wait("t", wq, wtok)
        if pi == 0:
            B.wait("t", p1_vmark[0], p1_vmark[1])
        for t in range(NT):
            for nh in range(2):
                gidx = t * 2 + nh
                bank = PF[gidx % 2]
                for dc in range(DC):
                    w = []
                    if dc == 0:
                        w.append(bank_cons[gidx % 2])
                    mm(bank[:], cT[:, dc * T + t * 128: dc * T + t * 128 + 128],
                       wslot[:, dc * 1024 + nh * 512: dc * 1024 + nh * 512 + 512],
                       start=(dc == 0), stop=False, waits=w)
                mm(bank[:], onesr_bf[:], BROW[:, nh * 512:nh * 512 + 512], False, True,
                   incs=[(pe, 1)])
                mmk = pe_inc()
                if pname in ("q", "k"):
                    stg = qkst[t % 2]
                    pv = bank[:].rearrange("p (h c) -> p h c", c=64)
                    cb = COS[:, t * 64:(t + 1) * 64].unsqueeze(1).broadcast_to([128, 8, 64])
                    sb0 = SIN[:, t * 64:t * 64 + 32].unsqueeze(1).broadcast_to([128, 8, 32])
                    sb1 = SIN[:, t * 64 + 32:t * 64 + 64].unsqueeze(1).broadcast_to([128, 8, 32])
                    t1v = tmp1[:].rearrange("p (h c) -> p h c", c=64)
                    t2v = tmp2[:].rearrange("p (h c) -> p h c", c=64)
                    B.wait("v", *mmk)
                    B.op("v", lambda v, _p=pv, _c=cb, _o=t1v: v.tensor_mul(_o, _p, _c))
                    B.op("v", lambda v, _p=pv, _s=sb0, _o=t2v:
                         v.tensor_mul(_o[:, :, 0:32], _p[:, :, 32:64], _s))
                    B.op("v", lambda v, _p=pv, _s=sb1, _o=t2v:
                         v.tensor_mul(_o[:, :, 32:64], _p[:, :, 0:32], _s))
                    bank_cons[gidx % 2] = B.mark("v")
                    B.op("v", lambda v: v.tensor_add(tmp1[:], tmp1[:], tmp2[:]))
                    B.op("v", lambda v: v.tensor_scalar(tmp2[:], tmp1[:], 0.0, None, ALU.min))
                    vm = B.mark("v")
                    B.wait("s", *vm)
                    B.op("s", lambda s: s.activation(out=tmp3[:], in_=tmp2[:], func=AF.Exp))
                    sm = B.mark("s")
                    B.wait("v", *sm)
                    ww = []
                    if pname == "q" and (pi, t - 2) in stage_cons_pe:
                        ww.append((pe, stage_cons_pe[(pi, t - 2)]))
                    if pname == "k" and (pi, t - 2) in spill_tok:
                        ww.append((aq, spill_tok[(pi, t - 2)]))
                    B.op("v", lambda v, _o=stg[:, nh * 512:nh * 512 + 512]:
                         v.scalar_tensor_tensor(_o, tmp1[:], 0.0, tmp3[:], ALU.max, ALU.add),
                         waits=ww)
                elif pname == "v":
                    vst = v65[t % 2]
                    ww = [(aq, spill_tok[(pi, t - 2)])] if (pi, t - 2) in spill_tok else []
                    B.wait("v", *mmk)
                    B.op("v", lambda v,
                         _o=vst.rearrange("p (h c) -> p h c", c=65)[:, nh * 8:nh * 8 + 8, 0:64],
                         _i=bank[:].rearrange("p (h c) -> p h c", c=64):
                         v.tensor_copy(_o, _i), waits=ww)
                    bank_cons[gidx % 2] = B.mark("v")
                elif pname == "g":
                    gst = g32[t % 2]
                    ww = [(aq, spill_tok[(pi, t - 2)])] if (pi, t - 2) in spill_tok else []
                    B.wait("s", *mmk)
                    B.op("s", lambda s, _o=gst[:, nh * 512:nh * 512 + 512], _i=bank[:]:
                         s.activation(out=_o, in_=_i, func=AF.Sigmoid), waits=ww)
                    bank_cons[gidx % 2] = B.mark("s")
            if pname == "q":
                stg = qkst[t % 2]
                qts = qTst[t % 2]
                vm = B.mark("v")
                war = [(aq, spill_tok[(pi, t - 2)])] if (pi, t - 2) in spill_tok else []
                tpose.run(stg, lambda dc, _q=qts: _q[:, dc * 128:(dc + 1) * 128], vm,
                          extra_first_wait=war)
                stage_cons_pe[(pi, t)] = pe.n
                vm2 = B.mark("v")
                B.wait("y", *vm2)
                spill_tok[(pi, t)] = B.dma("y", aq, sdst.ap()[t], qts)
            elif pname == "k":
                vm = B.mark("v")
                B.wait("y", *vm)
                spill_tok[(pi, t)] = B.dma("y", aq, sdst.ap()[t], qkst[t % 2])
            elif pname == "v":
                vm = B.mark("v")
                B.wait("y", *vm)
                spill_tok[(pi, t)] = B.dma("y", aq, sdst.ap()[t], v65[t % 2])
            elif pname == "g":
                sm = B.mark("s")
                B.wait("y", *sm)
                spill_tok[(pi, t)] = B.dma("y", aq, sdst.ap()[t], g32[t % 2])
        wslot_last_pe[pi % 2] = pe.n
    p2_aq = aq.snapshot()
    p2_vmark = B.mark("v")
    p2_smark = B.mark("s")

    if stop_after == "p2":
        return _finish_trunc(B, nc, [aq, wq])
    wo_slot = wbuf[:, 0:8192]
    wo_tok = B.dma("g", wq, wo_slot.rearrange("p (c n) -> p c n", n=D),
                   Wo.rearrange("(c p) n -> p c n", p=128),
                   waits=[(pe, pe.n)])
    wob_tok = B.dma("g", wq, BROW, bo, waits=[(pe, pe.n)])

    off = [0]

    def aw1(ncols):
        ap = work1[:, off[0]:off[0] + ncols]
        off[0] += ncols
        return ap

    qT_b = [aw1(512).bitcast(BF16) for _ in range(2)]
    k_b = [aw1(512).bitcast(BF16) for _ in range(2)]
    v_b = [aw1(520).bitcast(BF16) for _ in range(2)]
    g_b = [aw1(1024) for _ in range(2)]
    szsb = aw1(260).bitcast(BF16)
    s_acc = aw1(520)
    rden = aw1(16)
    ptmp = aw1(256)
    ao_b = [aw1(512).bitcast(BF16) for _ in range(2)]
    assert off[0] <= 8192, off[0]

    def S_ap(h):
        return PF[2 + h // 8][(h % 2) * 64:(h % 2) * 64 + 64,
                              ((h // 2) % 4) * 65:((h // 2) % 4) * 65 + 65]

    def SZ_ap(h):
        p0 = (h % 2) * 64
        c0 = (h // 8) * 260 + ((h // 2) % 4) * 65
        return szsb[p0:p0 + 64, c0:c0 + 65]

    NUMB = [PF[0], PF[1], PF[4], PF[5]]

    def num_ap(h, c):
        return NUMB[h // 4][(c % 2) * 64:(c % 2) * 64 + 64,
                            (h % 4) * 65:(h % 4) * 65 + 65]

    rl_tok = {}
    scan_done = {}

    def issue_reload(t):
        w = []
        if t >= 2 and (t - 2) in scan_done:
            w = [(pe, scan_done[t - 2][0]), scan_done[t - 2][1]]
            if (t - 2) in ao_spill:
                w.append((aq, ao_spill[t - 2]))
        pr = t % 2
        if t < 2:
            w = list(w) + p2_aq
        rl_tok[t] = [
            B.dma("y", aq, qT_b[pr], s_qT.ap()[t], waits=w),
            B.dma("y", aq, k_b[pr], s_k.ap()[t], waits=w),
            B.dma("y", aq, v_b[pr], s_v.ap()[t], waits=w),
            B.dma("y", aq, g_b[pr], s_g.ap()[t], waits=w),
        ]

    ao_spill = {}
    B.op("v", lambda v: v.memset(s_acc[:], 0.0), waits=p2_aq)
    issue_reload(0)
    issue_reload(1)
    B.wait("t", p2_vmark[0], p2_vmark[1])
    B.wait("t", p2_smark[0], p2_smark[1])
    sz_mark = None
    epi_mark = None
    prev_numk = None
    for c in range(NCH):
        t = c // 2
        pr = t % 2
        cp = (c % 2) * 64
        if c % 2 == 0:
            B.wait("t", aq, rl_tok[t][1])
            B.wait("t", aq, rl_tok[t][2])
        for h in range(H):
            w = []
            if h == 0 and sz_mark is not None:
                w.append(sz_mark)
            if h == 0 and prev_numk is not None:
                w.append(prev_numk)
            mm(S_ap(h),
               k_b[pr][cp:cp + 64, h * 64:(h + 1) * 64],
               v_b[pr][cp:cp + 64, h * 65:h * 65 + 65],
               start=True, stop=True, waits=w,
               incs=([(pe, 1)] if h == H - 1 else []))
        supd = pe_inc()
        B.wait("v", *supd)
        B.op("v", lambda v: v.tensor_add(s_acc[:, 0:260], s_acc[:, 0:260], PF[2][:, 0:260]))
        B.op("v", lambda v: v.tensor_add(s_acc[:, 260:520], s_acc[:, 260:520], PF[3][:, 0:260]))
        sz_mark = B.mark("v")
        B.op("v", lambda v: v.tensor_copy(szsb[:, 0:260], s_acc[:, 0:260]))
        B.op("v", lambda v: v.tensor_copy(szsb[:, 260:520], s_acc[:, 260:520]))
        sz_mark2 = B.mark("v")
        B.wait("t", sz_mark2[0], sz_mark2[1])
        if epi_mark is not None:
            B.wait("t", epi_mark[0], epi_mark[1])
        if c % 2 == 0:
            B.wait("t", aq, rl_tok[t][0])
        if p3_mode < 2:
            epi_mark = sz_mark2
            if c % 2 == 1:
                scan_done[t] = (pe.n, sz_mark2)
                ao_spill[t] = rl_tok[t][3]
                if t + 2 < NT:
                    issue_reload(t + 2)
            continue
        order = list(range(0, H, 2)) + list(range(1, H, 2))
        evens_mark = None
        for i, h in enumerate(order):
            w = []
            if i == 8:
                w = [evens_mark]
            mm(num_ap(h, c),
               qT_b[pr][(h % 2) * 64:(h % 2) * 64 + 64,
                        (h // 2) * 128 + cp:(h // 2) * 128 + cp + 64],
               SZ_ap(h), start=True, stop=True, waits=w,
               incs=([(pe, 1)] if i in (7, 15) else []))
            if i == 7:
                evens_mark = pe_inc()
        numk = pe_inc()
        prev_numk = numk
        B.wait("v", *numk)
        if p3_mode < 3:
            epi_mark = (pe, numk[1]) if False else sz_mark2
            B.wait("v", *numk)
            B.op("v", lambda v: v.tensor_copy(ptmp[0:64, 0:256], NUMB[0][0:64, 0:256]))
            epi_mark = B.mark("v")
            if c % 2 == 1:
                scan_done[t] = (pe.n, epi_mark)
                ao_spill[t] = rl_tok[t][3]
                if t + 2 < NT:
                    issue_reload(t + 2)
            continue
        if c % 2 == 0:
            B.wait("v", aq, rl_tok[t][3])
        if c % 2 == 0 and t >= 2 and (t - 2) in ao_spill:
            B.wait("v", aq, ao_spill[t - 2])
        for b4 in range(4):
            bank = NUMB[b4]
            dv = bank[cp:cp + 64, 0:260].rearrange("p (h x) -> p h x", x=65)[:, :, 64:65]
            ov = rden[cp:cp + 64, b4 * 4:(b4 + 1) * 4].unsqueeze(2)
            B.op("v", lambda v, _d=dv, _o=ov: v.tensor_scalar(_o, _d, EPS_DEN, None, ALU.add))
        B.op("v", lambda v, _r=rden[cp:cp + 64, 0:16]: v.reciprocal(_r, _r))
        for b4 in range(4):
            bank = NUMB[b4]
            nv = bank[cp:cp + 64, 0:260].rearrange("p (h x) -> p h x", x=65)[:, :, 0:64]
            rv = rden[cp:cp + 64, b4 * 4:(b4 + 1) * 4].unsqueeze(2).broadcast_to([64, 4, 64])
            pt_ = ptmp[cp:cp + 64, 0:256].rearrange("p (h x) -> p h x", x=64)
            B.op("v", lambda v, _n=nv, _r=rv, _o=pt_: v.tensor_mul(_o, _n, _r))
            gv = g_b[pr][cp:cp + 64, b4 * 256:(b4 + 1) * 256]
            av = ao_b[pr][cp:cp + 64, b4 * 256:(b4 + 1) * 256]
            B.op("v", lambda v, _g=gv, _o=av, _p=ptmp[cp:cp + 64, 0:256]:
                 v.tensor_mul(_o, _p, _g))
        epi_mark = B.mark("v")
        if c % 2 == 1:
            B.wait("y", epi_mark[0], epi_mark[1])
            ao_spill[t] = B.dma("y", aq, s_ao.ap()[t], ao_b[pr])
            scan_done[t] = (pe.n, epi_mark)
            if t + 2 < NT:
                issue_reload(t + 2)
    p3_aq = aq.snapshot()
    p3_vmark = B.mark("v")

    if stop_after == "p3":
        return _finish_trunc(B, nc, [aq, wq])
    aol = [work1[:, i * 512:(i + 1) * 512].bitcast(BF16) for i in range(2)]
    aoT = work1[:, 1024:1536].bitcast(BF16)
    h1t = work1[:, 1536:2560]
    outst = [work1[:, 2560 + i * 1024:2560 + (i + 1) * 1024] for i in range(2)]
    xst4 = [work1[:, 4608 + i * 1024:4608 + (i + 1) * 1024] for i in range(2)]
    sq4 = work1[:, 6656:7680]
    xmst = [work2[:, i * 512:(i + 1) * 512].bitcast(BF16) for i in range(2)]
    B.wait("t", wq, wo_tok)
    B.wait("t", wq, wob_tok)
    B.wait("t", p3_vmark[0], p3_vmark[1])
    ao_tok = [None] * NT
    x4_tok = [None] * NT
    tpose_done_p4 = {}
    out_tok4 = {}
    h1dbg_tok = {}
    xm_sp = {}
    for t in range(2):
        w0 = [(pe, pe.n), p3_vmark] + p3_aq
        ao_tok[t] = B.dma("y", aq, aol[t % 2], s_ao.ap()[t], waits=w0)
        x4_tok[t] = B.dma("y", aq, xst4[t % 2], X_T[t], waits=w0)
    for t in range(NT):
        B.wait("t", aq, ao_tok[t])
        vm_prev = B.mark("v")
        tpose.run(aol[t % 2], lambda dc: aoT[:, dc * 128:(dc + 1) * 128], vm_prev)
        aoT_mark = B.mark("v")
        for nh in range(2):
            bank = PF[nh]
            w = [aoT_mark] if nh == 0 else []
            if nh == 0 and debug and (t - 1) in h1dbg_tok:
                wh1 = [(aq, h1dbg_tok[t - 1])]
            else:
                wh1 = []
            for dc in range(DC):
                ww = list(w) if dc == 0 else []
                mm(bank[:], aoT[:, dc * 128:(dc + 1) * 128],
                   wo_slot[:, dc * 1024 + nh * 512: dc * 1024 + nh * 512 + 512],
                   start=(dc == 0), stop=False, waits=ww)
            mm(bank[:], onesr_bf[:], BROW[:, nh * 512:nh * 512 + 512], False, True,
               incs=[(pe, 1)])
            mk = pe_inc()
            B.wait("v", *mk)
            B.op("v", lambda v, _o=h1t[:, nh * 512:nh * 512 + 512], _p=bank[:],
                 _x=xn[:, t * 1024 + nh * 512: t * 1024 + nh * 512 + 512]:
                 v.tensor_add(_o, _p, _x), waits=wh1)
        B.wait("v", aq, x4_tok[t])
        wout = [(stq, out_tok4[t - 2])] if t >= 2 else []
        B.op("v", lambda v, _o=outst[t % 2], _a=h1t[:], _b=xst4[t % 2]:
             v.tensor_add(_o, _a, _b), waits=wout)
        vm = B.mark("v")
        B.wait("y", *vm)
        out_tok4[t] = B.dma("y", stq, OUT_T[t], outst[t % 2])
        if debug:
            h1dbg_tok[t] = B.dma("y", aq, d_h1.ap().rearrange("(t p) d -> t p d", p=128)[t], h1t[:])
        if t + 2 < NT:
            wpre = [(pe, pe.n), vm]
            ao_tok[t + 2] = B.dma("y", aq, aol[t % 2], s_ao.ap()[t + 2], waits=wpre)
            x4_tok[t + 2] = B.dma("y", aq, xst4[t % 2], X_T[t + 2], waits=wpre)
        c_ss = stats[:, 6:7]
        c_r2 = stats[:, 7:8]
        B.wait("s", *vm)
        B.op("s", lambda s, _a=c_ss: s.activation(out=sq4[:], in_=h1t[:], func=AF.Square, accum_out=_a))
        B.op("s", lambda s, _o=c_ss, _i=c_ss: s.activation(out=_o, in_=_i, func=AF.Sqrt, scale=1.0 / D))
        sm = B.mark("s")
        B.wait("v", *sm)
        B.op("v", lambda v, _o=c_r2, _i=c_ss: v.reciprocal(_o, _i))
        wxm = [(pe, tpose_done_p4[t - 2])] if t >= 2 else []
        if (t - 2) in xm_sp:
            wxm.append((aq, xm_sp[t - 2]))
        B.op("v", lambda v, _o=xmst[t % 2], _r=c_r2:
             v.tensor_scalar(_o, h1t[:], _r, None, ALU.mult), waits=wxm)
        vm2 = B.mark("v")
        xm_sp[t] = B.dma("y", aq, s_xm.ap().rearrange("(t p) d -> t p d", p=128)[t],
                         xmst[t % 2], waits=[vm2])
        tpose.run(xmst[t % 2], lambda dc, _t=t: cT[:, dc * T + _t * 128: dc * T + _t * 128 + 128], vm2,
                  scale_col=lambda dc: S2C[:, dc:dc + 1])
        tpose_done_p4[t] = pe.n
    p4_vmark = B.mark("v")
    p4_out_tok = stq.snapshot()

    if stop_after == "p4":
        return _finish_trunc(B, nc, [aq, wq, stq])
    B.wait_snap("t", wq_const_tok)
    B.wait("t", p4_vmark[0], p4_vmark[1])
    for t in range(NT):
        for dc in range(DC):
            mm(PF[0][:, t * 8:(t + 1) * 8], cT[:, dc * T + t * 128: dc * T + t * 128 + 128],
               rWsb[:, dc * 8:(dc + 1) * 8], start=(dc == 0), stop=False)
        mm(PF[0][:, t * 8:(t + 1) * 8], onesr_bf[:], RRB, False, True,
           incs=([(pe, 1)] if t == NT - 1 else []))
    rmk = pe_inc()
    B.wait("v", *rmk)
    B.op("v", lambda v: v.tensor_copy(logitsA[:], PF[0][:, 0:128]))
    B.op("v", lambda v: v.memset(run_row, 0.0))
    mx = stats[:, 8:9]
    sume = stats[:, 9:10]
    i1c = stats[:, 10:11]
    i2c = stats[:, 11:12]
    rsum = stats[:, 12:13]
    cnt_pe_hist = {}
    for t in range(NT):
        blk = logitsA[:, t * 8:(t + 1) * 8]
        B.op("v", lambda v, _b=blk: v.reduce_max(mx, _b, axis=AX.X))
        B.op("v", lambda v: v.tensor_scalar(mx, mx, -1.0, None, ALU.mult))
        vm = B.mark("v")
        B.wait("s", *vm)
        B.op("s", lambda s, _b=blk: s.activation(out=_b, in_=_b, func=AF.Exp, bias=mx, accum_out=sume))
        sm = B.mark("s")
        B.wait("v", *sm)
        B.op("v", lambda v: v.reciprocal(sume, sume))
        B.op("v", lambda v, _b=blk: v.tensor_scalar(_b, _b, sume, None, ALU.mult))
        m8 = scr8c
        B.op("v", lambda v, _o=m8, _b=blk: v.max(out=_o, in_=_b))
        B.op("v", lambda v, _b=blk, _m=m8: v.tensor_single_scalar(scr8a, _b, _m[:, 0:1], ALU.is_equal))
        B.op("v", lambda v: v.tensor_scalar(scr8a, scr8a, -1000.0, 1000.0, ALU.mult, ALU.add))
        B.op("v", lambda v: v.tensor_add(scr8a, scr8a, IOTA))
        B.op("v", lambda v: v.tensor_reduce(i1c, scr8a, AX.X, ALU.min))
        B.op("v", lambda v, _b=blk, _m=m8: v.tensor_single_scalar(scr8a, _b, _m[:, 1:2], ALU.is_equal))
        B.op("v", lambda v: v.tensor_scalar(scr8a, scr8a, -1000.0, 1000.0, ALU.mult, ALU.add))
        B.op("v", lambda v: v.tensor_add(scr8a, scr8a, IOTA))
        B.op("v", lambda v: v.tensor_single_scalar(scr8b, IOTA, i1c, ALU.is_equal))
        B.op("v", lambda v: v.tensor_scalar(scr8b, scr8b, 2000.0, None, ALU.mult))
        B.op("v", lambda v: v.tensor_add(scr8a, scr8a, scr8b))
        B.op("v", lambda v: v.tensor_reduce(i2c, scr8a, AX.X, ALU.min))
        B.op("v", lambda v, _m=m8: v.tensor_add(rsum, _m[:, 0:1], _m[:, 1:2]))
        B.op("v", lambda v: v.reciprocal(rsum, rsum))
        B.op("v", lambda v, _m=m8, _o=gcols[:, t * 4:t * 4 + 1]:
             v.tensor_scalar(_o, _m[:, 0:1], rsum, None, ALU.mult))
        B.op("v", lambda v, _m=m8, _o=gcols[:, t * 4 + 1:t * 4 + 2]:
             v.tensor_scalar(_o, _m[:, 1:2], rsum, None, ALU.mult))
        B.op("v", lambda v, _o=gcols[:, t * 4 + 2:t * 4 + 3]: v.tensor_copy(_o, i1c))
        B.op("v", lambda v, _o=gcols[:, t * 4 + 3:t * 4 + 4]: v.tensor_copy(_o, i2c))
        B.op("v", lambda v, _b=blk: v.tensor_single_scalar(_b, IOTA, i1c, ALU.is_equal))
        B.op("v", lambda v: v.tensor_single_scalar(scr8b, IOTA, i2c, ALU.is_equal))
        oh_slot = OHBF if t % 2 == 0 else OHBF2
        w_oh = [(pe, cnt_pe_hist[t - 2])] if (t - 2) in cnt_pe_hist else []
        B.op("v", lambda v, _b=blk: v.tensor_add(scr8a, _b, scr8b), waits=w_oh)
        B.op("v", lambda v, _o=oh_slot: v.tensor_copy(_o, scr8a))
        vm2 = B.mark("v")
        B.wait("t", vm2[0], vm2[1])
        mm(PF[1][:, 0:8], LSTRICT, oh_slot, True, True)
        mm(PF[1][:, 16:24], onesr_f[:], run_row, True, True)
        mm(PF[1][0:1, 8:16], ONESCOL, oh_slot, True, True, incs=[(pe, 1)])
        ck = pe_inc()
        cnt_pe_hist[t] = ck[1]
        B.wait("v", *ck)
        B.op("v", lambda v: v.tensor_copy(scr8b, PF[1][:, 16:24]))
        B.op("v", lambda v, _o=cntC[:, t * 8:(t + 1) * 8]: v.tensor_add(_o, PF[1][:, 0:8], scr8b))
        B.op("v", lambda v: v.tensor_add(run_row, run_row, PF[1][0:1, 8:16]))
    vm = B.mark("v")
    B.wait("g", *vm)
    cc_wr = B.dma("g", oq, cc_in.ap(), run_row)
    B.wait("g", oq, cc_wr)

    def do_cc(g):
        g.collective_compute(
            "AllGather", ALU.bypass, replica_groups=[list(range(NCORES))],
            ins=[cc_in.ap()], outs=[cc_out.ap()]).then_inc(cc.h, 1)
    B.ops["g"].append(do_cc)
    cc.n += 1
    B.wait("g", cc, 1)
    cct = B.dma("g", oq, ccsb[:], cc_out.ap())
    B.wait("t", oq, cct)
    B.wait("t", vm[0], vm[1])
    mm(PF[1][0:1, 24:32], mcol[:], ccsb[:], True, True, incs=[(pe, 1)])
    ok = pe_inc()
    B.wait("v", *ok)
    B.op("v", lambda v: v.tensor_scalar(off_row, PF[1][0:1, 24:32], -1.0, float(CAP),
                                        ALU.mult, ALU.add))
    vm = B.mark("v")
    B.wait("t", vm[0], vm[1])
    mm(PF[1][:, 32:40], onesr_f[:], off_row, True, True, incs=[(pe, 1)])
    tk = pe_inc()
    B.wait("v", *tk)
    B.op("v", lambda v: v.tensor_copy(thr_b, PF[1][:, 32:40]))
    B.op("v", lambda v: v.tensor_scalar(krow, off_row, 0.0, None, ALU.max))
    B.op("v", lambda v: v.tensor_tensor(krow, krow, run_row, ALU.min))
    B.op("v", lambda v: v.memset(prow, 0.0))
    for m in range(15):
        B.op("v", lambda v, _m=m: v.tensor_single_scalar(et_row_tmp[:, 0:8], krow,
                                                         float(512 * _m), ALU.is_gt))
        B.op("v", lambda v: v.tensor_add(prow, prow, et_row_tmp[:, 0:8]))
    B.op("v", lambda v: v.tensor_scalar(prow, prow, 512.0, None, ALU.mult))
    B.op("v", lambda v: v.memset(lbase[:, 0:1], 0.0))
    for e in range(1, NE):
        B.op("v", lambda v, _e=e: v.tensor_add(lbase[:, _e:_e + 1], lbase[:, _e - 1:_e],
                                               prow[:, _e - 1:_e]))
    B.op("v", lambda v: v.tensor_add(endr, lbase, prow))
    B.op("v", lambda v: v.memset(et_row[:, 0:16], 0.0))
    for e in range(NE):
        B.op("v", lambda v, _e=e: v.tensor_single_scalar(
            et_row_tmp[:, 0:NSTA], ST512R[:, 0:NSTA], endr[:, _e:_e + 1], ALU.is_ge))
        B.op("v", lambda v: v.tensor_add(et_row[:, 0:NSTA], et_row[:, 0:NSTA],
                                         et_row_tmp[:, 0:NSTA]))
    B.op("v", lambda v: v.tensor_scalar(et_row[:, 0:NSTA], et_row[:, 0:NSTA], 7.0, None, ALU.min))
    vm = B.mark("v")
    B.wait("t", vm[0], vm[1])
    mm(PF[1][:, 40:40 + NSTA], onesr_f[:], et_row[:, 0:NSTA], True, True)
    mm(PF[1][:, 56:64], onesr_f[:], lbase, True, True, incs=[(pe, 1)])
    bk = pe_inc()
    B.wait("v", *bk)
    B.op("v", lambda v: v.tensor_copy(etb[:, 0:NSTA], PF[1][:, 40:40 + NSTA]))
    B.op("v", lambda v: v.tensor_copy(lbb[:], PF[1][:, 56:64]))
    sc_toks_a = []
    for t in range(NT):
        cntt = cntC[:, t * 8:(t + 1) * 8]
        g1 = gcols[:, t * 4:t * 4 + 1]
        g2 = gcols[:, t * 4 + 1:t * 4 + 2]
        i1s = gcols[:, t * 4 + 2:t * 4 + 3]
        i2s = gcols[:, t * 4 + 3:t * 4 + 4]
        B.op("v", lambda v, _i=i1s: v.tensor_single_scalar(scr8a, IOTA, _i, ALU.is_equal))
        B.op("v", lambda v, _i=i2s: v.tensor_single_scalar(scr8b, IOTA, _i, ALU.is_equal))
        B.op("v", lambda v, _c=cntt: v.tensor_tensor(scr8c, _c, thr_b, ALU.is_lt))
        B.op("v", lambda v: v.tensor_mul(scr8c, scr8c, scr8a))
        B.op("v", lambda v, _o=stats[:, 14:15]: v.reduce_sum(_o, scr8c, axis=AX.X))
        B.op("v", lambda v, _c=cntt: v.tensor_add(scr8c, _c, scr8a))
        B.op("v", lambda v: v.tensor_tensor(scr8c, scr8c, thr_b, ALU.is_lt))
        B.op("v", lambda v: v.tensor_mul(scr8c, scr8c, scr8b))
        B.op("v", lambda v, _o=stats[:, 15:16]: v.reduce_sum(_o, scr8c, axis=AX.X))
        B.op("v", lambda v, _c=cntt: v.tensor_add(scr8c, lbb[:], _c))
        B.op("v", lambda v: v.tensor_mul(scr8a, scr8a, scr8c))
        B.op("v", lambda v, _o=stats[:, 16:17]: v.reduce_sum(_o, scr8a, axis=AX.X))
        B.op("v", lambda v: v.tensor_mul(scr8b, scr8b, scr8c))
        B.op("v", lambda v, _o=stats[:, 17:18]: v.reduce_sum(_o, scr8b, axis=AX.X))
        B.op("v", lambda v, _g=g1, _o=gcA[:, 2 * t:2 * t + 1]:
             v.tensor_mul(_o, _g, stats[:, 14:15]))
        B.op("v", lambda v, _g=g2, _o=gcA[:, 2 * t + 1:2 * t + 2]:
             v.tensor_mul(_o, _g, stats[:, 15:16]))
        B.op("v", lambda v: v.tensor_mul(stats[:, 18:19], stats[:, 16:17], stats[:, 14:15]))
        B.op("v", lambda v: v.tensor_mul(stats[:, 19:20], stats[:, 17:18], stats[:, 15:16]))
        B.op("v", lambda v, _o=posA[:, 2 * t:2 * t + 1]: v.tensor_copy(_o, stats[:, 18:19]))
        B.op("v", lambda v, _o=posA[:, 2 * t + 1:2 * t + 2]: v.tensor_copy(_o, stats[:, 19:20]))
        B.op("v", lambda v: v.tensor_scalar(stats[:, 20:21], stats[:, 14:15], -1.0, 1.0, ALU.mult, ALU.add))
        B.op("v", lambda v: v.scalar_tensor_tensor(stats[:, 20:21], stats[:, 20:21], 1e7,
                                                   stats[:, 16:17], ALU.mult, ALU.add))
        B.op("v", lambda v: v.tensor_scalar(stats[:, 21:22], stats[:, 15:16], -1.0, 1.0, ALU.mult, ALU.add))
        B.op("v", lambda v: v.scalar_tensor_tensor(stats[:, 21:22], stats[:, 21:22], 1e7,
                                                   stats[:, 17:18], ALU.mult, ALU.add))
        c0 = 4 * (t % 2)
        wsc = ([(iq, sc_toks_a[2 * (t - 2)]), (iq, sc_toks_a[2 * (t - 2) + 1])]
               if t >= 2 else [])
        B.op("v", lambda v, _o=sct[:, c0:c0 + 1]: v.tensor_copy(_o, stats[:, 20:21]),
             waits=wsc)
        B.op("v", lambda v, _t=t:
             v.tensor_scalar(stats[:, 22:23], iotac[:], float(128 * _t), None, ALU.add))
        B.op("v", lambda v, _o=sct[:, c0 + 1:c0 + 2]: v.tensor_copy(_o, stats[:, 22:23]))
        B.op("v", lambda v, _o=sct[:, c0 + 2:c0 + 3]: v.tensor_copy(_o, stats[:, 21:22]))
        vmk = B.mark("v")
        sc_toks_a.append(B.dma("g", iq, s_atok.ap(), sct[:, c0 + 1:c0 + 2],
                               indirect_out=sct[:, c0:c0 + 1], bounds=rbA,
                               waits=[vmk]))
        sc_toks_a.append(B.dma("g", iq, s_atok.ap(), sct[:, c0 + 1:c0 + 2],
                               indirect_out=sct[:, c0 + 2:c0 + 3], bounds=rbA))
    if debug:
        vmD = B.mark("v")
        B.wait("y", *vmD)
        B.dma("y", aq, d_cnt.ap().rearrange("(t p) e -> p t e", p=128),
              cntC[:].rearrange("p (t e) -> p t e", e=NE))
        B.dma("y", aq, d_rows.ap()[:, 0:64], smalls[0:1, 288:352])
    p5_vmark = B.mark("v")

    if stop_after == "p5":
        return _finish_trunc(B, nc, [aq, wq, stq, oq])
    W1sb = big1[:].bitcast(BF16)
    W2sb = big2[:].bitcast(BF16)
    hTs = work1[:].bitcast(BF16)
    xg = work2[:].bitcast(BF16)
    xgT = cT[:, 0:4096]
    yst2 = [cT[:, 4096 + i * 2048:4096 + (i + 1) * 2048].bitcast(F32) for i in range(2)]
    Ycmb = [cT[:, 8192 + i * 2048:8192 + (i + 1) * 2048].bitcast(F32) for i in range(2)]

    def sparse_moe(NST, sw1, sw2, b1src, b2src, s_x, s_yv, stok, act_fn,
                   sc_toks, first_waits, scale_cols=None):
        ph1_mark = {"pe": None}
        q_marks = {"m": [None] * 4}
        ph2_mark = {"pe": None}
        ph2_vmark = {"m": None}
        idx_vm = None
        st_store = {}
        xfer = {}
        yev = {0: None, 1: None}
        for st in range(NST):
            w_idx = []
            if idx_vm is not None:
                w_idx = [(iq, tk) for tk in xfer[st - 1]["wall"]]
            B.op("v", lambda v, _st=st: v.scalar_tensor_tensor(
                stats[:, 24:25], etb[:, _st:_st + 1], 128.0, iotac[:], ALU.mult, ALU.add),
                waits=w_idx)
            B.op("v", lambda v: v.tensor_copy(idxw[:, 0:1], stats[:, 24:25]))
            B.op("v", lambda v, _st=st: v.tensor_copy(idxw[:, 1:2], etb[:, _st:_st + 1]))
            idx_vm = B.mark("v")
            sl = 4 * (st % 2)
            if st == 0:
                wtk = list(sc_toks)
            elif st >= 2:
                wtk = [(iq, tk) for tk in xfer[st - 2]["xg"]]
            else:
                wtk = []
            t_ti = B.dma("g", iq, tik[:, sl:sl + 4],
                         stok.ap()[st * 512:(st + 1) * 512].rearrange("(p j) one -> p (j one)", p=128),
                         waits=wtk)
            xg_toks = []
            for jj in range(4):
                wxg = [(iq, t_ti)]
                if st >= 1:
                    wxg.append((pe, xfer[st - 1]["tp"]))
                else:
                    wxg.extend(first_waits.get("xg", []))
                xg_toks.append(B.dma("g", iq, xg[:, jj * 1024:(jj + 1) * 1024], s_x.ap(),
                                     indirect_in=tik[:, sl + jj:sl + jj + 1], waits=wxg))
            ww2 = [idx_vm] + ([(pe, ph2_mark["pe"])] if ph2_mark["pe"] else
                              first_waits.get("w2", []))
            t_w1q = []
            for q in range(4):
                if q_marks["m"][q] is not None:
                    wwq = [idx_vm, (pe, q_marks["m"][q])]
                else:
                    wwq = [idx_vm] + first_waits.get("w1", [])
                t_w1q.append(B.dma("g", iq, W1sb[:, q * 8192:(q + 1) * 8192],
                                   sw1, indirect_in=idxw[:, 0:1], waits=wwq,
                                   elem_off=q * 8192))
            wb = [idx_vm] + ([(pe, ph2_mark["pe"])] if ph2_mark["pe"] else [])
            t_b1 = B.dma("g", iq, b1g[:], b1src, indirect_in=idxw[:, 0:1], waits=wb)
            t_b2 = B.dma("g", iq, smalls[0:2, 640:1152].bitcast(BF16), b2src,
                         indirect_in=idxw[0:2, 1:2], waits=wb)
            t_w2 = [B.dma("g", iq, W2sb[:, part * 8192:(part + 1) * 8192],
                          sw2, indirect_in=idxw[:, 0:1], waits=ww2,
                          elem_off=part * 8192) for part in range(4)]

            for jj in range(4):
                wtp = [(iq, xg_toks[jj])]
                if jj == 0 and ph1_mark["pe"]:
                    wtp.append((pe, ph1_mark["pe"]))
                tpose.run(xg[:, jj * 1024:(jj + 1) * 1024],
                          lambda dc, _j=jj: xgT[:, dc * 512 + _j * 128: dc * 512 + _j * 128 + 128],
                          wtp[0], extra_first_wait=wtp[1:],
                          scale_col=(None if scale_cols is None else
                                     (lambda dc: scale_cols[:, dc:dc + 1])))
            tp_pe = pe.n
            tp_vm = B.mark("v")
            smk0 = B.mark("s")
            gel_mark = {0: smk0, 1: smk0}
            B.wait("t", iq, t_w1q[0])
            B.wait("t", tp_vm[0], tp_vm[1])
            for hcg in range(16):
                if hcg % 4 == 0 and hcg > 0:
                    B.wait("t", iq, t_w1q[hcg // 4])
                pr2 = hcg % 2
                banks = (PF[0], PF[1]) if pr2 == 0 else (PF[2], PF[3])
                for dc in range(DC):
                    for j in range(2):
                        hc = hcg * 2 + j
                        h, lh = hc // 8, hc % 8
                        w = []
                        if dc == 0 and j == 0:
                            w.append(gel_mark[pr2])
                            if hcg == 1 and ph2_vmark["m"] is not None:
                                w.append(ph2_vmark["m"])
                        mm(banks[j][:],
                           W1sb[:, h * 8192 + dc * 1024 + lh * 128:
                                h * 8192 + dc * 1024 + (lh + 1) * 128],
                           xgT[:, dc * 512: dc * 512 + 512],
                           start=(dc == 0), stop=(dc == DC - 1), waits=w,
                           incs=([(pe, 1)] if (dc == DC - 1 and j == 1) else []))
                mk = pe_inc()
                if hcg % 4 == 3:
                    q_marks["m"][hcg // 4] = pe.n
                B.wait("s", *mk)
                if hcg == 0:
                    B.wait("s", iq, t_b1)
                for j in range(2):
                    hc = hcg * 2 + j
                    B.op("s", lambda s, _o=hTs[:, hc * 512:(hc + 1) * 512],
                         _p=banks[j][:], _b=b1g[:, hc:hc + 1], _f=act_fn:
                         s.activation(out=_o, in_=_p, func=_f, bias=_b))
                gel_mark[pr2] = B.mark("s")
            ph1_mark["pe"] = pe.n
            smk = B.mark("s")
            B.wait("t", smk[0], smk[1])
            for _tk in t_w2:
                B.wait("t", iq, _tk)
            B.wait("t", iq, t_b2)
            for tt in range(4):
                ybanks = [PF[4][:], PF[5][:]] if tt % 2 == 0 else [PF[2][:], PF[3][:]]
                for nh in range(2):
                    bank = ybanks[nh]
                    for hc in range(HC):
                        w = []
                        if hc == 0 and nh == 0 and yev[tt % 2] is not None:
                            w.append(yev[tt % 2])
                        mm(bank,
                           hTs[:, hc * 512 + tt * 128: hc * 512 + tt * 128 + 128],
                           W2sb[:, hc * 1024 + nh * 512: hc * 1024 + nh * 512 + 512],
                           start=(hc == 0), stop=False, waits=w)
                    mm(bank, onesr_bf[:], BROW[:, nh * 512:nh * 512 + 512],
                       False, True, incs=([(pe, 1)] if nh == 1 else []))
                yk = pe_inc()
                B.wait("v", *yk)
                ww = []
                prevk = st_store.get((st, tt - 2)) or st_store.get((st - 1, tt + 2))
                if prevk is not None:
                    ww.append((aq, prevk))
                B.op("v", lambda v, _o=yst2[tt % 2][:, 0:512], _p=ybanks[0]:
                     v.tensor_copy(_o, _p), waits=ww)
                B.op("v", lambda v, _o=yst2[tt % 2][:, 512:1024], _p=ybanks[1]:
                     v.tensor_copy(_o, _p))
                yev[tt % 2] = B.mark("v")
                st_store[(st, tt)] = B.dma(
                    "y", aq,
                    s_yv.ap()[st * 512:(st + 1) * 512].rearrange("(p j) d -> j p d", j=4)[tt],
                    yst2[tt % 2][:, 0:1024], waits=[yev[tt % 2]])
            ph2_mark["pe"] = pe.n
            ph2_vmark["m"] = B.mark("v")
            xfer[st] = dict(wall=t_w1q + t_w2 + [t_b1, t_b2],
                            xg=xg_toks, tp=tp_pe)
        return aq.snapshot()

    if debug:
        dbg_toks.append((aq, h1dbg_tok[NT - 1]))
        for tok in dbg_toks:
            B.wait("s", aq, tok[1])
            B.wait("v", aq, tok[1])
    B.wait_snap("g", p4_out_tok)
    p4_aq_snap = aq.snapshot()
    B.wait_snap("g", p4_aq_snap)
    b1sa = ab1p.rearrange("e p c -> (e p) c")
    sw1a = aW1p.rearrange("e p c -> (e p) c")
    sw2a = aW2p.rearrange("e p c -> (e p) c")
    ay_snap = sparse_moe(NSTA, sw1a, sw2a, b1sa, ab2p, s_xm, s_ay, s_atok,
                         AF.Gelu,
                         iq.snapshot(),
                         {"w1": [p4_vmark] + [aq.tok(tk[1]) for tk in dbg_toks],
                          "w2": [(pe, pe.n)],
                          "xg": [(pe, pe.n)]}, scale_cols=S2C)
    B.wait_snap("g", ay_snap)
    cmb_tok = {}
    for t in range(NT):
        wY = [(pe, pe.n)] if t == 0 else []
        ty1 = B.dma("g", iq, Ycmb[0][:, 0:1024], s_ay.ap(),
                    indirect_in=posA[:, 2 * t:2 * t + 1],
                    waits=wY + ([(oq, cmb_tok[t - 1])] if t >= 1 else []))
        ty2 = B.dma("g", iq, Ycmb[1][:, 0:1024], s_ay.ap(),
                    indirect_in=posA[:, 2 * t + 1:2 * t + 2], waits=[])
        B.op("v", lambda v, _t=t, _g=gcA[:, 2 * t:2 * t + 1]:
             v.tensor_scalar(Ycmb[0][:, 0:1024], Ycmb[0][:, 0:1024], _g, None, ALU.mult),
             waits=[(iq, ty1), (iq, ty2)])
        B.op("v", lambda v, _g=gcA[:, 2 * t + 1:2 * t + 2]:
             v.scalar_tensor_tensor(Ycmb[0][:, 0:1024], Ycmb[1][:, 0:1024], _g,
                                    Ycmb[0][:, 0:1024], ALU.mult, ALU.add))
        cvm = B.mark("v")
        cmb_tok[t] = B.dma("g", oq, OUT_T[t], Ycmb[0][:, 0:1024], accum=ALU.add,
                           waits=[cvm])
    if stop_after == "p6":
        return _finish_trunc(B, nc, [aq, wq, stq, oq, iq])
    moe_pe = (pe, pe.n)
    cmb_vm = B.mark("v")
    xnT_rl = []
    cT3 = cT[:].rearrange("p (d x) -> p d x", x=T)
    for t in range(NT):
        dst = cT3[:, :, t * 128:t * 128 + 128]
        xnT_rl.append(B.dma("y", aq, dst, s_xnT.ap()[t].rearrange("p (d x) -> p d x", x=128),
                            waits=[moe_pe, cmb_vm, (oq, cmb_tok[NT - 1])]))
    xnT_done = aq.snapshot()

    B.wait_snap("t", xnT_done)
    for t in range(NT):
        for dc in range(DC):
            mm(PF[0][:, t * 8:(t + 1) * 8], cT[:, dc * T + t * 128: dc * T + t * 128 + 128],
               gWsb[:, dc * 8:(dc + 1) * 8], start=(dc == 0), stop=False)
        mm(PF[0][:, t * 8:(t + 1) * 8], onesr_bf[:], RGB, False, False)
        mm(PF[0][:, t * 8:(t + 1) * 8], onesr_bf[:], REB, False, True,
           incs=([(pe, 1)] if t == NT - 1 else []))
    gk = pe_inc()
    B.wait("v", *gk)
    B.op("v", lambda v: v.tensor_copy(logitsA[:], PF[0][:, 0:128]))
    B.op("v", lambda v: v.memset(run_row, 0.0))
    cnt_pe_hist2 = {}
    for t in range(NT):
        blk = logitsA[:, t * 8:(t + 1) * 8]
        B.op("v", lambda v, _b=blk: v.reduce_max(mx, _b, axis=AX.X))
        B.op("v", lambda v, _b=blk: v.tensor_single_scalar(scr8a, _b, mx, ALU.is_equal))
        B.op("v", lambda v: v.tensor_scalar(scr8a, scr8a, -1000.0, 1000.0, ALU.mult, ALU.add))
        B.op("v", lambda v: v.tensor_add(scr8a, scr8a, IOTA))
        B.op("v", lambda v: v.tensor_reduce(i1c, scr8a, AX.X, ALU.min))
        B.op("v", lambda v, _o=ohF[:, t * 8:(t + 1) * 8]:
             v.tensor_single_scalar(_o, IOTA, i1c, ALU.is_equal))
        oh_slot = OHBF if t % 2 == 0 else OHBF2
        w_oh = [(pe, cnt_pe_hist2[t - 2])] if (t - 2) in cnt_pe_hist2 else []
        B.op("v", lambda v, _b=ohF[:, t * 8:(t + 1) * 8], _o=oh_slot:
             v.tensor_copy(_o, _b), waits=w_oh)
        vm2 = B.mark("v")
        B.wait("t", vm2[0], vm2[1])
        mm(PF[1][:, 0:8], LSTRICT, oh_slot, True, True)
        mm(PF[1][:, 16:24], onesr_f[:], run_row, True, True)
        mm(PF[1][0:1, 8:16], ONESCOL, oh_slot, True, True, incs=[(pe, 1)])
        ck = pe_inc()
        cnt_pe_hist2[t] = ck[1]
        B.wait("v", *ck)
        B.op("v", lambda v: v.tensor_copy(scr8b, PF[1][:, 16:24]))
        B.op("v", lambda v, _o=cntC[:, t * 8:(t + 1) * 8]: v.tensor_add(_o, PF[1][:, 0:8], scr8b))
        B.op("v", lambda v: v.tensor_add(run_row, run_row, PF[1][0:1, 8:16]))
    p7_vmark = B.mark("v")
    if debug:
        B.wait("y", *p7_vmark)
        B.dma("y", aq, d_ohf.ap().rearrange("(t p) e -> p t e", p=128),
              ohF[:].rearrange("p (t e) -> p t e", e=NE))
    B.op("v", lambda v: v.tensor_copy(krow, run_row))
    B.op("v", lambda v: v.memset(prow, 0.0))
    for m in range(11):
        B.op("v", lambda v, _m=m: v.tensor_single_scalar(et_row_tmp[:, 0:8], krow,
                                                         float(512 * _m), ALU.is_gt))
        B.op("v", lambda v: v.tensor_add(prow, prow, et_row_tmp[:, 0:8]))
    B.op("v", lambda v: v.tensor_scalar(prow, prow, 512.0, None, ALU.mult))
    B.op("v", lambda v: v.memset(lbase[:, 0:1], 0.0))
    for e in range(1, NE):
        B.op("v", lambda v, _e=e: v.tensor_add(lbase[:, _e:_e + 1], lbase[:, _e - 1:_e],
                                               prow[:, _e - 1:_e]))
    B.op("v", lambda v: v.tensor_add(endr, lbase, prow))
    B.op("v", lambda v: v.memset(et_row[:, 0:16], 0.0))
    for e in range(NE):
        B.op("v", lambda v, _e=e: v.tensor_single_scalar(et_row_tmp[:, 0:NSTF],
             ST512R[:, 0:NSTF], endr[:, _e:_e + 1], ALU.is_ge))
        B.op("v", lambda v: v.tensor_add(et_row[:, 0:NSTF], et_row[:, 0:NSTF],
                                         et_row_tmp[:, 0:NSTF]))
    B.op("v", lambda v: v.tensor_scalar(et_row[:, 0:NSTF], et_row[:, 0:NSTF], 7.0, None, ALU.min))
    vm = B.mark("v")
    B.wait("t", vm[0], vm[1])
    mm(PF[1][:, 40:40 + NSTF], onesr_f[:], et_row[:, 0:NSTF], True, True)
    mm(PF[1][:, 56:64], onesr_f[:], lbase, True, True, incs=[(pe, 1)])
    bk = pe_inc()
    B.wait("v", *bk)
    B.op("v", lambda v: v.tensor_copy(etb[:, 0:NSTF], PF[1][:, 40:40 + NSTF]))
    B.op("v", lambda v: v.tensor_copy(lbb[:], PF[1][:, 56:64]))
    sc_toks_f = []
    for t in range(NT):
        cntt = cntC[:, t * 8:(t + 1) * 8]
        oht = ohF[:, t * 8:(t + 1) * 8]
        B.op("v", lambda v, _c=cntt: v.tensor_add(scr8c, lbb[:], _c))
        B.op("v", lambda v, _oh=oht: v.tensor_mul(scr8c, scr8c, _oh))
        B.op("v", lambda v, _o=stats[:, 16:17]: v.reduce_sum(_o, scr8c, axis=AX.X))
        B.op("v", lambda v, _o=posF[:, t:t + 1]: v.tensor_copy(_o, stats[:, 16:17]))
        c0 = 2 * (t % 4)
        wsc = [(iq, sc_toks_f[t - 4])] if t >= 4 else []
        B.op("v", lambda v, _o=sct[:, c0:c0 + 1]: v.tensor_copy(_o, stats[:, 16:17]),
             waits=wsc)
        B.op("v", lambda v, _t=t:
             v.tensor_scalar(stats[:, 22:23], iotac[:], float(128 * _t), None, ALU.add))
        B.op("v", lambda v, _o=sct[:, c0 + 1:c0 + 2]: v.tensor_copy(_o, stats[:, 22:23]))
        vmk = B.mark("v")
        sc_toks_f.append(B.dma("g", iq, s_ftok.ap(), sct[:, c0 + 1:c0 + 2],
                               indirect_out=sct[:, c0:c0 + 1], bounds=rbF,
                               waits=[vmk]))
    b1sf = fb1p.rearrange("e p c -> (e p) c")
    sw1f = fW1p.rearrange("e p c -> (e p) c")
    sw2f = fW2p.rearrange("e p c -> (e p) c")
    fy_snap = sparse_moe(NSTF, sw1f, sw2f, b1sf, fb2p, s_xn, s_fy, s_ftok,
                         AF.Relu,
                         iq.snapshot(),
                         {"w1": [(pe, moe_pe[1])], "w2": [(pe, moe_pe[1])],
                          "xg": [(pe, moe_pe[1])]})
    B.wait_snap("g", fy_snap)
    for t in range(NT):
        wY = [(pe, pe.n)] if t == 0 else []
        if t >= 1:
            wY.append((oq, cmb_tok[("f", t - 1)]))
        ty1 = B.dma("g", iq, Ycmb[0][:, 0:1024], s_fy.ap(),
                    indirect_in=posF[:, t:t + 1], waits=wY)
        cmb_tok[("f", t)] = B.dma("g", oq, OUT_T[t], Ycmb[0][:, 0:1024], accum=ALU.add,
                                  waits=[(iq, ty1)])

    B.wait_snap("y", stq.snapshot())
    B.wait_snap("y", oq.snapshot())
    B.wait_snap("g", oq.snapshot())

    return _emit(B, nc)


def _finish_trunc(B, nc, qs):
    for q in qs:
        B.wait_snap("y", q.snapshot())
        B.wait_snap("g", q.snapshot())
    return _emit(B, nc)


def _emit(B, nc):
    with B.stack:
        with nc.Block() as block:
            @block.tensor
            def _(t_):
                for f in B.ops["t"]:
                    f(t_)

            @block.vector
            def _(v):
                for f in B.ops["v"]:
                    f(v)

            @block.scalar
            def _(s):
                for f in B.ops["s"]:
                    f(s)

            @block.gpsimd
            def _(g):
                for f in B.ops["g"]:
                    f(g)

            @block.sync
            def _(y):
                for f in B.ops["y"]:
                    f(y)
    return B


def kernel(**inputs):
    c = make_constants()
    B = build(debug=False)
    xfull = np.ascontiguousarray(np.asarray(inputs["L0_attn_x"], np.float32))

    def row(v):
        return np.ascontiguousarray(np.asarray(v, np.float32).reshape(1, -1))

    def arr(v):
        return np.ascontiguousarray(np.asarray(v, np.float32))

    shared = dict(
        Wq=arr(inputs["L0_attn_Wq"]), bq=row(inputs["L0_attn_bq"]),
        Wk=arr(inputs["L0_attn_Wk"]), bk=row(inputs["L0_attn_bk"]),
        Wv=arr(inputs["L0_attn_Wv"]), bv=row(inputs["L0_attn_bv"]),
        Wg=arr(inputs["L0_attn_Wg"]), bg=row(inputs["L0_attn_bg"]),
        Wo=arr(inputs["L0_attn_Wo"]), bo=row(inputs["L0_attn_bo"]),
        rW=np.ascontiguousarray(np.asarray(inputs["L0_attn_router_W"], np.float32)
                                .reshape(DC, 128, NE).transpose(1, 0, 2)),
        rb=row(inputs["L0_attn_router_b"]),
        aW1p=np.ascontiguousarray(np.asarray(inputs["L0_attn_W1"], np.float32)
                                  .reshape(NE, DC, 128, 4, HID // 4)
                                  .transpose(0, 2, 3, 1, 4)
                                  .reshape(NE, 128, DC * HID)),
        aW2p=np.ascontiguousarray(np.asarray(inputs["L0_attn_W2"], np.float32)
                                  .reshape(NE, HC, 128, D).transpose(0, 2, 1, 3)
                                  .reshape(NE, 128, HC * D)),
        ab1p=np.ascontiguousarray(np.asarray(inputs["L0_attn_b1"], np.float32)
                                  .reshape(NE, HID // 128, 128).transpose(0, 2, 1)),
        ab2p=arr(inputs["L0_attn_b2"]),
        s1c=np.ascontiguousarray(np.asarray(inputs["L0_attn_norm1_scale"], np.float32).reshape(DC, 128).T),
        s2c=np.ascontiguousarray(np.asarray(inputs["L0_attn_norm2_scale"], np.float32).reshape(DC, 128).T),
        gwT=np.ascontiguousarray(np.asarray(inputs["L0_ffn_gate_w"], np.float32).T
                                 .reshape(DC, 128, NE).transpose(1, 0, 2)),
        gbr=row(inputs["L0_ffn_gate_b"]), ebr=row(inputs["L0_ffn_expert_bias"]),
        fW1p=np.ascontiguousarray(np.asarray(inputs["L0_ffn_w1"], np.float32)
                                  .reshape(NE, DC, 128, 4, HID // 4)
                                  .transpose(0, 2, 3, 1, 4)
                                  .reshape(NE, 128, DC * HID)),
        fW2p=np.ascontiguousarray(np.asarray(inputs["L0_ffn_w2"], np.float32)
                                  .reshape(NE, HC, 128, D).transpose(0, 2, 1, 3)
                                  .reshape(NE, 128, HC * D)),
        fb1p=np.ascontiguousarray(np.asarray(inputs["L0_ffn_b1"], np.float32)
                                  .reshape(NE, HID // 128, 128).transpose(0, 2, 1)),
        fb2p=arr(inputs["L0_ffn_b2"]),
        ident=c["ident"], lstrict=c["lstrict"], ones_bf=c["ones_bf"], ones_f=c["ones_f"],
        onescol_bf=c["onescol_bf"], iota8=c["iota8"], iotac=c["iotac"],
        st512=c["st512"], cosx=c["cosx"], sinx=c["sinx"],
    )
    in_maps = []
    for i in range(NCORES):
        m = dict(shared)
        m["x"] = np.ascontiguousarray(xfull[i])
        mc = np.zeros((NE, 1), np.float32)
        mc[:i] = 1.0
        m["maskcore"] = mc
        in_maps.append(m)
    res = run_bass_kernel_spmd(B.nc, in_maps, core_ids=list(range(NCORES)))
    return np.stack([res.results[i]["out"] for i in range(NCORES)], axis=0)


# version 12
# speedup vs baseline: 1.1745x; 1.1745x over previous
import numpy as np
import ml_dtypes
from contextlib import ExitStack

import concourse.bass as bass
import concourse.mybir as mybir
from concourse.bass_utils import run_bass_kernel_spmd

F32 = mybir.dt.float32
BF16 = mybir.dt.bfloat16
AF = mybir.ActivationFunctionType
ALU = mybir.AluOpType
AX = mybir.AxisListType

T = 2048
D = 1024
NT = 16
DC = 8
H = 16
DK = 64
CH = 64
NCH = T // CH
NE = 8
HID = 4096
HC = 32
CAP = 5120
NCORES = 8
NSTA = 15
NSTF = 11
EPS_LN = 1e-5
EPS_DEN = 1e-6


class SC:
    def __init__(self, handle):
        self.h = handle
        self.n = 0


class DQ:

    def __init__(self, sems):
        self.sems = sems
        self.k = len(sems)
        self.i = 0

    def tok(self, token):
        j = token // 16 - 1
        return (self.sems[j % self.k], 16 * (j // self.k + 1))

    def snapshot(self):
        return [(s, s.n) for s in self.sems]


class Build:

    def __init__(self, debug=False):
        self.nc = bass.Bass("TRN2", target_bir_lowering=False, debug=False)
        self.debug = debug
        self.ops = {k: [] for k in ("t", "v", "s", "g", "y")}
        self.auto = {}
        self.cnt = {"v": 0, "s": 0, "g": 0}
        self.stack = ExitStack()
        self.dbg = {}

    def op(self, eng, build_fn, waits=(), incs=()):
        my_idx = self.cnt[eng]
        self.cnt[eng] = my_idx + 1
        waits, incs = tuple(self.norm_waits(waits)), tuple(incs)
        a = self.auto[eng]

        def f(e, _w=waits, _i=incs, _k=my_idx, _fn=build_fn, _a=a):
            for sc, val in _w:
                if val > 0:
                    e.wait_ge(sc.h, val)
            if _k > 0:
                e.wait_ge(_a.h, _k)
            inst = _fn(e)
            inst.then_inc(_a.h, 1)
            for sc, k in _i:
                inst.then_inc(sc.h, k)

        self.ops[eng].append(f)

    def raw(self, eng, build_fn, waits=(), incs=()):
        waits, incs = tuple(self.norm_waits(waits)), tuple(incs)

        def f(e, _w=waits, _i=incs, _fn=build_fn):
            for sc, val in _w:
                if val > 0:
                    e.wait_ge(sc.h, val)
            inst = _fn(e)
            if inst is not None:
                for sc, k in _i:
                    inst.then_inc(sc.h, k)

        self.ops[eng].append(f)

    def wait(self, eng, sc, val=None):
        if isinstance(sc, DQ):
            assert val is not None
            if isinstance(val, list):
                for s, v in val:
                    self.wait(eng, s, v)
                return
            sc, val = sc.tok(val)
        v = sc.n if val is None else val
        if v > 0:
            def f(e, _s=sc, _v=v):
                e.wait_ge(_s.h, _v)
            self.ops[eng].append(f)

    def wait_snap(self, eng, snap):
        for s, v in snap:
            self.wait(eng, s, v)

    def mark(self, eng):
        return (self.auto[eng], self.cnt[eng])

    def norm_waits(self, waits):
        out = []
        for w in waits:
            if isinstance(w[0], DQ):
                out.append(w[0].tok(w[1]))
            else:
                out.append(w)
        return out

    def dma(self, eng, q, out, in_, accum=None, waits=(), indirect_out=None,
            indirect_in=None, bounds=None, elem_off=0):
        kw = {}
        if accum is not None:
            kw["accum_op"] = accum
        waits = tuple(self.norm_waits(waits))
        j = q.i
        q.i += 1
        s = q.sems[j % q.k]
        gate = s.n
        s.n += 16
        ind = (indirect_out is not None) or (indirect_in is not None)

        def f(e, _w=waits, _o=out, _i=in_, _kw=kw, _s=s, _g=gate,
              _io=indirect_out, _ii=indirect_in, _b=bounds, _eo=elem_off):
            for sc, val in _w:
                if val > 0:
                    e.wait_ge(sc.h, val)
            if _g > 0:
                e.wait_ge(_s.h, _g)
            if _io is not None or _ii is not None:
                ikw = {}
                if _eo:
                    ikw["element_offset"] = _eo
                if _b is not None:
                    ikw["bounds_check"] = _b
                    ikw["oob_is_err"] = False
                e.indirect_dma_start(
                    out=_o,
                    out_offset=(bass.IndirectOffsetOnAxis(_io, 0) if _io is not None else None),
                    in_=_i,
                    in_offset=(bass.IndirectOffsetOnAxis(_ii, 0) if _ii is not None else None),
                    **ikw).then_inc(_s.h, 16)
            else:
                e.dma_start(out=_o, in_=_i, **_kw).then_inc(_s.h, 16)

        self.ops[eng].append(f)
        return 16 * (j + 1)

    def scratch(self, name, shape, dtype):
        kind = "ExternalOutput" if self.debug else "Internal"
        t = self.nc.dram_tensor(name, list(shape), dtype, kind=kind)
        if self.debug:
            self.dbg[name] = t
        return t


def _bf(x):
    return np.asarray(x, dtype=ml_dtypes.bfloat16)


def make_constants():
    c = {}
    c["ident"] = _bf(np.eye(128, dtype=np.float32))
    c["lstrict"] = _bf(np.tril(np.ones((128, 128), np.float32), -1))
    c["ones_bf"] = _bf(np.ones((1, 128), np.float32))
    c["ones_f"] = np.ones((1, 128), np.float32)
    c["onescol_bf"] = _bf(np.ones((128, 1), np.float32))
    c["iota8"] = np.tile(np.arange(8, dtype=np.float32), (128, 1))
    c["iotac"] = np.arange(128, dtype=np.float32).reshape(128, 1)
    c["st512"] = (np.arange(16, dtype=np.float32) * 512).reshape(1, 16)
    inv = (1.0 / 10000.0 ** (np.arange(32, dtype=np.float32) / 32.0)).astype(np.float32)
    ang = np.arange(T, dtype=np.float32)[:, None] * inv[None, :]
    cosv = np.cos(ang).astype(np.float32)
    sinv = np.sin(ang).astype(np.float32)
    cosx = np.concatenate([cosv, cosv], axis=1)
    sinx = np.concatenate([-sinv, sinv], axis=1)
    c["cosx"] = np.ascontiguousarray(cosx.reshape(NT, 128, 64).transpose(1, 0, 2))
    c["sinx"] = np.ascontiguousarray(sinx.reshape(NT, 128, 64).transpose(1, 0, 2))
    return c


def build(debug=False, stop_after="all", p3_mode=3):
    B = Build(debug=debug)
    nc = B.nc
    E = B.stack.enter_context

    def din(name, shape, dtype=F32):
        return nc.dram_tensor(name, list(shape), dtype, kind="ExternalInput").ap()

    x_in = din("x", (T, D))
    Wq, bq = din("Wq", (D, D)), din("bq", (1, D))
    Wk, bk = din("Wk", (D, D)), din("bk", (1, D))
    Wv, bv = din("Wv", (D, D)), din("bv", (1, D))
    Wg, bg = din("Wg", (D, D)), din("bg", (1, D))
    Wo, bo = din("Wo", (D, D)), din("bo", (1, D))
    rW, rb = din("rW", (128, DC, NE)), din("rb", (1, NE))
    aW1p = din("aW1p", (NE, 128, DC * HID))
    aW2p = din("aW2p", (NE, 128, HC * D))
    ab1p = din("ab1p", (NE, 128, HID // 128))
    ab2p = din("ab2p", (NE, D))
    s1c, s2c = din("s1c", (128, DC)), din("s2c", (128, DC))
    gwT = din("gwT", (128, DC, NE))
    gbr, ebr = din("gbr", (1, NE)), din("ebr", (1, NE))
    fW1p = din("fW1p", (NE, 128, DC * HID))
    fW2p = din("fW2p", (NE, 128, HC * D))
    fb1p = din("fb1p", (NE, 128, HID // 128))
    fb2p = din("fb2p", (NE, D))
    ident_i = din("ident", (128, 128), BF16)
    lstrict_i = din("lstrict", (128, 128), BF16)
    ones_bf_i = din("ones_bf", (1, 128), BF16)
    ones_f_i = din("ones_f", (1, 128))
    onescol_i = din("onescol_bf", (128, 1), BF16)
    iota8_i = din("iota8", (128, NE))
    iotac_i = din("iotac", (128, 1))
    st512_i = din("st512", (1, 16))
    cosx_i = din("cosx", (128, NT, 64))
    sinx_i = din("sinx", (128, NT, 64))
    maskcore = din("maskcore", (NE, 1))

    OUT = nc.dram_tensor("out", [T, D], F32, kind="ExternalOutput").ap()
    OUT_T = OUT.rearrange("(t p) d -> t p d", p=128)
    X_T = x_in.rearrange("(t p) d -> t p d", p=128)

    s_qT = B.scratch("s_qT", (NT, 128, D), BF16)
    s_k = B.scratch("s_k", (NT, 128, D), BF16)
    s_v = B.scratch("s_v", (NT, 128, H * 65), BF16)
    s_g = B.scratch("s_g", (NT, 128, D), F32)
    s_ao = B.scratch("s_ao", (NT, 128, D), BF16)
    s_xnT = B.scratch("s_xnT", (NT, 128, D), BF16)
    s_xm = B.scratch("s_xm", (T, D), BF16)
    s_xn = B.scratch("s_xn", (T, D), BF16)
    s_atok = B.scratch("s_atok", (NSTA * 512, 1), mybir.dt.int32)
    s_ftok = B.scratch("s_ftok", (NSTF * 512, 1), mybir.dt.int32)
    s_ay = B.scratch("s_ay", (NSTA * 512, D), F32)
    s_fy = B.scratch("s_fy", (NSTF * 512, D), F32)
    cc_in = nc.dram_tensor("cc_in", [1, NE], F32)
    cc_out = nc.dram_tensor("cc_out", [NCORES, NE], F32)
    if debug:
        d_xn = B.scratch("d_xn", (T, D), F32)
        d_cnt = B.scratch("d_cnt", (T, NE), F32)
        d_ohf = B.scratch("d_ohf", (T, NE), F32)
        d_h1 = B.scratch("d_h1", (T, D), F32)
        d_rows = B.scratch("d_rows", (1, 128), F32)

    cT = E(nc.sbuf_tensor("cT", [128, DC * T], BF16))
    big1 = E(nc.sbuf_tensor("big1", [128, 16384], F32))
    big2 = E(nc.sbuf_tensor("big2", [128, 16384], F32))
    work1 = E(nc.sbuf_tensor("work1", [128, 8192], F32))
    work2 = E(nc.sbuf_tensor("work2", [128, 2048], F32))
    smalls = E(nc.sbuf_tensor("smalls", [128, 1184], F32))
    consts = E(nc.sbuf_tensor("consts", [128, 24], F32))
    cbf = E(nc.sbuf_tensor("cbf", [128, 400], BF16))
    onesr_bf = E(nc.sbuf_tensor("onesr_bf", [1, 128], BF16))
    onesr_f = E(nc.sbuf_tensor("onesr_f", [1, 128], F32))
    bcol = E(nc.sbuf_tensor("bcol", [128, NE * 32], F32))
    mcol = E(nc.sbuf_tensor("mcol", [NE, 1], F32))
    iotac = E(nc.sbuf_tensor("iotac_sb", [128, 1], F32))
    posA = E(nc.sbuf_tensor("posA", [128, 32], mybir.dt.int32))
    posF = E(nc.sbuf_tensor("posF", [128, 16], mybir.dt.int32))
    idxw = E(nc.sbuf_tensor("idxw", [128, 4], mybir.dt.int32))
    tik = E(nc.sbuf_tensor("tik", [128, 8], mybir.dt.int32))
    sct = E(nc.sbuf_tensor("sct", [128, 8], mybir.dt.int32))
    b1g = E(nc.sbuf_tensor("b1g", [128, 32], F32))
    etb = E(nc.sbuf_tensor("etb", [128, 16], F32))
    lbb = E(nc.sbuf_tensor("lbb", [128, NE], F32))
    ccsb = E(nc.sbuf_tensor("ccsb", [NCORES, NE], F32))


    PF = [E(nc.psum_tensor(f"pf{i}", [128, 512], F32)) for i in range(6)]
    PT = [E(nc.psum_tensor(f"pt{i}", [128, 1024], BF16)) for i in range(2)]

    B.auto["v"] = SC(E(nc.semaphore("auto_v")))
    B.auto["s"] = SC(E(nc.semaphore("auto_s")))
    B.auto["g"] = SC(E(nc.semaphore("auto_g")))
    pe = SC(E(nc.semaphore("pe")))
    wq = DQ([SC(E(nc.semaphore(f"wq{i}"))) for i in range(4)])
    aq = DQ([SC(E(nc.semaphore(f"aq{i}"))) for i in range(8)])
    oq = DQ([SC(E(nc.semaphore(f"oq{i}"))) for i in range(4)])
    iq = DQ([SC(E(nc.semaphore(f"iq{i}"))) for i in range(6)])
    rbA = E(nc.gpsimd.register("rbA"))
    rbF = E(nc.gpsimd.register("rbF"))
    B.raw("g", lambda g: g.reg_mov(rbA, NSTA * 512 - 1) and None)
    B.raw("g", lambda g: g.reg_mov(rbF, NSTF * 512 - 1) and None)
    stq = DQ([SC(E(nc.semaphore(f"stq{i}"))) for i in range(4)])
    cc = SC(E(nc.semaphore("cc")))

    def mm(out_ap, lhsT, rhs, start, stop, waits=(), incs=()):
        def f(t_, _o=out_ap, _l=lhsT, _r=rhs, _s=start, _p=stop):
            return t_.matmul(_o, _l, _r, start=_s, stop=_p, skip_group_check=True)
        B.raw("t", f, waits=waits, incs=incs)

    def pe_inc():
        pe.n += 1
        return (pe, pe.n)

    IDENT = cbf[:, 0:128]
    LSTRICT = cbf[:, 128:256]
    ONESCOL = cbf[:, 256:257]
    rWsb = cbf[:, 260:324]
    gWsb = cbf[:, 324:388]
    OHBF = cbf[:, 388:396]
    OHBF2 = smalls[:, 400:404].bitcast(BF16)

    def transp(out_ap, in_ap, waits=(), incs=()):
        def f(t_, _o=out_ap, _i=in_ap):
            return t_.transpose(_o, _i, IDENT)
        B.raw("t", f, waits=waits, incs=incs)

    class TT:

        def __init__(self):
            self.bank_free = [None, None]

        def run(self, src_tile, dst_fn, in_mark, extra_first_wait=(), scale_col=None):
            for dc in range(DC):
                bk = PT[dc % 2]
                w = []
                if dc == 0 and in_mark is not None:
                    w.append(in_mark)
                if self.bank_free[dc % 2] is not None:
                    w.append(self.bank_free[dc % 2])
                transp(bk[:, 0:128], src_tile[:, dc * 128:(dc + 1) * 128],
                       waits=w, incs=[(pe, 1)])
                m = pe_inc()
                B.wait("v", *m)
                ww = list(extra_first_wait) if dc == 0 else []
                if scale_col is None:
                    B.op("v", lambda v, _o=dst_fn(dc), _i=bk[:, 0:128]:
                         v.tensor_copy(_o, _i), waits=ww)
                else:
                    B.op("v", lambda v, _o=dst_fn(dc), _i=bk[:, 0:128],
                         _s=scale_col(dc):
                         v.tensor_scalar(_o, _i, _s, None, ALU.mult), waits=ww)
                self.bank_free[dc % 2] = B.mark("v")

    tpose = TT()

    B.dma("y", aq, onesr_bf[:], ones_bf_i)
    B.dma("y", aq, onesr_f[:], ones_f_i)
    B.dma("y", aq, mcol[:], maskcore)
    IOTA = consts[:, 0:8]
    S1C = consts[:, 8:16]
    S2C = consts[:, 16:24]
    B.dma("y", aq, IOTA, iota8_i)
    B.dma("y", aq, iotac[:], iotac_i)
    B.dma("y", aq, S1C, s1c)
    B.dma("y", aq, S2C, s2c)
    B.dma("y", aq, IDENT, ident_i)
    B.dma("y", aq, LSTRICT, lstrict_i)
    B.dma("y", aq, ONESCOL, onescol_i)
    BROW = smalls[0:1, 640:1152].bitcast(BF16)
    RRB = smalls[0:1, 1168:1172].bitcast(BF16)
    RGB = smalls[0:1, 1172:1176].bitcast(BF16)
    REB = smalls[0:1, 1176:1180].bitcast(BF16)
    for dst, src_ in ((RRB, rb), (RGB, gbr), (REB, ebr)):
        B.dma("g", wq, dst, src_)

    COS = work1[:, 0:NT * 64]
    SIN = work1[:, NT * 64:2 * NT * 64]
    B.dma("y", aq, COS, cosx_i.rearrange("p t c -> p (t c)"))
    B.dma("y", aq, SIN, sinx_i.rearrange("p t c -> p (t c)"))
    consts_tok = aq.snapshot()
    B.dma("g", wq, rWsb.rearrange("p (c e) -> p c e", e=NE), rW)
    B.dma("g", wq, gWsb.rearrange("p (c e) -> p c e", e=NE), gwT)
    wq_const_tok = wq.snapshot()

    stats = smalls[:, 0:32]
    logitsA = smalls[:, 32:160]
    cntC = smalls[:, 160:288]
    wgA = smalls[:, 288:416]
    ohF = smalls[:, 416:544]
    gcols = smalls[:, 544:608]
    scr8a = smalls[:, 608:616]
    scr8b = smalls[:, 616:624]
    scr8c = smalls[:, 624:632]
    thr_b = smalls[:, 632:640]
    run_row = smalls[0:1, 1152:1160]
    off_row = smalls[0:1, 1160:1168]
    ST512R = smalls[0:1, 288:304]
    krow = smalls[0:1, 304:312]
    prow = smalls[0:1, 312:320]
    lbase = smalls[0:1, 320:328]
    endr = smalls[0:1, 328:336]
    et_row = smalls[0:1, 336:352]
    et_row_tmp = smalls[0:1, 384:400]
    gcA = smalls[:, 352:384]
    B.dma("y", aq, ST512R, st512_i)
    B.op("v", lambda v: v.memset(posA[:], 0))
    zvm = B.mark("v")
    for tgt, nst in ((s_atok, NSTA), (s_ftok, NSTF)):
        cols = nst * 4
        done = 0
        while done < cols:
            cw = min(32, cols - done)
            B.dma("y", aq, tgt.ap()[done * 128:(done + cw) * 128].rearrange(
                "(p c) one -> p (c one)", p=128), posA[:, 0:cw], waits=[zvm])
            done += cw


    xn = big1[:]
    xstage = [work1[:, 2048 + i * 1024:2048 + (i + 1) * 1024] for i in range(2)]
    sq = work1[:, 4096:4096 + 1024]
    x_tok = [None] * NT
    cons_mark = {}
    B.wait_snap("v", consts_tok)
    B.wait_snap("s", consts_tok)
    eps_col = stats[:, 30:31]
    B.op("v", lambda v: v.memset(eps_col, EPS_LN))
    for t in range(NT):
        w = []
        if t >= 2:
            w.extend(cons_mark[t - 2])
        x_tok[t] = B.dma("y", aq, xstage[t % 2], X_T[t], waits=w)
        xt = xstage[t % 2]
        xnt = xn[:, t * 1024:(t + 1) * 1024]
        c_s1 = stats[:, 0:1]
        c_ss = stats[:, 1:2]
        c_m = stats[:, 2:3]
        c_r = stats[:, 3:4]
        B.wait("v", aq, x_tok[t])
        B.wait("s", aq, x_tok[t])
        prev_vm0 = B.mark("v")
        B.op("v", lambda v, _x=xt, _o=c_s1: v.reduce_sum(_o, _x, axis=AX.X))
        B.op("s", lambda s, _x=xt, _a=c_ss: s.activation(
            out=sq[:], in_=_x, func=AF.Square, accum_out=_a), waits=[prev_vm0])
        B.op("v", lambda v, _o=c_m, _i=c_s1: v.tensor_scalar(_o, _i, 1.0 / D, None, ALU.mult))
        sm = B.mark("s")
        B.wait("v", *sm)
        B.op("v", lambda v, _o=c_s1, _m=c_m: v.tensor_mul(_o, _m, _m))
        B.op("v", lambda v, _o=c_ss, _i=c_ss: v.tensor_scalar(_o, _i, 1.0 / D, None, ALU.mult))
        B.op("v", lambda v, _o=c_ss, _a=c_ss, _b=c_s1: v.tensor_sub(_o, _a, _b))
        vm = B.mark("v")
        B.wait("s", *vm)
        B.op("s", lambda s, _o=c_ss, _i=c_ss: s.activation(out=_o, in_=_i, func=AF.Sqrt, bias=eps_col))
        sm = B.mark("s")
        B.wait("v", *sm)
        B.op("v", lambda v, _o=c_r, _i=c_ss: v.reciprocal(_o, _i))
        B.op("v", lambda v, _o=xnt, _x=xt, _m=c_m, _r=c_r:
             v.tensor_scalar(_o, _x, _m, _r, ALU.subtract, ALU.mult))
        cons_mark[t] = [B.mark("v"), B.mark("s")]
    dbg_toks = []
    if debug:
        vm = B.mark("v")
        B.wait("y", *vm)
        dbg_toks.append((aq, B.dma("y", aq, d_xn.ap().rearrange("(t p) d -> p t d", p=128),
              xn.rearrange("p (t d) -> p t d", d=D))))

    if stop_after == "p0":
        return _finish_trunc(B, nc, [aq, wq])
    B.wait_snap("t", consts_tok)

    abf = [work1[:, 5120 + i * 512:5120 + (i + 1) * 512].bitcast(BF16) for i in range(2)]
    xnbf = [work1[:, 6144 + i * 512:6144 + (i + 1) * 512].bitcast(BF16) for i in range(2)]
    tsg = [work1[:, 7168 + i * 512:7168 + (i + 1) * 512].bitcast(BF16) for i in range(2)]
    xnT_spill = {}
    xn_sp = {}
    tpose_done_pe = {}
    for t in range(NT):
        xnt = xn[:, t * 1024:(t + 1) * 1024]
        c_ss = stats[:, 4:5]
        c_r1 = stats[:, 5:6]
        a_t = abf[t % 2]
        xn_t = xnbf[t % 2]
        ts_t = tsg[t % 2]
        prev_vm1 = B.mark("v")
        B.op("s", lambda s, _x=xnt, _a=c_ss: s.activation(
            out=sq[:], in_=_x, func=AF.Square, accum_out=_a), waits=[prev_vm1])
        B.op("s", lambda s, _o=c_ss, _i=c_ss: s.activation(out=_o, in_=_i, func=AF.Sqrt, scale=1.0 / D))
        sm = B.mark("s")
        B.wait("v", *sm)
        B.op("v", lambda v, _o=c_r1, _i=c_ss: v.reciprocal(_o, _i))
        wv = [(pe, tpose_done_pe[t - 2])] if t >= 2 else []
        B.op("v", lambda v, _o=a_t, _x=xnt, _r=c_r1:
             v.tensor_scalar(_o, _x, _r, None, ALU.mult), waits=wv)
        wxn = [(aq, xn_sp[t - 2])] if (t - 2) in xn_sp else []
        B.op("v", lambda v, _o=xn_t, _x=xnt: v.tensor_copy(_o, _x), waits=wxn)
        vm = B.mark("v")
        tpose.run(a_t, lambda dc, _t=t: cT[:, dc * T + _t * 128: dc * T + _t * 128 + 128], vm,
                  scale_col=lambda dc: S1C[:, dc:dc + 1])
        war = [(aq, xnT_spill[t - 2])] if t >= 2 else []
        tpose.run(xn_t, lambda dc, _ts=ts_t: _ts[:, dc * 128:(dc + 1) * 128], None,
                  extra_first_wait=war)
        tpose_done_pe[t] = pe.n
        vm2 = B.mark("v")
        B.wait("y", *vm2)
        xnT_spill[t] = B.dma("y", aq, s_xnT.ap()[t], ts_t)
        xn_sp[t] = B.dma("y", aq, s_xn.ap().rearrange("(t p) d -> t p d", p=128)[t], xn_t,
                         waits=[vm2])
    p1_vmark = B.mark("v")

    if stop_after == "p1":
        return _finish_trunc(B, nc, [aq, wq])
    wbuf = big2[:].bitcast(BF16)
    qkst = [work1[:, 2048 + i * 512:2048 + (i + 1) * 512].bitcast(BF16) for i in range(2)]
    tmp1 = work1[:, 3072:3584]
    tmp2 = work1[:, 3584:4096]
    tmp3 = work1[:, 4096:4608]
    v65 = [work1[:, 4608 + i * 520:4608 + (i + 1) * 520].bitcast(BF16) for i in range(2)]
    qTst = [work1[:, 5648 + i * 512:5648 + (i + 1) * 512].bitcast(BF16) for i in range(2)]
    g32 = [work2[:, i * 1024:(i + 1) * 1024] for i in range(2)]

    p1_aq = aq.snapshot()
    B.wait_snap("v", p1_aq)
    B.wait_snap("s", p1_aq)
    B.op("v", lambda v: v.memset(
        v65[0].rearrange("p (h c) -> p h c", c=65)[:, :, 64:65], 1.0))
    B.op("v", lambda v: v.memset(
        v65[1].rearrange("p (h c) -> p h c", c=65)[:, :, 64:65], 1.0))

    PROJS = [("q", Wq, bq, s_qT), ("k", Wk, bk, s_k), ("v", Wv, bv, s_v), ("g", Wg, bg, s_g)]
    spill_tok = {}
    stage_cons_pe = {}
    bank_cons = {0: p1_vmark, 1: p1_vmark}
    wslot_last_pe = {0: None, 1: None}
    for pi, (pname, Wsrc, brow, sdst) in enumerate(PROJS):
        wslot = wbuf[:, (pi % 2) * 8192:(pi % 2) * 8192 + 8192]
        wwar = [(pe, wslot_last_pe[pi % 2])] if wslot_last_pe[pi % 2] else []
        wtok = B.dma("g", wq, wslot.rearrange("p (c n) -> p c n", n=D),
                     Wsrc.rearrange("(c p) n -> p c n", p=128), waits=wwar)
        btok = B.dma("g", wq, BROW, brow, waits=[(pe, pe.n)])
        B.wait("t", wq, btok)
        B.wait("t", wq, wtok)
        if pi == 0:
            B.wait("t", p1_vmark[0], p1_vmark[1])
        for t in range(NT):
            for nh in range(2):
                gidx = t * 2 + nh
                bank = PF[gidx % 2]
                for dc in range(DC):
                    w = []
                    if dc == 0:
                        w.append(bank_cons[gidx % 2])
                    mm(bank[:], cT[:, dc * T + t * 128: dc * T + t * 128 + 128],
                       wslot[:, dc * 1024 + nh * 512: dc * 1024 + nh * 512 + 512],
                       start=(dc == 0), stop=False, waits=w)
                mm(bank[:], onesr_bf[:], BROW[:, nh * 512:nh * 512 + 512], False, True,
                   incs=[(pe, 1)])
                mmk = pe_inc()
                if pname in ("q", "k"):
                    stg = qkst[t % 2]
                    pv = bank[:].rearrange("p (h c) -> p h c", c=64)
                    cb = COS[:, t * 64:(t + 1) * 64].unsqueeze(1).broadcast_to([128, 8, 64])
                    sb0 = SIN[:, t * 64:t * 64 + 32].unsqueeze(1).broadcast_to([128, 8, 32])
                    sb1 = SIN[:, t * 64 + 32:t * 64 + 64].unsqueeze(1).broadcast_to([128, 8, 32])
                    t1v = tmp1[:].rearrange("p (h c) -> p h c", c=64)
                    t2v = tmp2[:].rearrange("p (h c) -> p h c", c=64)
                    B.wait("v", *mmk)
                    B.op("v", lambda v, _p=pv, _c=cb, _o=t1v: v.tensor_mul(_o, _p, _c))
                    B.op("v", lambda v, _p=pv, _s=sb0, _o=t2v:
                         v.tensor_mul(_o[:, :, 0:32], _p[:, :, 32:64], _s))
                    B.op("v", lambda v, _p=pv, _s=sb1, _o=t2v:
                         v.tensor_mul(_o[:, :, 32:64], _p[:, :, 0:32], _s))
                    bank_cons[gidx % 2] = B.mark("v")
                    B.op("v", lambda v: v.tensor_add(tmp1[:], tmp1[:], tmp2[:]))
                    B.op("v", lambda v: v.tensor_scalar(tmp2[:], tmp1[:], 0.0, None, ALU.min))
                    vm = B.mark("v")
                    B.wait("s", *vm)
                    B.op("s", lambda s: s.activation(out=tmp3[:], in_=tmp2[:], func=AF.Exp))
                    sm = B.mark("s")
                    B.wait("v", *sm)
                    ww = []
                    if pname == "q" and (pi, t - 2) in stage_cons_pe:
                        ww.append((pe, stage_cons_pe[(pi, t - 2)]))
                    if pname == "k" and (pi, t - 2) in spill_tok:
                        ww.append((aq, spill_tok[(pi, t - 2)]))
                    B.op("v", lambda v, _o=stg[:, nh * 512:nh * 512 + 512]:
                         v.scalar_tensor_tensor(_o, tmp1[:], 0.0, tmp3[:], ALU.max, ALU.add),
                         waits=ww)
                elif pname == "v":
                    vst = v65[t % 2]
                    ww = [(aq, spill_tok[(pi, t - 2)])] if (pi, t - 2) in spill_tok else []
                    B.wait("v", *mmk)
                    B.op("v", lambda v,
                         _o=vst.rearrange("p (h c) -> p h c", c=65)[:, nh * 8:nh * 8 + 8, 0:64],
                         _i=bank[:].rearrange("p (h c) -> p h c", c=64):
                         v.tensor_copy(_o, _i), waits=ww)
                    bank_cons[gidx % 2] = B.mark("v")
                elif pname == "g":
                    gst = g32[t % 2]
                    ww = [(aq, spill_tok[(pi, t - 2)])] if (pi, t - 2) in spill_tok else []
                    B.wait("s", *mmk)
                    B.op("s", lambda s, _o=gst[:, nh * 512:nh * 512 + 512], _i=bank[:]:
                         s.activation(out=_o, in_=_i, func=AF.Sigmoid), waits=ww)
                    bank_cons[gidx % 2] = B.mark("s")
            if pname == "q":
                stg = qkst[t % 2]
                qts = qTst[t % 2]
                vm = B.mark("v")
                war = [(aq, spill_tok[(pi, t - 2)])] if (pi, t - 2) in spill_tok else []
                tpose.run(stg, lambda dc, _q=qts: _q[:, dc * 128:(dc + 1) * 128], vm,
                          extra_first_wait=war)
                stage_cons_pe[(pi, t)] = pe.n
                vm2 = B.mark("v")
                B.wait("y", *vm2)
                spill_tok[(pi, t)] = B.dma("y", aq, sdst.ap()[t], qts)
            elif pname == "k":
                vm = B.mark("v")
                B.wait("y", *vm)
                spill_tok[(pi, t)] = B.dma("y", aq, sdst.ap()[t], qkst[t % 2])
            elif pname == "v":
                vm = B.mark("v")
                B.wait("y", *vm)
                spill_tok[(pi, t)] = B.dma("y", aq, sdst.ap()[t], v65[t % 2])
            elif pname == "g":
                sm = B.mark("s")
                B.wait("y", *sm)
                spill_tok[(pi, t)] = B.dma("y", aq, sdst.ap()[t], g32[t % 2])
        wslot_last_pe[pi % 2] = pe.n
    p2_aq = aq.snapshot()
    p2_vmark = B.mark("v")
    p2_smark = B.mark("s")

    if stop_after == "p2":
        return _finish_trunc(B, nc, [aq, wq])
    wo_slot = wbuf[:, 0:8192]
    wo_tok = B.dma("g", wq, wo_slot.rearrange("p (c n) -> p c n", n=D),
                   Wo.rearrange("(c p) n -> p c n", p=128),
                   waits=[(pe, pe.n)])
    wob_tok = B.dma("g", wq, BROW, bo, waits=[(pe, pe.n)])

    off = [0]

    def aw1(ncols):
        ap = work1[:, off[0]:off[0] + ncols]
        off[0] += ncols
        return ap

    qT_b = [aw1(512).bitcast(BF16) for _ in range(2)]
    k_b = [aw1(512).bitcast(BF16) for _ in range(2)]
    v_b = [aw1(520).bitcast(BF16) for _ in range(2)]
    g_b = [aw1(1024) for _ in range(2)]
    szsb = aw1(260).bitcast(BF16)
    s_acc = aw1(520)
    rden = aw1(16)
    ptmp = aw1(256)
    ao_b = [aw1(512).bitcast(BF16) for _ in range(2)]
    assert off[0] <= 8192, off[0]

    def S_ap(h):
        return PF[2 + h // 8][(h % 2) * 64:(h % 2) * 64 + 64,
                              ((h // 2) % 4) * 65:((h // 2) % 4) * 65 + 65]

    def SZ_ap(h):
        p0 = (h % 2) * 64
        c0 = (h // 8) * 260 + ((h // 2) % 4) * 65
        return szsb[p0:p0 + 64, c0:c0 + 65]

    NUMB = [PF[0], PF[1], PF[4], PF[5]]

    def num_ap(h, c):
        return NUMB[h // 4][(c % 2) * 64:(c % 2) * 64 + 64,
                            (h % 4) * 65:(h % 4) * 65 + 65]

    rl_tok = {}
    scan_done = {}

    def issue_reload(t):
        w = []
        if t >= 2 and (t - 2) in scan_done:
            w = [(pe, scan_done[t - 2][0]), scan_done[t - 2][1]]
            if (t - 2) in ao_spill:
                w.append((aq, ao_spill[t - 2]))
        pr = t % 2
        if t < 2:
            w = list(w) + p2_aq
        rl_tok[t] = [
            B.dma("y", aq, qT_b[pr], s_qT.ap()[t], waits=w),
            B.dma("y", aq, k_b[pr], s_k.ap()[t], waits=w),
            B.dma("y", aq, v_b[pr], s_v.ap()[t], waits=w),
            B.dma("y", aq, g_b[pr], s_g.ap()[t], waits=w),
        ]

    ao_spill = {}
    B.op("v", lambda v: v.memset(s_acc[:], 0.0), waits=p2_aq)
    issue_reload(0)
    issue_reload(1)
    B.wait("t", p2_vmark[0], p2_vmark[1])
    B.wait("t", p2_smark[0], p2_smark[1])
    sz_mark = None
    epi_mark = None
    prev_numk = None
    for c in range(NCH):
        t = c // 2
        pr = t % 2
        cp = (c % 2) * 64
        if c % 2 == 0:
            B.wait("t", aq, rl_tok[t][1])
            B.wait("t", aq, rl_tok[t][2])
        for h in range(H):
            w = []
            if h == 0 and sz_mark is not None:
                w.append(sz_mark)
            if h == 0 and prev_numk is not None:
                w.append(prev_numk)
            mm(S_ap(h),
               k_b[pr][cp:cp + 64, h * 64:(h + 1) * 64],
               v_b[pr][cp:cp + 64, h * 65:h * 65 + 65],
               start=True, stop=True, waits=w,
               incs=([(pe, 1)] if h == H - 1 else []))
        supd = pe_inc()
        B.wait("v", *supd)
        B.op("v", lambda v: v.tensor_add(s_acc[:, 0:260], s_acc[:, 0:260], PF[2][:, 0:260]))
        B.op("v", lambda v: v.tensor_add(s_acc[:, 260:520], s_acc[:, 260:520], PF[3][:, 0:260]))
        sz_mark = B.mark("v")
        B.op("v", lambda v: v.tensor_copy(szsb[:, 0:260], s_acc[:, 0:260]))
        B.op("v", lambda v: v.tensor_copy(szsb[:, 260:520], s_acc[:, 260:520]))
        sz_mark2 = B.mark("v")
        B.wait("t", sz_mark2[0], sz_mark2[1])
        if epi_mark is not None:
            B.wait("t", epi_mark[0], epi_mark[1])
        if c % 2 == 0:
            B.wait("t", aq, rl_tok[t][0])
        if p3_mode < 2:
            epi_mark = sz_mark2
            if c % 2 == 1:
                scan_done[t] = (pe.n, sz_mark2)
                ao_spill[t] = rl_tok[t][3]
                if t + 2 < NT:
                    issue_reload(t + 2)
            continue
        order = list(range(0, H, 2)) + list(range(1, H, 2))
        evens_mark = None
        for i, h in enumerate(order):
            w = []
            if i == 8:
                w = [evens_mark]
            mm(num_ap(h, c),
               qT_b[pr][(h % 2) * 64:(h % 2) * 64 + 64,
                        (h // 2) * 128 + cp:(h // 2) * 128 + cp + 64],
               SZ_ap(h), start=True, stop=True, waits=w,
               incs=([(pe, 1)] if i in (7, 15) else []))
            if i == 7:
                evens_mark = pe_inc()
        numk = pe_inc()
        prev_numk = numk
        B.wait("v", *numk)
        if p3_mode < 3:
            epi_mark = (pe, numk[1]) if False else sz_mark2
            B.wait("v", *numk)
            B.op("v", lambda v: v.tensor_copy(ptmp[0:64, 0:256], NUMB[0][0:64, 0:256]))
            epi_mark = B.mark("v")
            if c % 2 == 1:
                scan_done[t] = (pe.n, epi_mark)
                ao_spill[t] = rl_tok[t][3]
                if t + 2 < NT:
                    issue_reload(t + 2)
            continue
        if c % 2 == 0:
            B.wait("v", aq, rl_tok[t][3])
        if c % 2 == 0 and t >= 2 and (t - 2) in ao_spill:
            B.wait("v", aq, ao_spill[t - 2])
        for b4 in range(4):
            bank = NUMB[b4]
            dv = bank[cp:cp + 64, 0:260].rearrange("p (h x) -> p h x", x=65)[:, :, 64:65]
            ov = rden[cp:cp + 64, b4 * 4:(b4 + 1) * 4].unsqueeze(2)
            B.op("v", lambda v, _d=dv, _o=ov: v.tensor_scalar(_o, _d, EPS_DEN, None, ALU.add))
        B.op("v", lambda v, _r=rden[cp:cp + 64, 0:16]: v.reciprocal(_r, _r))
        for b4 in range(4):
            bank = NUMB[b4]
            nv = bank[cp:cp + 64, 0:260].rearrange("p (h x) -> p h x", x=65)[:, :, 0:64]
            rv = rden[cp:cp + 64, b4 * 4:(b4 + 1) * 4].unsqueeze(2).broadcast_to([64, 4, 64])
            pt_ = ptmp[cp:cp + 64, 0:256].rearrange("p (h x) -> p h x", x=64)
            B.op("v", lambda v, _n=nv, _r=rv, _o=pt_: v.tensor_mul(_o, _n, _r))
            gv = g_b[pr][cp:cp + 64, b4 * 256:(b4 + 1) * 256]
            av = ao_b[pr][cp:cp + 64, b4 * 256:(b4 + 1) * 256]
            B.op("v", lambda v, _g=gv, _o=av, _p=ptmp[cp:cp + 64, 0:256]:
                 v.tensor_mul(_o, _p, _g))
        epi_mark = B.mark("v")
        if c % 2 == 1:
            B.wait("y", epi_mark[0], epi_mark[1])
            ao_spill[t] = B.dma("y", aq, s_ao.ap()[t], ao_b[pr])
            scan_done[t] = (pe.n, epi_mark)
            if t + 2 < NT:
                issue_reload(t + 2)
    p3_aq = aq.snapshot()
    p3_vmark = B.mark("v")

    if stop_after == "p3":
        return _finish_trunc(B, nc, [aq, wq])
    aol = [work1[:, i * 512:(i + 1) * 512].bitcast(BF16) for i in range(2)]
    aoT = work1[:, 1024:1536].bitcast(BF16)
    h1t = work1[:, 1536:2560]
    outst = [work1[:, 2560 + i * 1024:2560 + (i + 1) * 1024] for i in range(2)]
    xst4 = [work1[:, 4608 + i * 1024:4608 + (i + 1) * 1024] for i in range(2)]
    sq4 = work1[:, 6656:7680]
    xmst = [work2[:, i * 512:(i + 1) * 512].bitcast(BF16) for i in range(2)]
    B.wait("t", wq, wo_tok)
    B.wait("t", wq, wob_tok)
    B.wait("t", p3_vmark[0], p3_vmark[1])
    ao_tok = [None] * NT
    x4_tok = [None] * NT
    tpose_done_p4 = {}
    out_tok4 = {}
    h1dbg_tok = {}
    xm_sp = {}
    for t in range(2):
        w0 = [(pe, pe.n), p3_vmark] + p3_aq
        ao_tok[t] = B.dma("y", aq, aol[t % 2], s_ao.ap()[t], waits=w0)
        x4_tok[t] = B.dma("y", aq, xst4[t % 2], X_T[t], waits=w0)
    for t in range(NT):
        B.wait("t", aq, ao_tok[t])
        vm_prev = B.mark("v")
        tpose.run(aol[t % 2], lambda dc: aoT[:, dc * 128:(dc + 1) * 128], vm_prev)
        aoT_mark = B.mark("v")
        for nh in range(2):
            bank = PF[nh]
            w = [aoT_mark] if nh == 0 else []
            if nh == 0 and debug and (t - 1) in h1dbg_tok:
                wh1 = [(aq, h1dbg_tok[t - 1])]
            else:
                wh1 = []
            for dc in range(DC):
                ww = list(w) if dc == 0 else []
                mm(bank[:], aoT[:, dc * 128:(dc + 1) * 128],
                   wo_slot[:, dc * 1024 + nh * 512: dc * 1024 + nh * 512 + 512],
                   start=(dc == 0), stop=False, waits=ww)
            mm(bank[:], onesr_bf[:], BROW[:, nh * 512:nh * 512 + 512], False, True,
               incs=[(pe, 1)])
            mk = pe_inc()
            B.wait("v", *mk)
            B.op("v", lambda v, _o=h1t[:, nh * 512:nh * 512 + 512], _p=bank[:],
                 _x=xn[:, t * 1024 + nh * 512: t * 1024 + nh * 512 + 512]:
                 v.tensor_add(_o, _p, _x), waits=wh1)
        B.wait("v", aq, x4_tok[t])
        wout = [(stq, out_tok4[t - 2])] if t >= 2 else []
        B.op("v", lambda v, _o=outst[t % 2], _a=h1t[:], _b=xst4[t % 2]:
             v.tensor_add(_o, _a, _b), waits=wout)
        vm = B.mark("v")
        B.wait("y", *vm)
        out_tok4[t] = B.dma("y", stq, OUT_T[t], outst[t % 2])
        if debug:
            h1dbg_tok[t] = B.dma("y", aq, d_h1.ap().rearrange("(t p) d -> t p d", p=128)[t], h1t[:])
        if t + 2 < NT:
            wpre = [(pe, pe.n), vm]
            ao_tok[t + 2] = B.dma("y", aq, aol[t % 2], s_ao.ap()[t + 2], waits=wpre)
            x4_tok[t + 2] = B.dma("y", aq, xst4[t % 2], X_T[t + 2], waits=wpre)
        c_ss = stats[:, 6:7]
        c_r2 = stats[:, 7:8]
        B.wait("s", *vm)
        B.op("s", lambda s, _a=c_ss: s.activation(out=sq4[:], in_=h1t[:], func=AF.Square, accum_out=_a))
        B.op("s", lambda s, _o=c_ss, _i=c_ss: s.activation(out=_o, in_=_i, func=AF.Sqrt, scale=1.0 / D))
        sm = B.mark("s")
        B.wait("v", *sm)
        B.op("v", lambda v, _o=c_r2, _i=c_ss: v.reciprocal(_o, _i))
        wxm = [(pe, tpose_done_p4[t - 2])] if t >= 2 else []
        if (t - 2) in xm_sp:
            wxm.append((aq, xm_sp[t - 2]))
        B.op("v", lambda v, _o=xmst[t % 2], _r=c_r2:
             v.tensor_scalar(_o, h1t[:], _r, None, ALU.mult), waits=wxm)
        vm2 = B.mark("v")
        xm_sp[t] = B.dma("y", aq, s_xm.ap().rearrange("(t p) d -> t p d", p=128)[t],
                         xmst[t % 2], waits=[vm2])
        tpose.run(xmst[t % 2], lambda dc, _t=t: cT[:, dc * T + _t * 128: dc * T + _t * 128 + 128], vm2,
                  scale_col=lambda dc: S2C[:, dc:dc + 1])
        tpose_done_p4[t] = pe.n
    p4_vmark = B.mark("v")
    p4_out_tok = stq.snapshot()

    if stop_after == "p4":
        return _finish_trunc(B, nc, [aq, wq, stq])
    B.wait_snap("t", wq_const_tok)
    B.wait("t", p4_vmark[0], p4_vmark[1])
    for t in range(NT):
        for dc in range(DC):
            mm(PF[0][:, t * 8:(t + 1) * 8], cT[:, dc * T + t * 128: dc * T + t * 128 + 128],
               rWsb[:, dc * 8:(dc + 1) * 8], start=(dc == 0), stop=False)
        mm(PF[0][:, t * 8:(t + 1) * 8], onesr_bf[:], RRB, False, True,
           incs=([(pe, 1)] if t == NT - 1 else []))
    rmk = pe_inc()
    B.wait("v", *rmk)
    B.op("v", lambda v: v.tensor_copy(logitsA[:], PF[0][:, 0:128]))
    B.op("v", lambda v: v.memset(run_row, 0.0))
    mx = stats[:, 8:9]
    sume = stats[:, 9:10]
    i1c = stats[:, 10:11]
    i2c = stats[:, 11:12]
    rsum = stats[:, 12:13]
    cnt_pe_hist = {}
    for t in range(NT):
        blk = logitsA[:, t * 8:(t + 1) * 8]
        B.op("v", lambda v, _b=blk: v.reduce_max(mx, _b, axis=AX.X))
        B.op("v", lambda v: v.tensor_scalar(mx, mx, -1.0, None, ALU.mult))
        vm = B.mark("v")
        B.wait("s", *vm)
        B.op("s", lambda s, _b=blk: s.activation(out=_b, in_=_b, func=AF.Exp, bias=mx, accum_out=sume))
        sm = B.mark("s")
        B.wait("v", *sm)
        B.op("v", lambda v: v.reciprocal(sume, sume))
        B.op("v", lambda v, _b=blk: v.tensor_scalar(_b, _b, sume, None, ALU.mult))
        m8 = scr8c
        B.op("v", lambda v, _o=m8, _b=blk: v.max(out=_o, in_=_b))
        B.op("v", lambda v, _b=blk, _m=m8: v.tensor_single_scalar(scr8a, _b, _m[:, 0:1], ALU.is_equal))
        B.op("v", lambda v: v.tensor_scalar(scr8a, scr8a, -1000.0, 1000.0, ALU.mult, ALU.add))
        B.op("v", lambda v: v.tensor_add(scr8a, scr8a, IOTA))
        B.op("v", lambda v: v.tensor_reduce(i1c, scr8a, AX.X, ALU.min))
        B.op("v", lambda v, _b=blk, _m=m8: v.tensor_single_scalar(scr8a, _b, _m[:, 1:2], ALU.is_equal))
        B.op("v", lambda v: v.tensor_scalar(scr8a, scr8a, -1000.0, 1000.0, ALU.mult, ALU.add))
        B.op("v", lambda v: v.tensor_add(scr8a, scr8a, IOTA))
        B.op("v", lambda v: v.tensor_single_scalar(scr8b, IOTA, i1c, ALU.is_equal))
        B.op("v", lambda v: v.tensor_scalar(scr8b, scr8b, 2000.0, None, ALU.mult))
        B.op("v", lambda v: v.tensor_add(scr8a, scr8a, scr8b))
        B.op("v", lambda v: v.tensor_reduce(i2c, scr8a, AX.X, ALU.min))
        B.op("v", lambda v, _m=m8: v.tensor_add(rsum, _m[:, 0:1], _m[:, 1:2]))
        B.op("v", lambda v: v.reciprocal(rsum, rsum))
        B.op("v", lambda v, _m=m8, _o=gcols[:, t * 4:t * 4 + 1]:
             v.tensor_scalar(_o, _m[:, 0:1], rsum, None, ALU.mult))
        B.op("v", lambda v, _m=m8, _o=gcols[:, t * 4 + 1:t * 4 + 2]:
             v.tensor_scalar(_o, _m[:, 1:2], rsum, None, ALU.mult))
        B.op("v", lambda v, _o=gcols[:, t * 4 + 2:t * 4 + 3]: v.tensor_copy(_o, i1c))
        B.op("v", lambda v, _o=gcols[:, t * 4 + 3:t * 4 + 4]: v.tensor_copy(_o, i2c))
        B.op("v", lambda v, _b=blk: v.tensor_single_scalar(_b, IOTA, i1c, ALU.is_equal))
        B.op("v", lambda v: v.tensor_single_scalar(scr8b, IOTA, i2c, ALU.is_equal))
        oh_slot = OHBF if t % 2 == 0 else OHBF2
        w_oh = [(pe, cnt_pe_hist[t - 2])] if (t - 2) in cnt_pe_hist else []
        B.op("v", lambda v, _b=blk: v.tensor_add(scr8a, _b, scr8b), waits=w_oh)
        B.op("v", lambda v, _o=oh_slot: v.tensor_copy(_o, scr8a))
        vm2 = B.mark("v")
        B.wait("t", vm2[0], vm2[1])
        mm(PF[1][:, 0:8], LSTRICT, oh_slot, True, True)
        mm(PF[1][:, 16:24], onesr_f[:], run_row, True, True)
        mm(PF[1][0:1, 8:16], ONESCOL, oh_slot, True, True, incs=[(pe, 1)])
        ck = pe_inc()
        cnt_pe_hist[t] = ck[1]
        B.wait("v", *ck)
        B.op("v", lambda v: v.tensor_copy(scr8b, PF[1][:, 16:24]))
        B.op("v", lambda v, _o=cntC[:, t * 8:(t + 1) * 8]: v.tensor_add(_o, PF[1][:, 0:8], scr8b))
        B.op("v", lambda v: v.tensor_add(run_row, run_row, PF[1][0:1, 8:16]))
    vm = B.mark("v")
    B.wait("g", *vm)
    cc_wr = B.dma("g", oq, cc_in.ap(), run_row)
    B.wait("g", oq, cc_wr)

    def do_cc(g):
        g.collective_compute(
            "AllGather", ALU.bypass, replica_groups=[list(range(NCORES))],
            ins=[cc_in.ap()], outs=[cc_out.ap()]).then_inc(cc.h, 1)
    B.ops["g"].append(do_cc)
    cc.n += 1
    B.wait("g", cc, 1)
    cct = B.dma("g", oq, ccsb[:], cc_out.ap())
    B.wait("t", oq, cct)
    B.wait("t", vm[0], vm[1])
    mm(PF[1][0:1, 24:32], mcol[:], ccsb[:], True, True, incs=[(pe, 1)])
    ok = pe_inc()
    B.wait("v", *ok)
    B.op("v", lambda v: v.tensor_scalar(off_row, PF[1][0:1, 24:32], -1.0, float(CAP),
                                        ALU.mult, ALU.add))
    vm = B.mark("v")
    B.wait("t", vm[0], vm[1])
    mm(PF[1][:, 32:40], onesr_f[:], off_row, True, True, incs=[(pe, 1)])
    tk = pe_inc()
    B.wait("v", *tk)
    B.op("v", lambda v: v.tensor_copy(thr_b, PF[1][:, 32:40]))
    B.op("v", lambda v: v.tensor_scalar(krow, off_row, 0.0, None, ALU.max))
    B.op("v", lambda v: v.tensor_tensor(krow, krow, run_row, ALU.min))
    B.op("v", lambda v: v.memset(prow, 0.0))
    for m in range(15):
        B.op("v", lambda v, _m=m: v.tensor_single_scalar(et_row_tmp[:, 0:8], krow,
                                                         float(512 * _m), ALU.is_gt))
        B.op("v", lambda v: v.tensor_add(prow, prow, et_row_tmp[:, 0:8]))
    B.op("v", lambda v: v.tensor_scalar(prow, prow, 512.0, None, ALU.mult))
    B.op("v", lambda v: v.memset(lbase[:, 0:1], 0.0))
    for e in range(1, NE):
        B.op("v", lambda v, _e=e: v.tensor_add(lbase[:, _e:_e + 1], lbase[:, _e - 1:_e],
                                               prow[:, _e - 1:_e]))
    B.op("v", lambda v: v.tensor_add(endr, lbase, prow))
    B.op("v", lambda v: v.memset(et_row[:, 0:16], 0.0))
    for e in range(NE):
        B.op("v", lambda v, _e=e: v.tensor_single_scalar(
            et_row_tmp[:, 0:NSTA], ST512R[:, 0:NSTA], endr[:, _e:_e + 1], ALU.is_ge))
        B.op("v", lambda v: v.tensor_add(et_row[:, 0:NSTA], et_row[:, 0:NSTA],
                                         et_row_tmp[:, 0:NSTA]))
    B.op("v", lambda v: v.tensor_scalar(et_row[:, 0:NSTA], et_row[:, 0:NSTA], 7.0, None, ALU.min))
    vm = B.mark("v")
    B.wait("t", vm[0], vm[1])
    mm(PF[1][:, 40:40 + NSTA], onesr_f[:], et_row[:, 0:NSTA], True, True)
    mm(PF[1][:, 56:64], onesr_f[:], lbase, True, True, incs=[(pe, 1)])
    bk = pe_inc()
    B.wait("v", *bk)
    B.op("v", lambda v: v.tensor_copy(etb[:, 0:NSTA], PF[1][:, 40:40 + NSTA]))
    B.op("v", lambda v: v.tensor_copy(lbb[:], PF[1][:, 56:64]))
    sc_toks_a = []
    for t in range(NT):
        cntt = cntC[:, t * 8:(t + 1) * 8]
        g1 = gcols[:, t * 4:t * 4 + 1]
        g2 = gcols[:, t * 4 + 1:t * 4 + 2]
        i1s = gcols[:, t * 4 + 2:t * 4 + 3]
        i2s = gcols[:, t * 4 + 3:t * 4 + 4]
        B.op("v", lambda v, _i=i1s: v.tensor_single_scalar(scr8a, IOTA, _i, ALU.is_equal))
        B.op("v", lambda v, _i=i2s: v.tensor_single_scalar(scr8b, IOTA, _i, ALU.is_equal))
        B.op("v", lambda v, _c=cntt: v.tensor_tensor(scr8c, _c, thr_b, ALU.is_lt))
        B.op("v", lambda v: v.tensor_mul(scr8c, scr8c, scr8a))
        B.op("v", lambda v, _o=stats[:, 14:15]: v.reduce_sum(_o, scr8c, axis=AX.X))
        B.op("v", lambda v, _c=cntt: v.tensor_add(scr8c, _c, scr8a))
        B.op("v", lambda v: v.tensor_tensor(scr8c, scr8c, thr_b, ALU.is_lt))
        B.op("v", lambda v: v.tensor_mul(scr8c, scr8c, scr8b))
        B.op("v", lambda v, _o=stats[:, 15:16]: v.reduce_sum(_o, scr8c, axis=AX.X))
        B.op("v", lambda v, _c=cntt: v.tensor_add(scr8c, lbb[:], _c))
        B.op("v", lambda v: v.tensor_mul(scr8a, scr8a, scr8c))
        B.op("v", lambda v, _o=stats[:, 16:17]: v.reduce_sum(_o, scr8a, axis=AX.X))
        B.op("v", lambda v: v.tensor_mul(scr8b, scr8b, scr8c))
        B.op("v", lambda v, _o=stats[:, 17:18]: v.reduce_sum(_o, scr8b, axis=AX.X))
        B.op("v", lambda v, _g=g1, _o=gcA[:, 2 * t:2 * t + 1]:
             v.tensor_mul(_o, _g, stats[:, 14:15]))
        B.op("v", lambda v, _g=g2, _o=gcA[:, 2 * t + 1:2 * t + 2]:
             v.tensor_mul(_o, _g, stats[:, 15:16]))
        B.op("v", lambda v: v.tensor_mul(stats[:, 18:19], stats[:, 16:17], stats[:, 14:15]))
        B.op("v", lambda v: v.tensor_mul(stats[:, 19:20], stats[:, 17:18], stats[:, 15:16]))
        B.op("v", lambda v, _o=posA[:, 2 * t:2 * t + 1]: v.tensor_copy(_o, stats[:, 18:19]))
        B.op("v", lambda v, _o=posA[:, 2 * t + 1:2 * t + 2]: v.tensor_copy(_o, stats[:, 19:20]))
        B.op("v", lambda v: v.tensor_scalar(stats[:, 20:21], stats[:, 14:15], -1.0, 1.0, ALU.mult, ALU.add))
        B.op("v", lambda v: v.scalar_tensor_tensor(stats[:, 20:21], stats[:, 20:21], 1e7,
                                                   stats[:, 16:17], ALU.mult, ALU.add))
        B.op("v", lambda v: v.tensor_scalar(stats[:, 21:22], stats[:, 15:16], -1.0, 1.0, ALU.mult, ALU.add))
        B.op("v", lambda v: v.scalar_tensor_tensor(stats[:, 21:22], stats[:, 21:22], 1e7,
                                                   stats[:, 17:18], ALU.mult, ALU.add))
        c0 = 4 * (t % 2)
        wsc = ([(iq, sc_toks_a[2 * (t - 2)]), (iq, sc_toks_a[2 * (t - 2) + 1])]
               if t >= 2 else [])
        B.op("v", lambda v, _o=sct[:, c0:c0 + 1]: v.tensor_copy(_o, stats[:, 20:21]),
             waits=wsc)
        B.op("v", lambda v, _t=t:
             v.tensor_scalar(stats[:, 22:23], iotac[:], float(128 * _t), None, ALU.add))
        B.op("v", lambda v, _o=sct[:, c0 + 1:c0 + 2]: v.tensor_copy(_o, stats[:, 22:23]))
        B.op("v", lambda v, _o=sct[:, c0 + 2:c0 + 3]: v.tensor_copy(_o, stats[:, 21:22]))
        vmk = B.mark("v")
        sc_toks_a.append(B.dma("g", iq, s_atok.ap(), sct[:, c0 + 1:c0 + 2],
                               indirect_out=sct[:, c0:c0 + 1], bounds=rbA,
                               waits=[vmk]))
        sc_toks_a.append(B.dma("g", iq, s_atok.ap(), sct[:, c0 + 1:c0 + 2],
                               indirect_out=sct[:, c0 + 2:c0 + 3], bounds=rbA))
    if debug:
        vmD = B.mark("v")
        B.wait("y", *vmD)
        B.dma("y", aq, d_cnt.ap().rearrange("(t p) e -> p t e", p=128),
              cntC[:].rearrange("p (t e) -> p t e", e=NE))
        B.dma("y", aq, d_rows.ap()[:, 0:64], smalls[0:1, 288:352])
    p5_vmark = B.mark("v")

    if stop_after == "p5":
        return _finish_trunc(B, nc, [aq, wq, stq, oq])
    W1sb = big1[:].bitcast(BF16)
    W2sb = big2[:].bitcast(BF16)
    hTs = work1[:].bitcast(BF16)
    xg = work2[:].bitcast(BF16)
    xgT = cT[:, 0:4096]
    yst2 = [cT[:, 4096 + i * 2048:4096 + (i + 1) * 2048].bitcast(F32) for i in range(2)]
    Ycmb = [cT[:, 8192 + i * 2048:8192 + (i + 1) * 2048].bitcast(F32) for i in range(2)]

    def sparse_moe(NST, sw1, sw2, b1src, b2src, s_x, s_yv, stok, act_fn,
                   sc_toks, first_waits, scale_cols=None):
        st_store = {}
        xfer = {}
        yev = {0: None, 1: None}
        ph2_vmark = {"m": None}

        def part_a(st):
            w_idx = [(iq, tk) for tk in xfer[st - 1]["wall"]] if st >= 1 else []
            B.op("v", lambda v, _st=st: v.scalar_tensor_tensor(
                stats[:, 24:25], etb[:, _st:_st + 1], 128.0, iotac[:], ALU.mult, ALU.add),
                waits=w_idx)
            B.op("v", lambda v: v.tensor_copy(idxw[:, 0:1], stats[:, 24:25]))
            B.op("v", lambda v, _st=st: v.tensor_copy(idxw[:, 1:2], etb[:, _st:_st + 1]))
            idx_vm = B.mark("v")
            sl = 4 * (st % 2)
            if st == 0:
                wtk = list(sc_toks)
            elif st >= 2:
                wtk = [(iq, tk) for tk in xfer[st - 2]["xg"]]
            else:
                wtk = []
            t_ti = B.dma("g", iq, tik[:, sl:sl + 4],
                         stok.ap()[st * 512:(st + 1) * 512].rearrange(
                             "(p j) one -> p (j one)", p=128),
                         waits=wtk)
            xg_toks = []
            for jj in range(4):
                wxg = [(iq, t_ti)]
                if st >= 1:
                    wxg.append((pe, xfer[st - 1]["tp"]))
                else:
                    wxg.extend(first_waits.get("xg", []))
                xg_toks.append(B.dma("g", iq, xg[:, jj * 1024:(jj + 1) * 1024], s_x.ap(),
                                     indirect_in=tik[:, sl + jj:sl + jj + 1], waits=wxg))
            t_w1q = []
            for q in range(4):
                if st == 0:
                    wwq = [idx_vm] + first_waits.get("w1", [])
                else:
                    wwq = [idx_vm, (pe, pe.n + 4 * (q + 1))]
                t_w1q.append(B.dma("g", iq, W1sb[:, q * 8192:(q + 1) * 8192],
                                   sw1, indirect_in=idxw[:, 0:1], waits=wwq,
                                   elem_off=q * 8192))
            xfer[st] = dict(xg=xg_toks, w1=t_w1q, idx_vm=idx_vm, wall=list(t_w1q))

        def part_b(st):
            idx_vm = xfer[st]["idx_vm"]
            gate = first_waits.get("w2", []) if st == 0 else [(pe, pe.n)]
            wb = [idx_vm] + gate
            t_b1 = B.dma("g", iq, b1g[:], b1src, indirect_in=idxw[:, 0:1], waits=wb)
            t_b2 = B.dma("g", iq, smalls[0:2, 640:1152].bitcast(BF16), b2src,
                         indirect_in=idxw[0:2, 1:2], waits=wb)
            t_w2 = [B.dma("g", iq, W2sb[:, part * 8192:(part + 1) * 8192],
                          sw2, indirect_in=idxw[:, 0:1], waits=list(wb),
                          elem_off=part * 8192) for part in range(4)]
            xfer[st]["b1"], xfer[st]["b2"], xfer[st]["w2"] = t_b1, t_b2, t_w2
            xfer[st]["wall"] += t_w2 + [t_b1, t_b2]

        def transposes(st):
            for jj in range(4):
                tpose.run(xg[:, jj * 1024:(jj + 1) * 1024],
                          lambda dc, _j=jj: xgT[:, dc * 512 + _j * 128:
                                               dc * 512 + _j * 128 + 128],
                          (iq, xfer[st]["xg"][jj]),
                          scale_col=(None if scale_cols is None else
                                     (lambda dc: scale_cols[:, dc:dc + 1])))
            xfer[st]["tp"] = pe.n
            xfer[st]["tp_vm"] = B.mark("v")

        part_a(0)
        part_b(0)
        transposes(0)
        for st in range(NST):
            if st + 1 < NST:
                part_a(st + 1)
            t_w1q = xfer[st]["w1"]
            tp_vm = xfer[st]["tp_vm"]
            smk0 = B.mark("s")
            gel_mark = {0: smk0, 1: smk0}
            B.wait("t", iq, t_w1q[0])
            B.wait("t", tp_vm[0], tp_vm[1])
            for hcg in range(16):
                if hcg % 4 == 0 and hcg > 0:
                    B.wait("t", iq, t_w1q[hcg // 4])
                pr2 = hcg % 2
                banks = (PF[0], PF[1]) if pr2 == 0 else (PF[2], PF[3])
                for dc in range(DC):
                    for j in range(2):
                        hc = hcg * 2 + j
                        h, lh = hc // 8, hc % 8
                        w = []
                        if dc == 0 and j == 0:
                            w.append(gel_mark[pr2])
                            if hcg == 1 and ph2_vmark["m"] is not None:
                                w.append(ph2_vmark["m"])
                        mm(banks[j][:],
                           W1sb[:, h * 8192 + dc * 1024 + lh * 128:
                                h * 8192 + dc * 1024 + (lh + 1) * 128],
                           xgT[:, dc * 512: dc * 512 + 512],
                           start=(dc == 0), stop=(dc == DC - 1), waits=w,
                           incs=([(pe, 1)] if (dc == DC - 1 and j == 1) else []))
                mk = pe_inc()
                B.wait("s", *mk)
                if hcg == 0:
                    B.wait("s", iq, xfer[st]["b1"])
                for j in range(2):
                    hc = hcg * 2 + j
                    B.op("s", lambda s, _o=hTs[:, hc * 512:(hc + 1) * 512],
                         _p=banks[j][:], _b=b1g[:, hc:hc + 1], _f=act_fn:
                         s.activation(out=_o, in_=_p, func=_f, bias=_b))
                gel_mark[pr2] = B.mark("s")
            if st + 1 < NST:
                transposes(st + 1)
            smk = B.mark("s")
            B.wait("t", smk[0], smk[1])
            for _tk in xfer[st]["w2"]:
                B.wait("t", iq, _tk)
            B.wait("t", iq, xfer[st]["b2"])
            for tt in range(4):
                ybanks = [PF[4][:], PF[5][:]] if tt % 2 == 0 else [PF[2][:], PF[3][:]]
                for nh in range(2):
                    bank = ybanks[nh]
                    for hc in range(HC):
                        w = []
                        if hc == 0 and nh == 0 and yev[tt % 2] is not None:
                            w.append(yev[tt % 2])
                        mm(bank,
                           hTs[:, hc * 512 + tt * 128: hc * 512 + tt * 128 + 128],
                           W2sb[:, hc * 1024 + nh * 512: hc * 1024 + nh * 512 + 512],
                           start=(hc == 0), stop=False, waits=w)
                    mm(bank, onesr_bf[:], BROW[:, nh * 512:nh * 512 + 512],
                       False, True, incs=([(pe, 1)] if nh == 1 else []))
                yk = pe_inc()
                B.wait("v", *yk)
                ww = []
                prevk = st_store.get((st, tt - 2)) or st_store.get((st - 1, tt + 2))
                if prevk is not None:
                    ww.append((aq, prevk))
                B.op("v", lambda v, _o=yst2[tt % 2][:, 0:512], _p=ybanks[0]:
                     v.tensor_copy(_o, _p), waits=ww)
                B.op("v", lambda v, _o=yst2[tt % 2][:, 512:1024], _p=ybanks[1]:
                     v.tensor_copy(_o, _p))
                yev[tt % 2] = B.mark("v")
                st_store[(st, tt)] = B.dma(
                    "y", aq,
                    s_yv.ap()[st * 512:(st + 1) * 512].rearrange("(p j) d -> j p d", j=4)[tt],
                    yst2[tt % 2][:, 0:1024], waits=[yev[tt % 2]])
            ph2_vmark["m"] = B.mark("v")
            if st + 1 < NST:
                part_b(st + 1)
        return aq.snapshot()

    if debug:
        dbg_toks.append((aq, h1dbg_tok[NT - 1]))
        for tok in dbg_toks:
            B.wait("s", aq, tok[1])
            B.wait("v", aq, tok[1])
    B.wait_snap("g", p4_out_tok)
    p4_aq_snap = aq.snapshot()
    B.wait_snap("g", p4_aq_snap)
    b1sa = ab1p.rearrange("e p c -> (e p) c")
    sw1a = aW1p.rearrange("e p c -> (e p) c")
    sw2a = aW2p.rearrange("e p c -> (e p) c")
    ay_snap = sparse_moe(NSTA, sw1a, sw2a, b1sa, ab2p, s_xm, s_ay, s_atok,
                         AF.Gelu,
                         iq.snapshot(),
                         {"w1": [p4_vmark] + [aq.tok(tk[1]) for tk in dbg_toks],
                          "w2": [(pe, pe.n)],
                          "xg": [(pe, pe.n)]}, scale_cols=S2C)
    B.wait_snap("g", ay_snap)
    cmb_tok = {}
    for t in range(NT):
        wY = [(pe, pe.n)] if t == 0 else []
        ty1 = B.dma("g", iq, Ycmb[0][:, 0:1024], s_ay.ap(),
                    indirect_in=posA[:, 2 * t:2 * t + 1],
                    waits=wY + ([(oq, cmb_tok[t - 1])] if t >= 1 else []))
        ty2 = B.dma("g", iq, Ycmb[1][:, 0:1024], s_ay.ap(),
                    indirect_in=posA[:, 2 * t + 1:2 * t + 2], waits=[])
        B.op("v", lambda v, _t=t, _g=gcA[:, 2 * t:2 * t + 1]:
             v.tensor_scalar(Ycmb[0][:, 0:1024], Ycmb[0][:, 0:1024], _g, None, ALU.mult),
             waits=[(iq, ty1), (iq, ty2)])
        B.op("v", lambda v, _g=gcA[:, 2 * t + 1:2 * t + 2]:
             v.scalar_tensor_tensor(Ycmb[0][:, 0:1024], Ycmb[1][:, 0:1024], _g,
                                    Ycmb[0][:, 0:1024], ALU.mult, ALU.add))
        cvm = B.mark("v")
        cmb_tok[t] = B.dma("g", oq, OUT_T[t], Ycmb[0][:, 0:1024], accum=ALU.add,
                           waits=[cvm])
    if stop_after == "p6":
        return _finish_trunc(B, nc, [aq, wq, stq, oq, iq])
    moe_pe = (pe, pe.n)
    cmb_vm = B.mark("v")
    xnT_rl = []
    cT3 = cT[:].rearrange("p (d x) -> p d x", x=T)
    for t in range(NT):
        dst = cT3[:, :, t * 128:t * 128 + 128]
        xnT_rl.append(B.dma("y", aq, dst, s_xnT.ap()[t].rearrange("p (d x) -> p d x", x=128),
                            waits=[moe_pe, cmb_vm, (oq, cmb_tok[NT - 1])]))
    xnT_done = aq.snapshot()

    B.wait_snap("t", xnT_done)
    for t in range(NT):
        for dc in range(DC):
            mm(PF[0][:, t * 8:(t + 1) * 8], cT[:, dc * T + t * 128: dc * T + t * 128 + 128],
               gWsb[:, dc * 8:(dc + 1) * 8], start=(dc == 0), stop=False)
        mm(PF[0][:, t * 8:(t + 1) * 8], onesr_bf[:], RGB, False, False)
        mm(PF[0][:, t * 8:(t + 1) * 8], onesr_bf[:], REB, False, True,
           incs=([(pe, 1)] if t == NT - 1 else []))
    gk = pe_inc()
    B.wait("v", *gk)
    B.op("v", lambda v: v.tensor_copy(logitsA[:], PF[0][:, 0:128]))
    B.op("v", lambda v: v.memset(run_row, 0.0))
    cnt_pe_hist2 = {}
    for t in range(NT):
        blk = logitsA[:, t * 8:(t + 1) * 8]
        B.op("v", lambda v, _b=blk: v.reduce_max(mx, _b, axis=AX.X))
        B.op("v", lambda v, _b=blk: v.tensor_single_scalar(scr8a, _b, mx, ALU.is_equal))
        B.op("v", lambda v: v.tensor_scalar(scr8a, scr8a, -1000.0, 1000.0, ALU.mult, ALU.add))
        B.op("v", lambda v: v.tensor_add(scr8a, scr8a, IOTA))
        B.op("v", lambda v: v.tensor_reduce(i1c, scr8a, AX.X, ALU.min))
        B.op("v", lambda v, _o=ohF[:, t * 8:(t + 1) * 8]:
             v.tensor_single_scalar(_o, IOTA, i1c, ALU.is_equal))
        oh_slot = OHBF if t % 2 == 0 else OHBF2
        w_oh = [(pe, cnt_pe_hist2[t - 2])] if (t - 2) in cnt_pe_hist2 else []
        B.op("v", lambda v, _b=ohF[:, t * 8:(t + 1) * 8], _o=oh_slot:
             v.tensor_copy(_o, _b), waits=w_oh)
        vm2 = B.mark("v")
        B.wait("t", vm2[0], vm2[1])
        mm(PF[1][:, 0:8], LSTRICT, oh_slot, True, True)
        mm(PF[1][:, 16:24], onesr_f[:], run_row, True, True)
        mm(PF[1][0:1, 8:16], ONESCOL, oh_slot, True, True, incs=[(pe, 1)])
        ck = pe_inc()
        cnt_pe_hist2[t] = ck[1]
        B.wait("v", *ck)
        B.op("v", lambda v: v.tensor_copy(scr8b, PF[1][:, 16:24]))
        B.op("v", lambda v, _o=cntC[:, t * 8:(t + 1) * 8]: v.tensor_add(_o, PF[1][:, 0:8], scr8b))
        B.op("v", lambda v: v.tensor_add(run_row, run_row, PF[1][0:1, 8:16]))
    p7_vmark = B.mark("v")
    if debug:
        B.wait("y", *p7_vmark)
        B.dma("y", aq, d_ohf.ap().rearrange("(t p) e -> p t e", p=128),
              ohF[:].rearrange("p (t e) -> p t e", e=NE))
    B.op("v", lambda v: v.tensor_copy(krow, run_row))
    B.op("v", lambda v: v.memset(prow, 0.0))
    for m in range(11):
        B.op("v", lambda v, _m=m: v.tensor_single_scalar(et_row_tmp[:, 0:8], krow,
                                                         float(512 * _m), ALU.is_gt))
        B.op("v", lambda v: v.tensor_add(prow, prow, et_row_tmp[:, 0:8]))
    B.op("v", lambda v: v.tensor_scalar(prow, prow, 512.0, None, ALU.mult))
    B.op("v", lambda v: v.memset(lbase[:, 0:1], 0.0))
    for e in range(1, NE):
        B.op("v", lambda v, _e=e: v.tensor_add(lbase[:, _e:_e + 1], lbase[:, _e - 1:_e],
                                               prow[:, _e - 1:_e]))
    B.op("v", lambda v: v.tensor_add(endr, lbase, prow))
    B.op("v", lambda v: v.memset(et_row[:, 0:16], 0.0))
    for e in range(NE):
        B.op("v", lambda v, _e=e: v.tensor_single_scalar(et_row_tmp[:, 0:NSTF],
             ST512R[:, 0:NSTF], endr[:, _e:_e + 1], ALU.is_ge))
        B.op("v", lambda v: v.tensor_add(et_row[:, 0:NSTF], et_row[:, 0:NSTF],
                                         et_row_tmp[:, 0:NSTF]))
    B.op("v", lambda v: v.tensor_scalar(et_row[:, 0:NSTF], et_row[:, 0:NSTF], 7.0, None, ALU.min))
    vm = B.mark("v")
    B.wait("t", vm[0], vm[1])
    mm(PF[1][:, 40:40 + NSTF], onesr_f[:], et_row[:, 0:NSTF], True, True)
    mm(PF[1][:, 56:64], onesr_f[:], lbase, True, True, incs=[(pe, 1)])
    bk = pe_inc()
    B.wait("v", *bk)
    B.op("v", lambda v: v.tensor_copy(etb[:, 0:NSTF], PF[1][:, 40:40 + NSTF]))
    B.op("v", lambda v: v.tensor_copy(lbb[:], PF[1][:, 56:64]))
    sc_toks_f = []
    for t in range(NT):
        cntt = cntC[:, t * 8:(t + 1) * 8]
        oht = ohF[:, t * 8:(t + 1) * 8]
        B.op("v", lambda v, _c=cntt: v.tensor_add(scr8c, lbb[:], _c))
        B.op("v", lambda v, _oh=oht: v.tensor_mul(scr8c, scr8c, _oh))
        B.op("v", lambda v, _o=stats[:, 16:17]: v.reduce_sum(_o, scr8c, axis=AX.X))
        B.op("v", lambda v, _o=posF[:, t:t + 1]: v.tensor_copy(_o, stats[:, 16:17]))
        c0 = 2 * (t % 4)
        wsc = [(iq, sc_toks_f[t - 4])] if t >= 4 else []
        B.op("v", lambda v, _o=sct[:, c0:c0 + 1]: v.tensor_copy(_o, stats[:, 16:17]),
             waits=wsc)
        B.op("v", lambda v, _t=t:
             v.tensor_scalar(stats[:, 22:23], iotac[:], float(128 * _t), None, ALU.add))
        B.op("v", lambda v, _o=sct[:, c0 + 1:c0 + 2]: v.tensor_copy(_o, stats[:, 22:23]))
        vmk = B.mark("v")
        sc_toks_f.append(B.dma("g", iq, s_ftok.ap(), sct[:, c0 + 1:c0 + 2],
                               indirect_out=sct[:, c0:c0 + 1], bounds=rbF,
                               waits=[vmk]))
    b1sf = fb1p.rearrange("e p c -> (e p) c")
    sw1f = fW1p.rearrange("e p c -> (e p) c")
    sw2f = fW2p.rearrange("e p c -> (e p) c")
    fy_snap = sparse_moe(NSTF, sw1f, sw2f, b1sf, fb2p, s_xn, s_fy, s_ftok,
                         AF.Relu,
                         iq.snapshot(),
                         {"w1": [(pe, moe_pe[1])], "w2": [(pe, moe_pe[1])],
                          "xg": [(pe, moe_pe[1])]})
    B.wait_snap("g", fy_snap)
    for t in range(NT):
        wY = [(pe, pe.n)] if t == 0 else []
        if t >= 1:
            wY.append((oq, cmb_tok[("f", t - 1)]))
        ty1 = B.dma("g", iq, Ycmb[0][:, 0:1024], s_fy.ap(),
                    indirect_in=posF[:, t:t + 1], waits=wY)
        cmb_tok[("f", t)] = B.dma("g", oq, OUT_T[t], Ycmb[0][:, 0:1024], accum=ALU.add,
                                  waits=[(iq, ty1)])

    B.wait_snap("y", stq.snapshot())
    B.wait_snap("y", oq.snapshot())
    B.wait_snap("g", oq.snapshot())

    return _emit(B, nc)


def _finish_trunc(B, nc, qs):
    for q in qs:
        B.wait_snap("y", q.snapshot())
        B.wait_snap("g", q.snapshot())
    return _emit(B, nc)


def _emit(B, nc):
    with B.stack:
        with nc.Block() as block:
            @block.tensor
            def _(t_):
                for f in B.ops["t"]:
                    f(t_)

            @block.vector
            def _(v):
                for f in B.ops["v"]:
                    f(v)

            @block.scalar
            def _(s):
                for f in B.ops["s"]:
                    f(s)

            @block.gpsimd
            def _(g):
                for f in B.ops["g"]:
                    f(g)

            @block.sync
            def _(y):
                for f in B.ops["y"]:
                    f(y)
    return B


def kernel(**inputs):
    c = make_constants()
    B = build(debug=False)
    xfull = np.ascontiguousarray(np.asarray(inputs["L0_attn_x"], np.float32))

    def row(v):
        return np.ascontiguousarray(np.asarray(v, np.float32).reshape(1, -1))

    def arr(v):
        return np.ascontiguousarray(np.asarray(v, np.float32))

    shared = dict(
        Wq=arr(inputs["L0_attn_Wq"]), bq=row(inputs["L0_attn_bq"]),
        Wk=arr(inputs["L0_attn_Wk"]), bk=row(inputs["L0_attn_bk"]),
        Wv=arr(inputs["L0_attn_Wv"]), bv=row(inputs["L0_attn_bv"]),
        Wg=arr(inputs["L0_attn_Wg"]), bg=row(inputs["L0_attn_bg"]),
        Wo=arr(inputs["L0_attn_Wo"]), bo=row(inputs["L0_attn_bo"]),
        rW=np.ascontiguousarray(np.asarray(inputs["L0_attn_router_W"], np.float32)
                                .reshape(DC, 128, NE).transpose(1, 0, 2)),
        rb=row(inputs["L0_attn_router_b"]),
        aW1p=np.ascontiguousarray(np.asarray(inputs["L0_attn_W1"], np.float32)
                                  .reshape(NE, DC, 128, 4, HID // 4)
                                  .transpose(0, 2, 3, 1, 4)
                                  .reshape(NE, 128, DC * HID)),
        aW2p=np.ascontiguousarray(np.asarray(inputs["L0_attn_W2"], np.float32)
                                  .reshape(NE, HC, 128, D).transpose(0, 2, 1, 3)
                                  .reshape(NE, 128, HC * D)),
        ab1p=np.ascontiguousarray(np.asarray(inputs["L0_attn_b1"], np.float32)
                                  .reshape(NE, HID // 128, 128).transpose(0, 2, 1)),
        ab2p=arr(inputs["L0_attn_b2"]),
        s1c=np.ascontiguousarray(np.asarray(inputs["L0_attn_norm1_scale"], np.float32).reshape(DC, 128).T),
        s2c=np.ascontiguousarray(np.asarray(inputs["L0_attn_norm2_scale"], np.float32).reshape(DC, 128).T),
        gwT=np.ascontiguousarray(np.asarray(inputs["L0_ffn_gate_w"], np.float32).T
                                 .reshape(DC, 128, NE).transpose(1, 0, 2)),
        gbr=row(inputs["L0_ffn_gate_b"]), ebr=row(inputs["L0_ffn_expert_bias"]),
        fW1p=np.ascontiguousarray(np.asarray(inputs["L0_ffn_w1"], np.float32)
                                  .reshape(NE, DC, 128, 4, HID // 4)
                                  .transpose(0, 2, 3, 1, 4)
                                  .reshape(NE, 128, DC * HID)),
        fW2p=np.ascontiguousarray(np.asarray(inputs["L0_ffn_w2"], np.float32)
                                  .reshape(NE, HC, 128, D).transpose(0, 2, 1, 3)
                                  .reshape(NE, 128, HC * D)),
        fb1p=np.ascontiguousarray(np.asarray(inputs["L0_ffn_b1"], np.float32)
                                  .reshape(NE, HID // 128, 128).transpose(0, 2, 1)),
        fb2p=arr(inputs["L0_ffn_b2"]),
        ident=c["ident"], lstrict=c["lstrict"], ones_bf=c["ones_bf"], ones_f=c["ones_f"],
        onescol_bf=c["onescol_bf"], iota8=c["iota8"], iotac=c["iotac"],
        st512=c["st512"], cosx=c["cosx"], sinx=c["sinx"],
    )
    in_maps = []
    for i in range(NCORES):
        m = dict(shared)
        m["x"] = np.ascontiguousarray(xfull[i])
        mc = np.zeros((NE, 1), np.float32)
        mc[:i] = 1.0
        m["maskcore"] = mc
        in_maps.append(m)
    res = run_bass_kernel_spmd(B.nc, in_maps, core_ids=list(range(NCORES)))
    return np.stack([res.results[i]["out"] for i in range(NCORES)], axis=0)


# version 13
# speedup vs baseline: 1.1756x; 1.0009x over previous
import numpy as np
import ml_dtypes
from contextlib import ExitStack

import concourse.bass as bass
import concourse.mybir as mybir
from concourse.bass_utils import run_bass_kernel_spmd

F32 = mybir.dt.float32
BF16 = mybir.dt.bfloat16
AF = mybir.ActivationFunctionType
ALU = mybir.AluOpType
AX = mybir.AxisListType

T = 2048
D = 1024
NT = 16
DC = 8
H = 16
DK = 64
CH = 64
NCH = T // CH
NE = 8
HID = 4096
HC = 32
CAP = 5120
NCORES = 8
NSTA = 15
NSTF = 11
EPS_LN = 1e-5
EPS_DEN = 1e-6


class SC:
    def __init__(self, handle):
        self.h = handle
        self.n = 0


class DQ:

    def __init__(self, sems):
        self.sems = sems
        self.k = len(sems)
        self.i = 0

    def tok(self, token):
        j = token // 16 - 1
        return (self.sems[j % self.k], 16 * (j // self.k + 1))

    def snapshot(self):
        return [(s, s.n) for s in self.sems]


class Build:

    def __init__(self, debug=False):
        self.nc = bass.Bass("TRN2", target_bir_lowering=False, debug=False)
        self.debug = debug
        self.ops = {k: [] for k in ("t", "v", "s", "g", "y")}
        self.auto = {}
        self.cnt = {"v": 0, "s": 0, "g": 0}
        self.stack = ExitStack()
        self.dbg = {}

    def op(self, eng, build_fn, waits=(), incs=()):
        my_idx = self.cnt[eng]
        self.cnt[eng] = my_idx + 1
        waits, incs = tuple(self.norm_waits(waits)), tuple(incs)
        a = self.auto[eng]

        def f(e, _w=waits, _i=incs, _k=my_idx, _fn=build_fn, _a=a):
            for sc, val in _w:
                if val > 0:
                    e.wait_ge(sc.h, val)
            if _k > 0:
                e.wait_ge(_a.h, _k)
            inst = _fn(e)
            inst.then_inc(_a.h, 1)
            for sc, k in _i:
                inst.then_inc(sc.h, k)

        self.ops[eng].append(f)

    def raw(self, eng, build_fn, waits=(), incs=()):
        waits, incs = tuple(self.norm_waits(waits)), tuple(incs)

        def f(e, _w=waits, _i=incs, _fn=build_fn):
            for sc, val in _w:
                if val > 0:
                    e.wait_ge(sc.h, val)
            inst = _fn(e)
            if inst is not None:
                for sc, k in _i:
                    inst.then_inc(sc.h, k)

        self.ops[eng].append(f)

    def wait(self, eng, sc, val=None):
        if isinstance(sc, DQ):
            assert val is not None
            if isinstance(val, list):
                for s, v in val:
                    self.wait(eng, s, v)
                return
            sc, val = sc.tok(val)
        v = sc.n if val is None else val
        if v > 0:
            def f(e, _s=sc, _v=v):
                e.wait_ge(_s.h, _v)
            self.ops[eng].append(f)

    def wait_snap(self, eng, snap):
        for s, v in snap:
            self.wait(eng, s, v)

    def mark(self, eng):
        return (self.auto[eng], self.cnt[eng])

    def norm_waits(self, waits):
        out = []
        for w in waits:
            if isinstance(w[0], DQ):
                out.append(w[0].tok(w[1]))
            else:
                out.append(w)
        return out

    def dma(self, eng, q, out, in_, accum=None, waits=(), indirect_out=None,
            indirect_in=None, bounds=None, elem_off=0):
        kw = {}
        if accum is not None:
            kw["accum_op"] = accum
        waits = tuple(self.norm_waits(waits))
        j = q.i
        q.i += 1
        s = q.sems[j % q.k]
        gate = s.n
        s.n += 16
        ind = (indirect_out is not None) or (indirect_in is not None)

        def f(e, _w=waits, _o=out, _i=in_, _kw=kw, _s=s, _g=gate,
              _io=indirect_out, _ii=indirect_in, _b=bounds, _eo=elem_off):
            for sc, val in _w:
                if val > 0:
                    e.wait_ge(sc.h, val)
            if _g > 0:
                e.wait_ge(_s.h, _g)
            if _io is not None or _ii is not None:
                ikw = {}
                if _eo:
                    ikw["element_offset"] = _eo
                if _b is not None:
                    ikw["bounds_check"] = _b
                    ikw["oob_is_err"] = False
                e.indirect_dma_start(
                    out=_o,
                    out_offset=(bass.IndirectOffsetOnAxis(_io, 0) if _io is not None else None),
                    in_=_i,
                    in_offset=(bass.IndirectOffsetOnAxis(_ii, 0) if _ii is not None else None),
                    **ikw).then_inc(_s.h, 16)
            else:
                e.dma_start(out=_o, in_=_i, **_kw).then_inc(_s.h, 16)

        self.ops[eng].append(f)
        return 16 * (j + 1)

    def scratch(self, name, shape, dtype):
        kind = "ExternalOutput" if self.debug else "Internal"
        t = self.nc.dram_tensor(name, list(shape), dtype, kind=kind)
        if self.debug:
            self.dbg[name] = t
        return t


def _bf(x):
    return np.asarray(x, dtype=ml_dtypes.bfloat16)


def make_constants():
    c = {}
    c["ident"] = _bf(np.eye(128, dtype=np.float32))
    c["lstrict"] = _bf(np.tril(np.ones((128, 128), np.float32), -1))
    c["ones_bf"] = _bf(np.ones((1, 128), np.float32))
    c["ones_f"] = np.ones((1, 128), np.float32)
    c["onescol_bf"] = _bf(np.ones((128, 1), np.float32))
    c["iota8"] = np.tile(np.arange(8, dtype=np.float32), (128, 1))
    c["iotac"] = np.arange(128, dtype=np.float32).reshape(128, 1)
    c["st512"] = (np.arange(16, dtype=np.float32) * 512).reshape(1, 16)
    inv = (1.0 / 10000.0 ** (np.arange(32, dtype=np.float32) / 32.0)).astype(np.float32)
    ang = np.arange(T, dtype=np.float32)[:, None] * inv[None, :]
    cosv = np.cos(ang).astype(np.float32)
    sinv = np.sin(ang).astype(np.float32)
    cosx = np.concatenate([cosv, cosv], axis=1)
    sinx = np.concatenate([-sinv, sinv], axis=1)
    c["cosx"] = np.ascontiguousarray(cosx.reshape(NT, 128, 64).transpose(1, 0, 2))
    c["sinx"] = np.ascontiguousarray(sinx.reshape(NT, 128, 64).transpose(1, 0, 2))
    return c


def build(debug=False, stop_after="all", p3_mode=3):
    B = Build(debug=debug)
    nc = B.nc
    E = B.stack.enter_context

    def din(name, shape, dtype=F32):
        return nc.dram_tensor(name, list(shape), dtype, kind="ExternalInput").ap()

    x_in = din("x", (T, D))
    Wq, bq = din("Wq", (D, D)), din("bq", (1, D))
    Wk, bk = din("Wk", (D, D)), din("bk", (1, D))
    Wv, bv = din("Wv", (D, D)), din("bv", (1, D))
    Wg, bg = din("Wg", (D, D)), din("bg", (1, D))
    Wo, bo = din("Wo", (D, D)), din("bo", (1, D))
    rW, rb = din("rW", (128, DC, NE)), din("rb", (1, NE))
    aW1p = din("aW1p", (NE, 128, DC * HID))
    aW2p = din("aW2p", (NE, 128, HC * D))
    ab1p = din("ab1p", (NE, 128, HID // 128))
    ab2p = din("ab2p", (NE, D))
    s1c, s2c = din("s1c", (128, DC)), din("s2c", (128, DC))
    gwT = din("gwT", (128, DC, NE))
    gbr, ebr = din("gbr", (1, NE)), din("ebr", (1, NE))
    fW1p = din("fW1p", (NE, 128, DC * HID))
    fW2p = din("fW2p", (NE, 128, HC * D))
    fb1p = din("fb1p", (NE, 128, HID // 128))
    fb2p = din("fb2p", (NE, D))
    ident_i = din("ident", (128, 128), BF16)
    lstrict_i = din("lstrict", (128, 128), BF16)
    ones_bf_i = din("ones_bf", (1, 128), BF16)
    ones_f_i = din("ones_f", (1, 128))
    onescol_i = din("onescol_bf", (128, 1), BF16)
    iota8_i = din("iota8", (128, NE))
    iotac_i = din("iotac", (128, 1))
    st512_i = din("st512", (1, 16))
    cosx_i = din("cosx", (128, NT, 64))
    sinx_i = din("sinx", (128, NT, 64))
    maskcore = din("maskcore", (NE, 1))

    OUT = nc.dram_tensor("out", [T, D], F32, kind="ExternalOutput").ap()
    OUT_T = OUT.rearrange("(t p) d -> t p d", p=128)
    X_T = x_in.rearrange("(t p) d -> t p d", p=128)

    s_qT = B.scratch("s_qT", (NT, 128, D), BF16)
    s_k = B.scratch("s_k", (NT, 128, D), BF16)
    s_v = B.scratch("s_v", (NT, 128, H * 65), BF16)
    s_g = B.scratch("s_g", (NT, 128, D), F32)
    s_ao = B.scratch("s_ao", (NT, 128, D), BF16)
    s_xnT = B.scratch("s_xnT", (NT, 128, D), BF16)
    s_xm = B.scratch("s_xm", (T, D), BF16)
    s_xn = B.scratch("s_xn", (T, D), BF16)
    s_atok = B.scratch("s_atok", (NSTA * 512, 1), mybir.dt.int32)
    s_ftok = B.scratch("s_ftok", (NSTF * 512, 1), mybir.dt.int32)
    s_ay = B.scratch("s_ay", (NSTA * 512, D), F32)
    s_fy = B.scratch("s_fy", (NSTF * 512, D), F32)
    cc_in = nc.dram_tensor("cc_in", [1, NE], F32)
    cc_out = nc.dram_tensor("cc_out", [NCORES, NE], F32)
    if debug:
        d_xn = B.scratch("d_xn", (T, D), F32)
        d_cnt = B.scratch("d_cnt", (T, NE), F32)
        d_ohf = B.scratch("d_ohf", (T, NE), F32)
        d_h1 = B.scratch("d_h1", (T, D), F32)
        d_rows = B.scratch("d_rows", (1, 128), F32)

    cT = E(nc.sbuf_tensor("cT", [128, DC * T], BF16))
    big1 = E(nc.sbuf_tensor("big1", [128, 16384], F32))
    big2 = E(nc.sbuf_tensor("big2", [128, 16384], F32))
    work1 = E(nc.sbuf_tensor("work1", [128, 8192], F32))
    work2 = E(nc.sbuf_tensor("work2", [128, 2048], F32))
    smalls = E(nc.sbuf_tensor("smalls", [128, 1184], F32))
    consts = E(nc.sbuf_tensor("consts", [128, 24], F32))
    cbf = E(nc.sbuf_tensor("cbf", [128, 400], BF16))
    onesr_bf = E(nc.sbuf_tensor("onesr_bf", [1, 128], BF16))
    onesr_f = E(nc.sbuf_tensor("onesr_f", [1, 128], F32))
    bcol = E(nc.sbuf_tensor("bcol", [128, NE * 32], F32))
    mcol = E(nc.sbuf_tensor("mcol", [NE, 1], F32))
    iotac = E(nc.sbuf_tensor("iotac_sb", [128, 1], F32))
    posA = E(nc.sbuf_tensor("posA", [128, 32], mybir.dt.int32))
    posF = E(nc.sbuf_tensor("posF", [128, 16], mybir.dt.int32))
    idxw = E(nc.sbuf_tensor("idxw", [128, 4], mybir.dt.int32))
    tik = E(nc.sbuf_tensor("tik", [128, 8], mybir.dt.int32))
    sct = E(nc.sbuf_tensor("sct", [128, 8], mybir.dt.int32))
    b1g = E(nc.sbuf_tensor("b1g", [128, 32], F32))
    etb = E(nc.sbuf_tensor("etb", [128, 16], F32))
    lbb = E(nc.sbuf_tensor("lbb", [128, NE], F32))
    ccsb = E(nc.sbuf_tensor("ccsb", [NCORES, NE], F32))


    PF = [E(nc.psum_tensor(f"pf{i}", [128, 512], F32)) for i in range(6)]
    PT = [E(nc.psum_tensor(f"pt{i}", [128, 1024], BF16)) for i in range(2)]

    B.auto["v"] = SC(E(nc.semaphore("auto_v")))
    B.auto["s"] = SC(E(nc.semaphore("auto_s")))
    B.auto["g"] = SC(E(nc.semaphore("auto_g")))
    pe = SC(E(nc.semaphore("pe")))
    wq = DQ([SC(E(nc.semaphore(f"wq{i}"))) for i in range(4)])
    aq = DQ([SC(E(nc.semaphore(f"aq{i}"))) for i in range(8)])
    oq = DQ([SC(E(nc.semaphore(f"oq{i}"))) for i in range(4)])
    iq = DQ([SC(E(nc.semaphore(f"iq{i}"))) for i in range(6)])
    rbA = E(nc.gpsimd.register("rbA"))
    rbF = E(nc.gpsimd.register("rbF"))
    B.raw("g", lambda g: g.reg_mov(rbA, NSTA * 512 - 1) and None)
    B.raw("g", lambda g: g.reg_mov(rbF, NSTF * 512 - 1) and None)
    stq = DQ([SC(E(nc.semaphore(f"stq{i}"))) for i in range(4)])
    cc = SC(E(nc.semaphore("cc")))

    def mm(out_ap, lhsT, rhs, start, stop, waits=(), incs=()):
        def f(t_, _o=out_ap, _l=lhsT, _r=rhs, _s=start, _p=stop):
            return t_.matmul(_o, _l, _r, start=_s, stop=_p, skip_group_check=True)
        B.raw("t", f, waits=waits, incs=incs)

    def pe_inc():
        pe.n += 1
        return (pe, pe.n)

    IDENT = cbf[:, 0:128]
    LSTRICT = cbf[:, 128:256]
    ONESCOL = cbf[:, 256:257]
    rWsb = cbf[:, 260:324]
    gWsb = cbf[:, 324:388]
    OHBF = cbf[:, 388:396]
    OHBF2 = smalls[:, 400:404].bitcast(BF16)

    def transp(out_ap, in_ap, waits=(), incs=()):
        def f(t_, _o=out_ap, _i=in_ap):
            return t_.transpose(_o, _i, IDENT)
        B.raw("t", f, waits=waits, incs=incs)

    class TT:

        def __init__(self):
            self.bank_free = [None, None]

        def run(self, src_tile, dst_fn, in_mark, extra_first_wait=(), scale_col=None):
            for dc in range(DC):
                bk = PT[dc % 2]
                w = []
                if dc == 0 and in_mark is not None:
                    w.append(in_mark)
                if self.bank_free[dc % 2] is not None:
                    w.append(self.bank_free[dc % 2])
                transp(bk[:, 0:128], src_tile[:, dc * 128:(dc + 1) * 128],
                       waits=w, incs=[(pe, 1)])
                m = pe_inc()
                B.wait("v", *m)
                ww = list(extra_first_wait) if dc == 0 else []
                if scale_col is None:
                    B.op("v", lambda v, _o=dst_fn(dc), _i=bk[:, 0:128]:
                         v.tensor_copy(_o, _i), waits=ww)
                else:
                    B.op("v", lambda v, _o=dst_fn(dc), _i=bk[:, 0:128],
                         _s=scale_col(dc):
                         v.tensor_scalar(_o, _i, _s, None, ALU.mult), waits=ww)
                self.bank_free[dc % 2] = B.mark("v")

    tpose = TT()

    B.dma("y", aq, onesr_bf[:], ones_bf_i)
    B.dma("y", aq, onesr_f[:], ones_f_i)
    B.dma("y", aq, mcol[:], maskcore)
    IOTA = consts[:, 0:8]
    S1C = consts[:, 8:16]
    S2C = consts[:, 16:24]
    B.dma("y", aq, IOTA, iota8_i)
    B.dma("y", aq, iotac[:], iotac_i)
    B.dma("y", aq, S1C, s1c)
    B.dma("y", aq, S2C, s2c)
    B.dma("y", aq, IDENT, ident_i)
    B.dma("y", aq, LSTRICT, lstrict_i)
    B.dma("y", aq, ONESCOL, onescol_i)
    BROW = smalls[0:1, 640:1152].bitcast(BF16)
    RRB = smalls[0:1, 1168:1172].bitcast(BF16)
    RGB = smalls[0:1, 1172:1176].bitcast(BF16)
    REB = smalls[0:1, 1176:1180].bitcast(BF16)
    for dst, src_ in ((RRB, rb), (RGB, gbr), (REB, ebr)):
        B.dma("g", wq, dst, src_)

    COS = work1[:, 0:NT * 64]
    SIN = work1[:, NT * 64:2 * NT * 64]
    B.dma("y", aq, COS, cosx_i.rearrange("p t c -> p (t c)"))
    B.dma("y", aq, SIN, sinx_i.rearrange("p t c -> p (t c)"))
    consts_tok = aq.snapshot()
    B.dma("g", wq, rWsb.rearrange("p (c e) -> p c e", e=NE), rW)
    B.dma("g", wq, gWsb.rearrange("p (c e) -> p c e", e=NE), gwT)
    wq_const_tok = wq.snapshot()

    stats = smalls[:, 0:32]
    logitsA = smalls[:, 32:160]
    cntC = smalls[:, 160:288]
    wgA = smalls[:, 288:416]
    ohF = smalls[:, 416:544]
    gcols = smalls[:, 544:608]
    scr8a = smalls[:, 608:616]
    scr8b = smalls[:, 616:624]
    scr8c = smalls[:, 624:632]
    thr_b = smalls[:, 632:640]
    run_row = smalls[0:1, 1152:1160]
    off_row = smalls[0:1, 1160:1168]
    ST512R = smalls[0:1, 288:304]
    krow = smalls[0:1, 304:312]
    prow = smalls[0:1, 312:320]
    lbase = smalls[0:1, 320:328]
    endr = smalls[0:1, 328:336]
    et_row = smalls[0:1, 336:352]
    et_row_tmp = smalls[0:1, 384:400]
    gcA = smalls[:, 352:384]
    B.dma("y", aq, ST512R, st512_i)
    B.op("v", lambda v: v.memset(posA[:], 0))
    zvm = B.mark("v")
    for tgt, nst in ((s_atok, NSTA), (s_ftok, NSTF)):
        cols = nst * 4
        done = 0
        while done < cols:
            cw = min(32, cols - done)
            B.dma("y", aq, tgt.ap()[done * 128:(done + cw) * 128].rearrange(
                "(p c) one -> p (c one)", p=128), posA[:, 0:cw], waits=[zvm])
            done += cw


    xn = big1[:]
    xstage = [work1[:, 2048 + i * 1024:2048 + (i + 1) * 1024] for i in range(2)]
    sq = work1[:, 4096:4096 + 1024]
    x_tok = [None] * NT
    cons_mark = {}
    B.wait_snap("v", consts_tok)
    B.wait_snap("s", consts_tok)
    eps_col = stats[:, 30:31]
    B.op("v", lambda v: v.memset(eps_col, EPS_LN))
    for t in range(NT):
        w = []
        if t >= 2:
            w.extend(cons_mark[t - 2])
        x_tok[t] = B.dma("y", aq, xstage[t % 2], X_T[t], waits=w)
        xt = xstage[t % 2]
        xnt = xn[:, t * 1024:(t + 1) * 1024]
        c_s1 = stats[:, 0:1]
        c_ss = stats[:, 1:2]
        c_m = stats[:, 2:3]
        c_r = stats[:, 3:4]
        B.wait("v", aq, x_tok[t])
        B.wait("s", aq, x_tok[t])
        prev_vm0 = B.mark("v")
        B.op("v", lambda v, _x=xt, _o=c_s1: v.reduce_sum(_o, _x, axis=AX.X))
        B.op("s", lambda s, _x=xt, _a=c_ss: s.activation(
            out=sq[:], in_=_x, func=AF.Square, accum_out=_a), waits=[prev_vm0])
        B.op("v", lambda v, _o=c_m, _i=c_s1: v.tensor_scalar(_o, _i, 1.0 / D, None, ALU.mult))
        sm = B.mark("s")
        B.wait("v", *sm)
        B.op("v", lambda v, _o=c_s1, _m=c_m: v.tensor_mul(_o, _m, _m))
        B.op("v", lambda v, _o=c_ss, _i=c_ss: v.tensor_scalar(_o, _i, 1.0 / D, None, ALU.mult))
        B.op("v", lambda v, _o=c_ss, _a=c_ss, _b=c_s1: v.tensor_sub(_o, _a, _b))
        vm = B.mark("v")
        B.wait("s", *vm)
        B.op("s", lambda s, _o=c_ss, _i=c_ss: s.activation(out=_o, in_=_i, func=AF.Sqrt, bias=eps_col))
        sm = B.mark("s")
        B.wait("v", *sm)
        B.op("v", lambda v, _o=c_r, _i=c_ss: v.reciprocal(_o, _i))
        B.op("v", lambda v, _o=xnt, _x=xt, _m=c_m, _r=c_r:
             v.tensor_scalar(_o, _x, _m, _r, ALU.subtract, ALU.mult))
        cons_mark[t] = [B.mark("v"), B.mark("s")]
    dbg_toks = []
    if debug:
        vm = B.mark("v")
        B.wait("y", *vm)
        dbg_toks.append((aq, B.dma("y", aq, d_xn.ap().rearrange("(t p) d -> p t d", p=128),
              xn.rearrange("p (t d) -> p t d", d=D))))

    if stop_after == "p0":
        return _finish_trunc(B, nc, [aq, wq])
    B.wait_snap("t", consts_tok)

    abf = [work1[:, 5120 + i * 512:5120 + (i + 1) * 512].bitcast(BF16) for i in range(2)]
    xnbf = [work1[:, 6144 + i * 512:6144 + (i + 1) * 512].bitcast(BF16) for i in range(2)]
    tsg = [work1[:, 7168 + i * 512:7168 + (i + 1) * 512].bitcast(BF16) for i in range(2)]
    xnT_spill = {}
    xn_sp = {}
    tpose_done_pe = {}
    for t in range(NT):
        xnt = xn[:, t * 1024:(t + 1) * 1024]
        c_ss = stats[:, 4:5]
        c_r1 = stats[:, 5:6]
        a_t = abf[t % 2]
        xn_t = xnbf[t % 2]
        ts_t = tsg[t % 2]
        prev_vm1 = B.mark("v")
        B.op("s", lambda s, _x=xnt, _a=c_ss: s.activation(
            out=sq[:], in_=_x, func=AF.Square, accum_out=_a), waits=[prev_vm1])
        B.op("s", lambda s, _o=c_ss, _i=c_ss: s.activation(out=_o, in_=_i, func=AF.Sqrt, scale=1.0 / D))
        sm = B.mark("s")
        B.wait("v", *sm)
        B.op("v", lambda v, _o=c_r1, _i=c_ss: v.reciprocal(_o, _i))
        wv = [(pe, tpose_done_pe[t - 2])] if t >= 2 else []
        B.op("v", lambda v, _o=a_t, _x=xnt, _r=c_r1:
             v.tensor_scalar(_o, _x, _r, None, ALU.mult), waits=wv)
        wxn = [(aq, xn_sp[t - 2])] if (t - 2) in xn_sp else []
        B.op("v", lambda v, _o=xn_t, _x=xnt: v.tensor_copy(_o, _x), waits=wxn)
        vm = B.mark("v")
        tpose.run(a_t, lambda dc, _t=t: cT[:, dc * T + _t * 128: dc * T + _t * 128 + 128], vm,
                  scale_col=lambda dc: S1C[:, dc:dc + 1])
        war = [(aq, xnT_spill[t - 2])] if t >= 2 else []
        tpose.run(xn_t, lambda dc, _ts=ts_t: _ts[:, dc * 128:(dc + 1) * 128], None,
                  extra_first_wait=war)
        tpose_done_pe[t] = pe.n
        vm2 = B.mark("v")
        B.wait("y", *vm2)
        xnT_spill[t] = B.dma("y", aq, s_xnT.ap()[t], ts_t)
        xn_sp[t] = B.dma("y", aq, s_xn.ap().rearrange("(t p) d -> t p d", p=128)[t], xn_t,
                         waits=[vm2])
    p1_vmark = B.mark("v")

    if stop_after == "p1":
        return _finish_trunc(B, nc, [aq, wq])
    wbuf = big2[:].bitcast(BF16)
    qkst = [work1[:, 2048 + i * 512:2048 + (i + 1) * 512].bitcast(BF16) for i in range(2)]
    tmp1 = work1[:, 3072:3584]
    tmp2 = work1[:, 3584:4096]
    tmp3 = work1[:, 4096:4608]
    v65 = [work1[:, 4608 + i * 520:4608 + (i + 1) * 520].bitcast(BF16) for i in range(2)]
    qTst = [work1[:, 5648 + i * 512:5648 + (i + 1) * 512].bitcast(BF16) for i in range(2)]
    g32 = [work2[:, i * 1024:(i + 1) * 1024] for i in range(2)]

    p1_aq = aq.snapshot()
    B.wait_snap("v", p1_aq)
    B.wait_snap("s", p1_aq)
    B.op("v", lambda v: v.memset(
        v65[0].rearrange("p (h c) -> p h c", c=65)[:, :, 64:65], 1.0))
    B.op("v", lambda v: v.memset(
        v65[1].rearrange("p (h c) -> p h c", c=65)[:, :, 64:65], 1.0))

    PROJS = [("q", Wq, bq, s_qT), ("k", Wk, bk, s_k), ("v", Wv, bv, s_v), ("g", Wg, bg, s_g)]
    spill_tok = {}
    stage_cons_pe = {}
    bank_cons = {0: p1_vmark, 1: p1_vmark}
    wslot_last_pe = {0: None, 1: None}
    for pi, (pname, Wsrc, brow, sdst) in enumerate(PROJS):
        wslot = wbuf[:, (pi % 2) * 8192:(pi % 2) * 8192 + 8192]
        wwar = [(pe, wslot_last_pe[pi % 2])] if wslot_last_pe[pi % 2] else []
        wtok = B.dma("g", wq, wslot.rearrange("p (c n) -> p c n", n=D),
                     Wsrc.rearrange("(c p) n -> p c n", p=128), waits=wwar)
        btok = B.dma("g", wq, BROW, brow, waits=[(pe, pe.n)])
        B.wait("t", wq, btok)
        B.wait("t", wq, wtok)
        if pi == 0:
            B.wait("t", p1_vmark[0], p1_vmark[1])
        for t in range(NT):
            for nh in range(2):
                gidx = t * 2 + nh
                bank = PF[gidx % 2]
                for dc in range(DC):
                    w = []
                    if dc == 0:
                        w.append(bank_cons[gidx % 2])
                    mm(bank[:], cT[:, dc * T + t * 128: dc * T + t * 128 + 128],
                       wslot[:, dc * 1024 + nh * 512: dc * 1024 + nh * 512 + 512],
                       start=(dc == 0), stop=False, waits=w)
                mm(bank[:], onesr_bf[:], BROW[:, nh * 512:nh * 512 + 512], False, True,
                   incs=[(pe, 1)])
                mmk = pe_inc()
                if pname in ("q", "k"):
                    stg = qkst[t % 2]
                    pv = bank[:].rearrange("p (h c) -> p h c", c=64)
                    cb = COS[:, t * 64:(t + 1) * 64].unsqueeze(1).broadcast_to([128, 8, 64])
                    sb0 = SIN[:, t * 64:t * 64 + 32].unsqueeze(1).broadcast_to([128, 8, 32])
                    sb1 = SIN[:, t * 64 + 32:t * 64 + 64].unsqueeze(1).broadcast_to([128, 8, 32])
                    t1v = tmp1[:].rearrange("p (h c) -> p h c", c=64)
                    t2v = tmp2[:].rearrange("p (h c) -> p h c", c=64)
                    B.wait("v", *mmk)
                    B.op("v", lambda v, _p=pv, _c=cb, _o=t1v: v.tensor_mul(_o, _p, _c))
                    B.op("v", lambda v, _p=pv, _s=sb0, _o=t2v:
                         v.tensor_mul(_o[:, :, 0:32], _p[:, :, 32:64], _s))
                    B.op("v", lambda v, _p=pv, _s=sb1, _o=t2v:
                         v.tensor_mul(_o[:, :, 32:64], _p[:, :, 0:32], _s))
                    bank_cons[gidx % 2] = B.mark("v")
                    B.op("v", lambda v: v.tensor_add(tmp1[:], tmp1[:], tmp2[:]))
                    B.op("v", lambda v: v.tensor_scalar(tmp2[:], tmp1[:], 0.0, None, ALU.min))
                    vm = B.mark("v")
                    B.wait("s", *vm)
                    B.op("s", lambda s: s.activation(out=tmp3[:], in_=tmp2[:], func=AF.Exp))
                    sm = B.mark("s")
                    B.wait("v", *sm)
                    ww = []
                    if pname == "q" and (pi, t - 2) in stage_cons_pe:
                        ww.append((pe, stage_cons_pe[(pi, t - 2)]))
                    if pname == "k" and (pi, t - 2) in spill_tok:
                        ww.append((aq, spill_tok[(pi, t - 2)]))
                    B.op("v", lambda v, _o=stg[:, nh * 512:nh * 512 + 512]:
                         v.scalar_tensor_tensor(_o, tmp1[:], 0.0, tmp3[:], ALU.max, ALU.add),
                         waits=ww)
                elif pname == "v":
                    vst = v65[t % 2]
                    ww = [(aq, spill_tok[(pi, t - 2)])] if (pi, t - 2) in spill_tok else []
                    B.wait("v", *mmk)
                    B.op("v", lambda v,
                         _o=vst.rearrange("p (h c) -> p h c", c=65)[:, nh * 8:nh * 8 + 8, 0:64],
                         _i=bank[:].rearrange("p (h c) -> p h c", c=64):
                         v.tensor_copy(_o, _i), waits=ww)
                    bank_cons[gidx % 2] = B.mark("v")
                elif pname == "g":
                    gst = g32[t % 2]
                    ww = [(aq, spill_tok[(pi, t - 2)])] if (pi, t - 2) in spill_tok else []
                    B.wait("s", *mmk)
                    B.op("s", lambda s, _o=gst[:, nh * 512:nh * 512 + 512], _i=bank[:]:
                         s.activation(out=_o, in_=_i, func=AF.Sigmoid), waits=ww)
                    bank_cons[gidx % 2] = B.mark("s")
            if pname == "q":
                stg = qkst[t % 2]
                qts = qTst[t % 2]
                vm = B.mark("v")
                war = [(aq, spill_tok[(pi, t - 2)])] if (pi, t - 2) in spill_tok else []
                tpose.run(stg, lambda dc, _q=qts: _q[:, dc * 128:(dc + 1) * 128], vm,
                          extra_first_wait=war)
                stage_cons_pe[(pi, t)] = pe.n
                vm2 = B.mark("v")
                B.wait("y", *vm2)
                spill_tok[(pi, t)] = B.dma("y", aq, sdst.ap()[t], qts)
            elif pname == "k":
                vm = B.mark("v")
                B.wait("y", *vm)
                spill_tok[(pi, t)] = B.dma("y", aq, sdst.ap()[t], qkst[t % 2])
            elif pname == "v":
                vm = B.mark("v")
                B.wait("y", *vm)
                spill_tok[(pi, t)] = B.dma("y", aq, sdst.ap()[t], v65[t % 2])
            elif pname == "g":
                sm = B.mark("s")
                B.wait("y", *sm)
                spill_tok[(pi, t)] = B.dma("y", aq, sdst.ap()[t], g32[t % 2])
        wslot_last_pe[pi % 2] = pe.n
    p2_aq = aq.snapshot()
    p2_vmark = B.mark("v")
    p2_smark = B.mark("s")

    if stop_after == "p2":
        return _finish_trunc(B, nc, [aq, wq])
    wo_slot = wbuf[:, 0:8192]
    wo_tok = B.dma("g", wq, wo_slot.rearrange("p (c n) -> p c n", n=D),
                   Wo.rearrange("(c p) n -> p c n", p=128),
                   waits=[(pe, pe.n)])
    wob_tok = B.dma("g", wq, BROW, bo, waits=[(pe, pe.n)])

    off = [0]

    def aw1(ncols):
        ap = work1[:, off[0]:off[0] + ncols]
        off[0] += ncols
        return ap

    qT_b = [aw1(512).bitcast(BF16) for _ in range(2)]
    k_b = [aw1(512).bitcast(BF16) for _ in range(2)]
    v_b = [aw1(520).bitcast(BF16) for _ in range(2)]
    g_b = [aw1(1024) for _ in range(2)]
    szsb = aw1(260).bitcast(BF16)
    s_acc = aw1(520)
    rden = aw1(16)
    ptmp = aw1(256)
    ao_b = [aw1(512).bitcast(BF16) for _ in range(2)]
    assert off[0] <= 8192, off[0]

    def S_ap(h):
        return PF[2 + h // 8][(h % 2) * 64:(h % 2) * 64 + 64,
                              ((h // 2) % 4) * 65:((h // 2) % 4) * 65 + 65]

    def SZ_ap(h):
        p0 = (h % 2) * 64
        c0 = (h // 8) * 260 + ((h // 2) % 4) * 65
        return szsb[p0:p0 + 64, c0:c0 + 65]

    NUMB = [PF[0], PF[1], PF[4], PF[5]]

    def num_ap(h, c):
        return NUMB[h // 4][(c % 2) * 64:(c % 2) * 64 + 64,
                            (h % 4) * 65:(h % 4) * 65 + 65]

    rl_tok = {}
    scan_done = {}

    def issue_reload(t):
        w = []
        if t >= 2 and (t - 2) in scan_done:
            w = [(pe, scan_done[t - 2][0]), scan_done[t - 2][1]]
            if (t - 2) in ao_spill:
                w.append((aq, ao_spill[t - 2]))
        pr = t % 2
        if t < 2:
            w = list(w) + p2_aq
        rl_tok[t] = [
            B.dma("y", aq, qT_b[pr], s_qT.ap()[t], waits=w),
            B.dma("y", aq, k_b[pr], s_k.ap()[t], waits=w),
            B.dma("y", aq, v_b[pr], s_v.ap()[t], waits=w),
            B.dma("y", aq, g_b[pr], s_g.ap()[t], waits=w),
        ]

    ao_spill = {}
    B.op("v", lambda v: v.memset(s_acc[:], 0.0), waits=p2_aq)
    issue_reload(0)
    issue_reload(1)
    B.wait("t", p2_vmark[0], p2_vmark[1])
    B.wait("t", p2_smark[0], p2_smark[1])
    sz_mark = None
    epi_mark = None
    prev_numk = None
    for c in range(NCH):
        t = c // 2
        pr = t % 2
        cp = (c % 2) * 64
        if c % 2 == 0:
            B.wait("t", aq, rl_tok[t][1])
            B.wait("t", aq, rl_tok[t][2])
        for h in range(H):
            w = []
            if h == 0 and sz_mark is not None:
                w.append(sz_mark)
            if h == 0 and prev_numk is not None:
                w.append(prev_numk)
            mm(S_ap(h),
               k_b[pr][cp:cp + 64, h * 64:(h + 1) * 64],
               v_b[pr][cp:cp + 64, h * 65:h * 65 + 65],
               start=True, stop=True, waits=w,
               incs=([(pe, 1)] if h == H - 1 else []))
        supd = pe_inc()
        B.wait("v", *supd)
        B.op("v", lambda v: v.tensor_add(s_acc[:, 0:260], s_acc[:, 0:260], PF[2][:, 0:260]))
        B.op("v", lambda v: v.tensor_add(s_acc[:, 260:520], s_acc[:, 260:520], PF[3][:, 0:260]))
        sz_mark = B.mark("v")
        B.op("v", lambda v: v.tensor_copy(szsb[:, 0:260], s_acc[:, 0:260]))
        B.op("v", lambda v: v.tensor_copy(szsb[:, 260:520], s_acc[:, 260:520]))
        sz_mark2 = B.mark("v")
        B.wait("t", sz_mark2[0], sz_mark2[1])
        if epi_mark is not None:
            B.wait("t", epi_mark[0], epi_mark[1])
        if c % 2 == 0:
            B.wait("t", aq, rl_tok[t][0])
        if p3_mode < 2:
            epi_mark = sz_mark2
            if c % 2 == 1:
                scan_done[t] = (pe.n, sz_mark2)
                ao_spill[t] = rl_tok[t][3]
                if t + 2 < NT:
                    issue_reload(t + 2)
            continue
        order = list(range(0, H, 2)) + list(range(1, H, 2))
        evens_mark = None
        for i, h in enumerate(order):
            w = []
            if i == 8:
                w = [evens_mark]
            mm(num_ap(h, c),
               qT_b[pr][(h % 2) * 64:(h % 2) * 64 + 64,
                        (h // 2) * 128 + cp:(h // 2) * 128 + cp + 64],
               SZ_ap(h), start=True, stop=True, waits=w,
               incs=([(pe, 1)] if i in (7, 15) else []))
            if i == 7:
                evens_mark = pe_inc()
        numk = pe_inc()
        prev_numk = numk
        B.wait("v", *numk)
        if p3_mode < 3:
            epi_mark = (pe, numk[1]) if False else sz_mark2
            B.wait("v", *numk)
            B.op("v", lambda v: v.tensor_copy(ptmp[0:64, 0:256], NUMB[0][0:64, 0:256]))
            epi_mark = B.mark("v")
            if c % 2 == 1:
                scan_done[t] = (pe.n, epi_mark)
                ao_spill[t] = rl_tok[t][3]
                if t + 2 < NT:
                    issue_reload(t + 2)
            continue
        if c % 2 == 0:
            B.wait("v", aq, rl_tok[t][3])
        if c % 2 == 0 and t >= 2 and (t - 2) in ao_spill:
            B.wait("v", aq, ao_spill[t - 2])
        for b4 in range(4):
            bank = NUMB[b4]
            dv = bank[cp:cp + 64, 0:260].rearrange("p (h x) -> p h x", x=65)[:, :, 64:65]
            ov = rden[cp:cp + 64, b4 * 4:(b4 + 1) * 4].unsqueeze(2)
            B.op("v", lambda v, _d=dv, _o=ov: v.tensor_scalar(_o, _d, EPS_DEN, None, ALU.add))
        B.op("v", lambda v, _r=rden[cp:cp + 64, 0:16]: v.reciprocal(_r, _r))
        for b4 in range(4):
            bank = NUMB[b4]
            nv = bank[cp:cp + 64, 0:260].rearrange("p (h x) -> p h x", x=65)[:, :, 0:64]
            rv = rden[cp:cp + 64, b4 * 4:(b4 + 1) * 4].unsqueeze(2).broadcast_to([64, 4, 64])
            pt_ = ptmp[cp:cp + 64, 0:256].rearrange("p (h x) -> p h x", x=64)
            B.op("v", lambda v, _n=nv, _r=rv, _o=pt_: v.tensor_mul(_o, _n, _r))
            gv = g_b[pr][cp:cp + 64, b4 * 256:(b4 + 1) * 256]
            av = ao_b[pr][cp:cp + 64, b4 * 256:(b4 + 1) * 256]
            B.op("v", lambda v, _g=gv, _o=av, _p=ptmp[cp:cp + 64, 0:256]:
                 v.tensor_mul(_o, _p, _g))
        epi_mark = B.mark("v")
        if c % 2 == 1:
            B.wait("y", epi_mark[0], epi_mark[1])
            ao_spill[t] = B.dma("y", aq, s_ao.ap()[t], ao_b[pr])
            scan_done[t] = (pe.n, epi_mark)
            if t + 2 < NT:
                issue_reload(t + 2)
    p3_aq = aq.snapshot()
    p3_vmark = B.mark("v")

    if stop_after == "p3":
        return _finish_trunc(B, nc, [aq, wq])
    aol = [work1[:, i * 512:(i + 1) * 512].bitcast(BF16) for i in range(2)]
    aoT = work1[:, 1024:1536].bitcast(BF16)
    h1t = work1[:, 1536:2560]
    outst = [work1[:, 2560 + i * 1024:2560 + (i + 1) * 1024] for i in range(2)]
    xst4 = [work1[:, 4608 + i * 1024:4608 + (i + 1) * 1024] for i in range(2)]
    sq4 = work1[:, 6656:7680]
    xmst = [work2[:, i * 512:(i + 1) * 512].bitcast(BF16) for i in range(2)]
    B.wait("t", wq, wo_tok)
    B.wait("t", wq, wob_tok)
    B.wait("t", p3_vmark[0], p3_vmark[1])
    ao_tok = [None] * NT
    x4_tok = [None] * NT
    tpose_done_p4 = {}
    out_tok4 = {}
    h1dbg_tok = {}
    xm_sp = {}
    for t in range(2):
        w0 = [(pe, pe.n), p3_vmark] + p3_aq
        ao_tok[t] = B.dma("y", aq, aol[t % 2], s_ao.ap()[t], waits=w0)
        x4_tok[t] = B.dma("y", aq, xst4[t % 2], X_T[t], waits=w0)
    for t in range(NT):
        B.wait("t", aq, ao_tok[t])
        vm_prev = B.mark("v")
        tpose.run(aol[t % 2], lambda dc: aoT[:, dc * 128:(dc + 1) * 128], vm_prev)
        aoT_mark = B.mark("v")
        for nh in range(2):
            bank = PF[nh]
            w = [aoT_mark] if nh == 0 else []
            if nh == 0 and debug and (t - 1) in h1dbg_tok:
                wh1 = [(aq, h1dbg_tok[t - 1])]
            else:
                wh1 = []
            for dc in range(DC):
                ww = list(w) if dc == 0 else []
                mm(bank[:], aoT[:, dc * 128:(dc + 1) * 128],
                   wo_slot[:, dc * 1024 + nh * 512: dc * 1024 + nh * 512 + 512],
                   start=(dc == 0), stop=False, waits=ww)
            mm(bank[:], onesr_bf[:], BROW[:, nh * 512:nh * 512 + 512], False, True,
               incs=[(pe, 1)])
            mk = pe_inc()
            B.wait("v", *mk)
            B.op("v", lambda v, _o=h1t[:, nh * 512:nh * 512 + 512], _p=bank[:],
                 _x=xn[:, t * 1024 + nh * 512: t * 1024 + nh * 512 + 512]:
                 v.tensor_add(_o, _p, _x), waits=wh1)
        B.wait("v", aq, x4_tok[t])
        wout = [(stq, out_tok4[t - 2])] if t >= 2 else []
        B.op("v", lambda v, _o=outst[t % 2], _a=h1t[:], _b=xst4[t % 2]:
             v.tensor_add(_o, _a, _b), waits=wout)
        vm = B.mark("v")
        B.wait("y", *vm)
        out_tok4[t] = B.dma("y", stq, OUT_T[t], outst[t % 2])
        if debug:
            h1dbg_tok[t] = B.dma("y", aq, d_h1.ap().rearrange("(t p) d -> t p d", p=128)[t], h1t[:])
        if t + 2 < NT:
            wpre = [(pe, pe.n), vm]
            ao_tok[t + 2] = B.dma("y", aq, aol[t % 2], s_ao.ap()[t + 2], waits=wpre)
            x4_tok[t + 2] = B.dma("y", aq, xst4[t % 2], X_T[t + 2], waits=wpre)
        c_ss = stats[:, 6:7]
        c_r2 = stats[:, 7:8]
        B.wait("s", *vm)
        B.op("s", lambda s, _a=c_ss: s.activation(out=sq4[:], in_=h1t[:], func=AF.Square, accum_out=_a))
        B.op("s", lambda s, _o=c_ss, _i=c_ss: s.activation(out=_o, in_=_i, func=AF.Sqrt, scale=1.0 / D))
        sm = B.mark("s")
        B.wait("v", *sm)
        B.op("v", lambda v, _o=c_r2, _i=c_ss: v.reciprocal(_o, _i))
        wxm = [(pe, tpose_done_p4[t - 2])] if t >= 2 else []
        if (t - 2) in xm_sp:
            wxm.append((aq, xm_sp[t - 2]))
        B.op("v", lambda v, _o=xmst[t % 2], _r=c_r2:
             v.tensor_scalar(_o, h1t[:], _r, None, ALU.mult), waits=wxm)
        vm2 = B.mark("v")
        xm_sp[t] = B.dma("y", aq, s_xm.ap().rearrange("(t p) d -> t p d", p=128)[t],
                         xmst[t % 2], waits=[vm2])
        tpose.run(xmst[t % 2], lambda dc, _t=t: cT[:, dc * T + _t * 128: dc * T + _t * 128 + 128], vm2,
                  scale_col=lambda dc: S2C[:, dc:dc + 1])
        tpose_done_p4[t] = pe.n
    p4_vmark = B.mark("v")
    p4_out_tok = stq.snapshot()

    if stop_after == "p4":
        return _finish_trunc(B, nc, [aq, wq, stq])
    B.wait_snap("t", wq_const_tok)
    B.wait("t", p4_vmark[0], p4_vmark[1])
    for t in range(NT):
        for dc in range(DC):
            mm(PF[0][:, t * 8:(t + 1) * 8], cT[:, dc * T + t * 128: dc * T + t * 128 + 128],
               rWsb[:, dc * 8:(dc + 1) * 8], start=(dc == 0), stop=False)
        mm(PF[0][:, t * 8:(t + 1) * 8], onesr_bf[:], RRB, False, True,
           incs=([(pe, 1)] if t == NT - 1 else []))
    rmk = pe_inc()
    B.wait("v", *rmk)
    B.op("v", lambda v: v.tensor_copy(logitsA[:], PF[0][:, 0:128]))
    B.op("v", lambda v: v.memset(run_row, 0.0))
    mx = stats[:, 8:9]
    sume = stats[:, 9:10]
    i1c = stats[:, 10:11]
    i2c = stats[:, 11:12]
    rsum = stats[:, 12:13]
    cnt_pe_hist = {}
    for t in range(NT):
        blk = logitsA[:, t * 8:(t + 1) * 8]
        B.op("v", lambda v, _b=blk: v.reduce_max(mx, _b, axis=AX.X))
        B.op("v", lambda v: v.tensor_scalar(mx, mx, -1.0, None, ALU.mult))
        vm = B.mark("v")
        B.wait("s", *vm)
        B.op("s", lambda s, _b=blk: s.activation(out=_b, in_=_b, func=AF.Exp, bias=mx, accum_out=sume))
        sm = B.mark("s")
        B.wait("v", *sm)
        B.op("v", lambda v: v.reciprocal(sume, sume))
        B.op("v", lambda v, _b=blk: v.tensor_scalar(_b, _b, sume, None, ALU.mult))
        m8 = scr8c
        B.op("v", lambda v, _o=m8, _b=blk: v.max(out=_o, in_=_b))
        B.op("v", lambda v, _b=blk, _m=m8: v.tensor_single_scalar(scr8a, _b, _m[:, 0:1], ALU.is_equal))
        B.op("v", lambda v: v.tensor_scalar(scr8a, scr8a, -1000.0, 1000.0, ALU.mult, ALU.add))
        B.op("v", lambda v: v.tensor_add(scr8a, scr8a, IOTA))
        B.op("v", lambda v: v.tensor_reduce(i1c, scr8a, AX.X, ALU.min))
        B.op("v", lambda v, _b=blk, _m=m8: v.tensor_single_scalar(scr8a, _b, _m[:, 1:2], ALU.is_equal))
        B.op("v", lambda v: v.tensor_scalar(scr8a, scr8a, -1000.0, 1000.0, ALU.mult, ALU.add))
        B.op("v", lambda v: v.tensor_add(scr8a, scr8a, IOTA))
        B.op("v", lambda v: v.tensor_single_scalar(scr8b, IOTA, i1c, ALU.is_equal))
        B.op("v", lambda v: v.tensor_scalar(scr8b, scr8b, 2000.0, None, ALU.mult))
        B.op("v", lambda v: v.tensor_add(scr8a, scr8a, scr8b))
        B.op("v", lambda v: v.tensor_reduce(i2c, scr8a, AX.X, ALU.min))
        B.op("v", lambda v, _m=m8: v.tensor_add(rsum, _m[:, 0:1], _m[:, 1:2]))
        B.op("v", lambda v: v.reciprocal(rsum, rsum))
        B.op("v", lambda v, _m=m8, _o=gcols[:, t * 4:t * 4 + 1]:
             v.tensor_scalar(_o, _m[:, 0:1], rsum, None, ALU.mult))
        B.op("v", lambda v, _m=m8, _o=gcols[:, t * 4 + 1:t * 4 + 2]:
             v.tensor_scalar(_o, _m[:, 1:2], rsum, None, ALU.mult))
        B.op("v", lambda v, _o=gcols[:, t * 4 + 2:t * 4 + 3]: v.tensor_copy(_o, i1c))
        B.op("v", lambda v, _o=gcols[:, t * 4 + 3:t * 4 + 4]: v.tensor_copy(_o, i2c))
        B.op("v", lambda v, _b=blk: v.tensor_single_scalar(_b, IOTA, i1c, ALU.is_equal))
        B.op("v", lambda v: v.tensor_single_scalar(scr8b, IOTA, i2c, ALU.is_equal))
        oh_slot = OHBF if t % 2 == 0 else OHBF2
        w_oh = [(pe, cnt_pe_hist[t - 2])] if (t - 2) in cnt_pe_hist else []
        B.op("v", lambda v, _b=blk: v.tensor_add(scr8a, _b, scr8b), waits=w_oh)
        B.op("v", lambda v, _o=oh_slot: v.tensor_copy(_o, scr8a))
        vm2 = B.mark("v")
        B.wait("t", vm2[0], vm2[1])
        mm(PF[1][:, 0:8], LSTRICT, oh_slot, True, True)
        mm(PF[1][:, 16:24], onesr_f[:], run_row, True, True)
        mm(PF[1][0:1, 8:16], ONESCOL, oh_slot, True, True, incs=[(pe, 1)])
        ck = pe_inc()
        cnt_pe_hist[t] = ck[1]
        B.wait("v", *ck)
        B.op("v", lambda v: v.tensor_copy(scr8b, PF[1][:, 16:24]))
        B.op("v", lambda v, _o=cntC[:, t * 8:(t + 1) * 8]: v.tensor_add(_o, PF[1][:, 0:8], scr8b))
        B.op("v", lambda v: v.tensor_add(run_row, run_row, PF[1][0:1, 8:16]))
    vm = B.mark("v")
    B.wait("g", *vm)
    cc_wr = B.dma("g", oq, cc_in.ap(), run_row)
    B.wait("g", oq, cc_wr)

    def do_cc(g):
        g.collective_compute(
            "AllGather", ALU.bypass, replica_groups=[list(range(NCORES))],
            ins=[cc_in.ap()], outs=[cc_out.ap()]).then_inc(cc.h, 1)
    B.ops["g"].append(do_cc)
    cc.n += 1
    B.wait("g", cc, 1)
    cct = B.dma("g", oq, ccsb[:], cc_out.ap())
    B.wait("t", oq, cct)
    B.wait("t", vm[0], vm[1])
    mm(PF[1][0:1, 24:32], mcol[:], ccsb[:], True, True, incs=[(pe, 1)])
    ok = pe_inc()
    B.wait("v", *ok)
    B.op("v", lambda v: v.tensor_scalar(off_row, PF[1][0:1, 24:32], -1.0, float(CAP),
                                        ALU.mult, ALU.add))
    vm = B.mark("v")
    B.wait("t", vm[0], vm[1])
    mm(PF[1][:, 32:40], onesr_f[:], off_row, True, True, incs=[(pe, 1)])
    tk = pe_inc()
    B.wait("v", *tk)
    B.op("v", lambda v: v.tensor_copy(thr_b, PF[1][:, 32:40]))
    B.op("v", lambda v: v.tensor_scalar(krow, off_row, 0.0, None, ALU.max))
    B.op("v", lambda v: v.tensor_tensor(krow, krow, run_row, ALU.min))
    B.op("v", lambda v: v.memset(prow, 0.0))
    for m in range(15):
        B.op("v", lambda v, _m=m: v.tensor_single_scalar(et_row_tmp[:, 0:8], krow,
                                                         float(512 * _m), ALU.is_gt))
        B.op("v", lambda v: v.tensor_add(prow, prow, et_row_tmp[:, 0:8]))
    B.op("v", lambda v: v.tensor_scalar(prow, prow, 512.0, None, ALU.mult))
    B.op("v", lambda v: v.memset(lbase[:, 0:1], 0.0))
    for e in range(1, NE):
        B.op("v", lambda v, _e=e: v.tensor_add(lbase[:, _e:_e + 1], lbase[:, _e - 1:_e],
                                               prow[:, _e - 1:_e]))
    B.op("v", lambda v: v.tensor_add(endr, lbase, prow))
    B.op("v", lambda v: v.memset(et_row[:, 0:16], 0.0))
    for e in range(NE):
        B.op("v", lambda v, _e=e: v.tensor_single_scalar(
            et_row_tmp[:, 0:NSTA], ST512R[:, 0:NSTA], endr[:, _e:_e + 1], ALU.is_ge))
        B.op("v", lambda v: v.tensor_add(et_row[:, 0:NSTA], et_row[:, 0:NSTA],
                                         et_row_tmp[:, 0:NSTA]))
    B.op("v", lambda v: v.tensor_scalar(et_row[:, 0:NSTA], et_row[:, 0:NSTA], 7.0, None, ALU.min))
    vm = B.mark("v")
    B.wait("t", vm[0], vm[1])
    mm(PF[1][:, 40:40 + NSTA], onesr_f[:], et_row[:, 0:NSTA], True, True)
    mm(PF[1][:, 56:64], onesr_f[:], lbase, True, True, incs=[(pe, 1)])
    bk = pe_inc()
    B.wait("v", *bk)
    B.op("v", lambda v: v.tensor_copy(etb[:, 0:NSTA], PF[1][:, 40:40 + NSTA]))
    B.op("v", lambda v: v.tensor_copy(lbb[:], PF[1][:, 56:64]))
    sc_toks_a = []
    for t in range(NT):
        cntt = cntC[:, t * 8:(t + 1) * 8]
        g1 = gcols[:, t * 4:t * 4 + 1]
        g2 = gcols[:, t * 4 + 1:t * 4 + 2]
        i1s = gcols[:, t * 4 + 2:t * 4 + 3]
        i2s = gcols[:, t * 4 + 3:t * 4 + 4]
        B.op("v", lambda v, _i=i1s: v.tensor_single_scalar(scr8a, IOTA, _i, ALU.is_equal))
        B.op("v", lambda v, _i=i2s: v.tensor_single_scalar(scr8b, IOTA, _i, ALU.is_equal))
        B.op("v", lambda v, _c=cntt: v.tensor_tensor(scr8c, _c, thr_b, ALU.is_lt))
        B.op("v", lambda v: v.tensor_mul(scr8c, scr8c, scr8a))
        B.op("v", lambda v, _o=stats[:, 14:15]: v.reduce_sum(_o, scr8c, axis=AX.X))
        B.op("v", lambda v, _c=cntt: v.tensor_add(scr8c, _c, scr8a))
        B.op("v", lambda v: v.tensor_tensor(scr8c, scr8c, thr_b, ALU.is_lt))
        B.op("v", lambda v: v.tensor_mul(scr8c, scr8c, scr8b))
        B.op("v", lambda v, _o=stats[:, 15:16]: v.reduce_sum(_o, scr8c, axis=AX.X))
        B.op("v", lambda v, _c=cntt: v.tensor_add(scr8c, lbb[:], _c))
        B.op("v", lambda v: v.tensor_mul(scr8a, scr8a, scr8c))
        B.op("v", lambda v, _o=stats[:, 16:17]: v.reduce_sum(_o, scr8a, axis=AX.X))
        B.op("v", lambda v: v.tensor_mul(scr8b, scr8b, scr8c))
        B.op("v", lambda v, _o=stats[:, 17:18]: v.reduce_sum(_o, scr8b, axis=AX.X))
        B.op("v", lambda v, _g=g1, _o=gcA[:, 2 * t:2 * t + 1]:
             v.tensor_mul(_o, _g, stats[:, 14:15]))
        B.op("v", lambda v, _g=g2, _o=gcA[:, 2 * t + 1:2 * t + 2]:
             v.tensor_mul(_o, _g, stats[:, 15:16]))
        B.op("v", lambda v: v.tensor_mul(stats[:, 18:19], stats[:, 16:17], stats[:, 14:15]))
        B.op("v", lambda v: v.tensor_mul(stats[:, 19:20], stats[:, 17:18], stats[:, 15:16]))
        B.op("v", lambda v, _o=posA[:, 2 * t:2 * t + 1]: v.tensor_copy(_o, stats[:, 18:19]))
        B.op("v", lambda v, _o=posA[:, 2 * t + 1:2 * t + 2]: v.tensor_copy(_o, stats[:, 19:20]))
        B.op("v", lambda v: v.tensor_scalar(stats[:, 20:21], stats[:, 14:15], -1.0, 1.0, ALU.mult, ALU.add))
        B.op("v", lambda v: v.scalar_tensor_tensor(stats[:, 20:21], stats[:, 20:21], 1e7,
                                                   stats[:, 16:17], ALU.mult, ALU.add))
        B.op("v", lambda v: v.tensor_scalar(stats[:, 21:22], stats[:, 15:16], -1.0, 1.0, ALU.mult, ALU.add))
        B.op("v", lambda v: v.scalar_tensor_tensor(stats[:, 21:22], stats[:, 21:22], 1e7,
                                                   stats[:, 17:18], ALU.mult, ALU.add))
        c0 = 4 * (t % 2)
        wsc = ([(iq, sc_toks_a[2 * (t - 2)]), (iq, sc_toks_a[2 * (t - 2) + 1])]
               if t >= 2 else [])
        B.op("v", lambda v, _o=sct[:, c0:c0 + 1]: v.tensor_copy(_o, stats[:, 20:21]),
             waits=wsc)
        B.op("v", lambda v, _t=t:
             v.tensor_scalar(stats[:, 22:23], iotac[:], float(128 * _t), None, ALU.add))
        B.op("v", lambda v, _o=sct[:, c0 + 1:c0 + 2]: v.tensor_copy(_o, stats[:, 22:23]))
        B.op("v", lambda v, _o=sct[:, c0 + 2:c0 + 3]: v.tensor_copy(_o, stats[:, 21:22]))
        vmk = B.mark("v")
        sc_toks_a.append(B.dma("g", iq, s_atok.ap(), sct[:, c0 + 1:c0 + 2],
                               indirect_out=sct[:, c0:c0 + 1], bounds=rbA,
                               waits=[vmk]))
        sc_toks_a.append(B.dma("g", iq, s_atok.ap(), sct[:, c0 + 1:c0 + 2],
                               indirect_out=sct[:, c0 + 2:c0 + 3], bounds=rbA))
    if debug:
        vmD = B.mark("v")
        B.wait("y", *vmD)
        B.dma("y", aq, d_cnt.ap().rearrange("(t p) e -> p t e", p=128),
              cntC[:].rearrange("p (t e) -> p t e", e=NE))
        B.dma("y", aq, d_rows.ap()[:, 0:64], smalls[0:1, 288:352])
    p5_vmark = B.mark("v")

    if stop_after == "p5":
        return _finish_trunc(B, nc, [aq, wq, stq, oq])
    W1sb = big1[:].bitcast(BF16)
    W2sb = big2[:].bitcast(BF16)
    hTs = work1[:].bitcast(BF16)
    xg = work2[:].bitcast(BF16)
    xgT = cT[:, 0:4096]
    yst2 = [cT[:, 4096 + i * 2048:4096 + (i + 1) * 2048].bitcast(F32) for i in range(2)]
    Ycmb = [cT[:, 8192 + i * 2048:8192 + (i + 1) * 2048].bitcast(F32) for i in range(2)]

    def sparse_moe(NST, sw1, sw2, b1src, b2src, s_x, s_yv, stok, act_fn,
                   sc_toks, first_waits, scale_cols=None):
        st_store = {}
        xfer = {}
        yev = {0: None, 1: None}
        ph2_vmark = {"m": None}

        def part_a(st):
            w_idx = [(iq, tk) for tk in xfer[st - 1]["wall"]] if st >= 1 else []
            B.op("v", lambda v, _st=st: v.scalar_tensor_tensor(
                stats[:, 24:25], etb[:, _st:_st + 1], 128.0, iotac[:], ALU.mult, ALU.add),
                waits=w_idx)
            B.op("v", lambda v: v.tensor_copy(idxw[:, 0:1], stats[:, 24:25]))
            B.op("v", lambda v, _st=st: v.tensor_copy(idxw[:, 1:2], etb[:, _st:_st + 1]))
            idx_vm = B.mark("v")
            sl = 4 * (st % 2)
            if st == 0:
                wtk = list(sc_toks)
            elif st >= 2:
                wtk = [(iq, tk) for tk in xfer[st - 2]["xg"]]
            else:
                wtk = []
            t_ti = B.dma("g", iq, tik[:, sl:sl + 4],
                         stok.ap()[st * 512:(st + 1) * 512].rearrange(
                             "(p j) one -> p (j one)", p=128),
                         waits=wtk)
            xg_toks = []
            for jj in range(4):
                wxg = [(iq, t_ti)]
                if st >= 1:
                    wxg.append((pe, xfer[st - 1]["tp"]))
                else:
                    wxg.extend(first_waits.get("xg", []))
                xg_toks.append(B.dma("g", iq, xg[:, jj * 1024:(jj + 1) * 1024], s_x.ap(),
                                     indirect_in=tik[:, sl + jj:sl + jj + 1], waits=wxg))
            t_w1q = []
            for q in range(4):
                if st == 0:
                    wwq = [idx_vm] + first_waits.get("w1", [])
                else:
                    wwq = [idx_vm, (pe, pe.n + 4 * (q + 1))]
                t_w1q.append(B.dma("g", iq, W1sb[:, q * 8192:(q + 1) * 8192],
                                   sw1, indirect_in=idxw[:, 0:1], waits=wwq,
                                   elem_off=q * 8192))
            xfer[st] = dict(xg=xg_toks, w1=t_w1q, idx_vm=idx_vm, wall=list(t_w1q))

        def part_b(st):
            idx_vm = xfer[st]["idx_vm"]
            gate = first_waits.get("w2", []) if st == 0 else [(pe, pe.n)]
            wb = [idx_vm] + gate
            t_b1 = B.dma("g", iq, b1g[:], b1src, indirect_in=idxw[:, 0:1], waits=wb)
            t_b2 = B.dma("g", iq, smalls[0:2, 640:1152].bitcast(BF16), b2src,
                         indirect_in=idxw[0:2, 1:2], waits=wb)
            t_w2 = [B.dma("g", iq, W2sb[:, part * 8192:(part + 1) * 8192],
                          sw2, indirect_in=idxw[:, 0:1], waits=list(wb),
                          elem_off=part * 8192) for part in range(4)]
            xfer[st]["b1"], xfer[st]["b2"], xfer[st]["w2"] = t_b1, t_b2, t_w2
            xfer[st]["wall"] += t_w2 + [t_b1, t_b2]

        def transposes(st):
            for jj in range(4):
                tpose.run(xg[:, jj * 1024:(jj + 1) * 1024],
                          lambda dc, _j=jj: xgT[:, dc * 512 + _j * 128:
                                               dc * 512 + _j * 128 + 128],
                          (iq, xfer[st]["xg"][jj]),
                          scale_col=(None if scale_cols is None else
                                     (lambda dc: scale_cols[:, dc:dc + 1])))
            xfer[st]["tp"] = pe.n
            xfer[st]["tp_vm"] = B.mark("v")

        part_a(0)
        part_b(0)
        transposes(0)
        for st in range(NST):
            if st + 1 < NST:
                part_a(st + 1)
            t_w1q = xfer[st]["w1"]
            tp_vm = xfer[st]["tp_vm"]
            smk0 = B.mark("s")
            gel_mark = {0: smk0, 1: smk0}
            B.wait("t", iq, t_w1q[0])
            B.wait("t", tp_vm[0], tp_vm[1])
            for hcg in range(16):
                if hcg % 4 == 0 and hcg > 0:
                    B.wait("t", iq, t_w1q[hcg // 4])
                pr2 = hcg % 2
                banks = (PF[0], PF[1]) if pr2 == 0 else (PF[2], PF[3])
                for dc in range(DC):
                    for j in range(2):
                        hc = hcg * 2 + j
                        h, lh = hc // 8, hc % 8
                        w = []
                        if dc == 0 and j == 0:
                            w.append(gel_mark[pr2])
                            if hcg == 1 and ph2_vmark["m"] is not None:
                                w.append(ph2_vmark["m"])
                        mm(banks[j][:],
                           W1sb[:, h * 8192 + dc * 1024 + lh * 128:
                                h * 8192 + dc * 1024 + (lh + 1) * 128],
                           xgT[:, dc * 512: dc * 512 + 512],
                           start=(dc == 0), stop=(dc == DC - 1), waits=w,
                           incs=([(pe, 1)] if (dc == DC - 1 and j == 1) else []))
                mk = pe_inc()
                B.wait("s", *mk)
                if hcg == 0:
                    B.wait("s", iq, xfer[st]["b1"])
                for j in range(2):
                    hc = hcg * 2 + j
                    B.op("s", lambda s, _o=hTs[:, hc * 512:(hc + 1) * 512],
                         _p=banks[j][:], _b=b1g[:, hc:hc + 1], _f=act_fn:
                         s.activation(out=_o, in_=_p, func=_f, bias=_b))
                gel_mark[pr2] = B.mark("s")
            if st + 1 < NST:
                transposes(st + 1)
            smk = B.mark("s")
            B.wait("t", smk[0], smk[1])
            for _tk in xfer[st]["w2"]:
                B.wait("t", iq, _tk)
            B.wait("t", iq, xfer[st]["b2"])
            for tt in range(4):
                ybanks = [PF[4][:], PF[5][:]] if tt % 2 == 0 else [PF[2][:], PF[3][:]]
                for nh in range(2):
                    bank = ybanks[nh]
                    for hc in range(HC):
                        w = []
                        if hc == 0 and nh == 0 and yev[tt % 2] is not None:
                            w.append(yev[tt % 2])
                        mm(bank,
                           hTs[:, hc * 512 + tt * 128: hc * 512 + tt * 128 + 128],
                           W2sb[:, hc * 1024 + nh * 512: hc * 1024 + nh * 512 + 512],
                           start=(hc == 0), stop=False, waits=w)
                    mm(bank, onesr_bf[:], BROW[:, nh * 512:nh * 512 + 512],
                       False, True, incs=([(pe, 1)] if nh == 1 else []))
                yk = pe_inc()
                B.wait("v", *yk)
                ww = []
                prevk = st_store.get((st, tt - 2)) or st_store.get((st - 1, tt + 2))
                if prevk is not None:
                    ww.append((aq, prevk))
                B.op("v", lambda v, _o=yst2[tt % 2][:, 0:512], _p=ybanks[0]:
                     v.tensor_copy(_o, _p), waits=ww)
                B.op("v", lambda v, _o=yst2[tt % 2][:, 512:1024], _p=ybanks[1]:
                     v.tensor_copy(_o, _p))
                yev[tt % 2] = B.mark("v")
                st_store[(st, tt)] = B.dma(
                    "y", aq,
                    s_yv.ap()[st * 512:(st + 1) * 512].rearrange("(p j) d -> j p d", j=4)[tt],
                    yst2[tt % 2][:, 0:1024], waits=[yev[tt % 2]])
            ph2_vmark["m"] = B.mark("v")
            if st + 1 < NST:
                part_b(st + 1)
        return aq.snapshot()

    if debug:
        dbg_toks.append((aq, h1dbg_tok[NT - 1]))
        for tok in dbg_toks:
            B.wait("s", aq, tok[1])
            B.wait("v", aq, tok[1])
    B.wait_snap("g", p4_out_tok)
    p4_aq_snap = aq.snapshot()
    B.wait_snap("g", p4_aq_snap)
    b1sa = ab1p.rearrange("e p c -> (e p) c")
    sw1a = aW1p.rearrange("e p c -> (e p) c")
    sw2a = aW2p.rearrange("e p c -> (e p) c")
    ay_snap = sparse_moe(NSTA, sw1a, sw2a, b1sa, ab2p, s_xm, s_ay, s_atok,
                         AF.Gelu,
                         iq.snapshot(),
                         {"w1": [p4_vmark] + [aq.tok(tk[1]) for tk in dbg_toks],
                          "w2": [(pe, pe.n)],
                          "xg": [(pe, pe.n)]}, scale_cols=S2C)
    moe_pe = (pe, pe.n)

    xst7 = [work2[:, i * 512:(i + 1) * 512].bitcast(BF16) for i in range(2)]
    rt_tok = {}
    rt_pe = {}
    for t in range(2):
        rt_tok[t] = B.dma("y", aq, xst7[t % 2], s_xnT.ap()[t], waits=[moe_pe])
    for t in range(NT):
        B.wait("t", aq, rt_tok[t])
        for dc in range(DC):
            mm(PF[0][:, t * 8:(t + 1) * 8], xst7[t % 2][:, dc * 128:(dc + 1) * 128],
               gWsb[:, dc * 8:(dc + 1) * 8], start=(dc == 0), stop=False)
        mm(PF[0][:, t * 8:(t + 1) * 8], onesr_bf[:], RGB, False, False)
        mm(PF[0][:, t * 8:(t + 1) * 8], onesr_bf[:], REB, False, True,
           incs=[(pe, 1)])
        rt_pe[t] = pe_inc()[1]
        if t + 2 < NT:
            rt_tok[t + 2] = B.dma("y", aq, xst7[t % 2], s_xnT.ap()[t + 2],
                                  waits=[(pe, rt_pe[t])])
    gk = (pe, pe.n)
    B.wait("v", *gk)
    B.op("v", lambda v: v.tensor_copy(logitsA[:], PF[0][:, 0:128]))
    B.op("v", lambda v: v.memset(run_row, 0.0))
    cnt_pe_hist2 = {}
    for t in range(NT):
        blk = logitsA[:, t * 8:(t + 1) * 8]
        B.op("v", lambda v, _b=blk: v.reduce_max(mx, _b, axis=AX.X))
        B.op("v", lambda v, _b=blk: v.tensor_single_scalar(scr8a, _b, mx, ALU.is_equal))
        B.op("v", lambda v: v.tensor_scalar(scr8a, scr8a, -1000.0, 1000.0, ALU.mult, ALU.add))
        B.op("v", lambda v: v.tensor_add(scr8a, scr8a, IOTA))
        B.op("v", lambda v: v.tensor_reduce(i1c, scr8a, AX.X, ALU.min))
        B.op("v", lambda v, _o=ohF[:, t * 8:(t + 1) * 8]:
             v.tensor_single_scalar(_o, IOTA, i1c, ALU.is_equal))
        oh_slot = OHBF if t % 2 == 0 else OHBF2
        w_oh = [(pe, cnt_pe_hist2[t - 2])] if (t - 2) in cnt_pe_hist2 else []
        B.op("v", lambda v, _b=ohF[:, t * 8:(t + 1) * 8], _o=oh_slot:
             v.tensor_copy(_o, _b), waits=w_oh)
        vm2 = B.mark("v")
        B.wait("t", vm2[0], vm2[1])
        mm(PF[1][:, 0:8], LSTRICT, oh_slot, True, True)
        mm(PF[1][:, 16:24], onesr_f[:], run_row, True, True)
        mm(PF[1][0:1, 8:16], ONESCOL, oh_slot, True, True, incs=[(pe, 1)])
        ck = pe_inc()
        cnt_pe_hist2[t] = ck[1]
        B.wait("v", *ck)
        B.op("v", lambda v: v.tensor_copy(scr8b, PF[1][:, 16:24]))
        B.op("v", lambda v, _o=cntC[:, t * 8:(t + 1) * 8]: v.tensor_add(_o, PF[1][:, 0:8], scr8b))
        B.op("v", lambda v: v.tensor_add(run_row, run_row, PF[1][0:1, 8:16]))
    p7_vmark = B.mark("v")
    if debug:
        B.wait("y", *p7_vmark)
        B.dma("y", aq, d_ohf.ap().rearrange("(t p) e -> p t e", p=128),
              ohF[:].rearrange("p (t e) -> p t e", e=NE))
    B.op("v", lambda v: v.tensor_copy(krow, run_row))
    B.op("v", lambda v: v.memset(prow, 0.0))
    for m in range(11):
        B.op("v", lambda v, _m=m: v.tensor_single_scalar(et_row_tmp[:, 0:8], krow,
                                                         float(512 * _m), ALU.is_gt))
        B.op("v", lambda v: v.tensor_add(prow, prow, et_row_tmp[:, 0:8]))
    B.op("v", lambda v: v.tensor_scalar(prow, prow, 512.0, None, ALU.mult))
    B.op("v", lambda v: v.memset(lbase[:, 0:1], 0.0))
    for e in range(1, NE):
        B.op("v", lambda v, _e=e: v.tensor_add(lbase[:, _e:_e + 1], lbase[:, _e - 1:_e],
                                               prow[:, _e - 1:_e]))
    B.op("v", lambda v: v.tensor_add(endr, lbase, prow))
    B.op("v", lambda v: v.memset(et_row[:, 0:16], 0.0))
    for e in range(NE):
        B.op("v", lambda v, _e=e: v.tensor_single_scalar(et_row_tmp[:, 0:NSTF],
             ST512R[:, 0:NSTF], endr[:, _e:_e + 1], ALU.is_ge))
        B.op("v", lambda v: v.tensor_add(et_row[:, 0:NSTF], et_row[:, 0:NSTF],
                                         et_row_tmp[:, 0:NSTF]))
    B.op("v", lambda v: v.tensor_scalar(et_row[:, 0:NSTF], et_row[:, 0:NSTF], 7.0, None, ALU.min))
    vm = B.mark("v")
    B.wait("t", vm[0], vm[1])
    mm(PF[1][:, 40:40 + NSTF], onesr_f[:], et_row[:, 0:NSTF], True, True)
    mm(PF[1][:, 56:64], onesr_f[:], lbase, True, True, incs=[(pe, 1)])
    bk = pe_inc()
    B.wait("v", *bk)
    B.op("v", lambda v: v.tensor_copy(etb[:, 0:NSTF], PF[1][:, 40:40 + NSTF]))
    B.op("v", lambda v: v.tensor_copy(lbb[:], PF[1][:, 56:64]))
    sc_toks_f = []
    for t in range(NT):
        cntt = cntC[:, t * 8:(t + 1) * 8]
        oht = ohF[:, t * 8:(t + 1) * 8]
        B.op("v", lambda v, _c=cntt: v.tensor_add(scr8c, lbb[:], _c))
        B.op("v", lambda v, _oh=oht: v.tensor_mul(scr8c, scr8c, _oh))
        B.op("v", lambda v, _o=stats[:, 16:17]: v.reduce_sum(_o, scr8c, axis=AX.X))
        B.op("v", lambda v, _o=posF[:, t:t + 1]: v.tensor_copy(_o, stats[:, 16:17]))
        c0 = 2 * (t % 4)
        wsc = [(iq, sc_toks_f[t - 4])] if t >= 4 else []
        B.op("v", lambda v, _o=sct[:, c0:c0 + 1]: v.tensor_copy(_o, stats[:, 16:17]),
             waits=wsc)
        B.op("v", lambda v, _t=t:
             v.tensor_scalar(stats[:, 22:23], iotac[:], float(128 * _t), None, ALU.add))
        B.op("v", lambda v, _o=sct[:, c0 + 1:c0 + 2]: v.tensor_copy(_o, stats[:, 22:23]))
        vmk = B.mark("v")
        sc_toks_f.append(B.dma("g", iq, s_ftok.ap(), sct[:, c0 + 1:c0 + 2],
                               indirect_out=sct[:, c0:c0 + 1], bounds=rbF,
                               waits=[vmk]))
    b1sf = fb1p.rearrange("e p c -> (e p) c")
    B.wait_snap("g", ay_snap)
    cmb_tok = {}
    for t in range(NT):
        wY = [(pe, pe.n)] if t == 0 else []
        ty1 = B.dma("g", iq, Ycmb[0][:, 0:1024], s_ay.ap(),
                    indirect_in=posA[:, 2 * t:2 * t + 1],
                    waits=wY + ([(oq, cmb_tok[t - 1])] if t >= 1 else []))
        ty2 = B.dma("g", iq, Ycmb[1][:, 0:1024], s_ay.ap(),
                    indirect_in=posA[:, 2 * t + 1:2 * t + 2], waits=[])
        B.op("v", lambda v, _t=t, _g=gcA[:, 2 * t:2 * t + 1]:
             v.tensor_scalar(Ycmb[0][:, 0:1024], Ycmb[0][:, 0:1024], _g, None, ALU.mult),
             waits=[(iq, ty1), (iq, ty2)])
        B.op("v", lambda v, _g=gcA[:, 2 * t + 1:2 * t + 2]:
             v.scalar_tensor_tensor(Ycmb[0][:, 0:1024], Ycmb[1][:, 0:1024], _g,
                                    Ycmb[0][:, 0:1024], ALU.mult, ALU.add))
        cvm = B.mark("v")
        cmb_tok[t] = B.dma("g", oq, OUT_T[t], Ycmb[0][:, 0:1024], accum=ALU.add,
                           waits=[cvm])
    sw1f = fW1p.rearrange("e p c -> (e p) c")
    sw2f = fW2p.rearrange("e p c -> (e p) c")
    fy_snap = sparse_moe(NSTF, sw1f, sw2f, b1sf, fb2p, s_xn, s_fy, s_ftok,
                         AF.Relu,
                         iq.snapshot(),
                         {"w1": [(pe, pe.n)], "w2": [(pe, pe.n)],
                          "xg": [(pe, pe.n)]})
    B.wait_snap("g", fy_snap)
    for t in range(NT):
        wY = [(pe, pe.n), (oq, cmb_tok[NT - 1])] if t == 0 else []
        if t >= 1:
            wY.append((oq, cmb_tok[("f", t - 1)]))
        ty1 = B.dma("g", iq, Ycmb[0][:, 0:1024], s_fy.ap(),
                    indirect_in=posF[:, t:t + 1], waits=wY)
        cmb_tok[("f", t)] = B.dma("g", oq, OUT_T[t], Ycmb[0][:, 0:1024], accum=ALU.add,
                                  waits=[(iq, ty1)])

    B.wait_snap("y", stq.snapshot())
    B.wait_snap("y", oq.snapshot())
    B.wait_snap("g", oq.snapshot())

    return _emit(B, nc)


def _finish_trunc(B, nc, qs):
    for q in qs:
        B.wait_snap("y", q.snapshot())
        B.wait_snap("g", q.snapshot())
    return _emit(B, nc)


def _emit(B, nc):
    with B.stack:
        with nc.Block() as block:
            @block.tensor
            def _(t_):
                for f in B.ops["t"]:
                    f(t_)

            @block.vector
            def _(v):
                for f in B.ops["v"]:
                    f(v)

            @block.scalar
            def _(s):
                for f in B.ops["s"]:
                    f(s)

            @block.gpsimd
            def _(g):
                for f in B.ops["g"]:
                    f(g)

            @block.sync
            def _(y):
                for f in B.ops["y"]:
                    f(y)
    return B


def kernel(**inputs):
    c = make_constants()
    B = build(debug=False)
    xfull = np.ascontiguousarray(np.asarray(inputs["L0_attn_x"], np.float32))

    def row(v):
        return np.ascontiguousarray(np.asarray(v, np.float32).reshape(1, -1))

    def arr(v):
        return np.ascontiguousarray(np.asarray(v, np.float32))

    shared = dict(
        Wq=arr(inputs["L0_attn_Wq"]), bq=row(inputs["L0_attn_bq"]),
        Wk=arr(inputs["L0_attn_Wk"]), bk=row(inputs["L0_attn_bk"]),
        Wv=arr(inputs["L0_attn_Wv"]), bv=row(inputs["L0_attn_bv"]),
        Wg=arr(inputs["L0_attn_Wg"]), bg=row(inputs["L0_attn_bg"]),
        Wo=arr(inputs["L0_attn_Wo"]), bo=row(inputs["L0_attn_bo"]),
        rW=np.ascontiguousarray(np.asarray(inputs["L0_attn_router_W"], np.float32)
                                .reshape(DC, 128, NE).transpose(1, 0, 2)),
        rb=row(inputs["L0_attn_router_b"]),
        aW1p=np.ascontiguousarray(np.asarray(inputs["L0_attn_W1"], np.float32)
                                  .reshape(NE, DC, 128, 4, HID // 4)
                                  .transpose(0, 2, 3, 1, 4)
                                  .reshape(NE, 128, DC * HID)),
        aW2p=np.ascontiguousarray(np.asarray(inputs["L0_attn_W2"], np.float32)
                                  .reshape(NE, HC, 128, D).transpose(0, 2, 1, 3)
                                  .reshape(NE, 128, HC * D)),
        ab1p=np.ascontiguousarray(np.asarray(inputs["L0_attn_b1"], np.float32)
                                  .reshape(NE, HID // 128, 128).transpose(0, 2, 1)),
        ab2p=arr(inputs["L0_attn_b2"]),
        s1c=np.ascontiguousarray(np.asarray(inputs["L0_attn_norm1_scale"], np.float32).reshape(DC, 128).T),
        s2c=np.ascontiguousarray(np.asarray(inputs["L0_attn_norm2_scale"], np.float32).reshape(DC, 128).T),
        gwT=np.ascontiguousarray(np.asarray(inputs["L0_ffn_gate_w"], np.float32).T
                                 .reshape(DC, 128, NE).transpose(1, 0, 2)),
        gbr=row(inputs["L0_ffn_gate_b"]), ebr=row(inputs["L0_ffn_expert_bias"]),
        fW1p=np.ascontiguousarray(np.asarray(inputs["L0_ffn_w1"], np.float32)
                                  .reshape(NE, DC, 128, 4, HID // 4)
                                  .transpose(0, 2, 3, 1, 4)
                                  .reshape(NE, 128, DC * HID)),
        fW2p=np.ascontiguousarray(np.asarray(inputs["L0_ffn_w2"], np.float32)
                                  .reshape(NE, HC, 128, D).transpose(0, 2, 1, 3)
                                  .reshape(NE, 128, HC * D)),
        fb1p=np.ascontiguousarray(np.asarray(inputs["L0_ffn_b1"], np.float32)
                                  .reshape(NE, HID // 128, 128).transpose(0, 2, 1)),
        fb2p=arr(inputs["L0_ffn_b2"]),
        ident=c["ident"], lstrict=c["lstrict"], ones_bf=c["ones_bf"], ones_f=c["ones_f"],
        onescol_bf=c["onescol_bf"], iota8=c["iota8"], iotac=c["iotac"],
        st512=c["st512"], cosx=c["cosx"], sinx=c["sinx"],
    )
    in_maps = []
    for i in range(NCORES):
        m = dict(shared)
        m["x"] = np.ascontiguousarray(xfull[i])
        mc = np.zeros((NE, 1), np.float32)
        mc[:i] = 1.0
        m["maskcore"] = mc
        in_maps.append(m)
    res = run_bass_kernel_spmd(B.nc, in_maps, core_ids=list(range(NCORES)))
    return np.stack([res.results[i]["out"] for i in range(NCORES)], axis=0)


# version 14 (final)
# speedup vs baseline: 1.2149x; 1.0334x over previous
import numpy as np
import ml_dtypes
from contextlib import ExitStack

import concourse.bass as bass
import concourse.mybir as mybir
from concourse.bass_utils import run_bass_kernel_spmd

F32 = mybir.dt.float32
BF16 = mybir.dt.bfloat16
AF = mybir.ActivationFunctionType
ALU = mybir.AluOpType
AX = mybir.AxisListType

T = 2048
D = 1024
NT = 16
DC = 8
H = 16
DK = 64
CH = 64
NCH = T // CH
NE = 8
HID = 4096
HC = 32
CAP = 5120
NCORES = 8
NSTA = 15
NSTF = 11
EPS_LN = 1e-5
EPS_DEN = 1e-6


class SC:
    def __init__(self, handle):
        self.h = handle
        self.n = 0


class DQ:

    def __init__(self, sems):
        self.sems = sems
        self.k = len(sems)
        self.i = 0

    def tok(self, token):
        j = token // 16 - 1
        return (self.sems[j % self.k], 16 * (j // self.k + 1))

    def snapshot(self):
        return [(s, s.n) for s in self.sems]


class Build:

    def __init__(self, debug=False):
        self.nc = bass.Bass("TRN2", target_bir_lowering=False, debug=False)
        self.debug = debug
        self.ops = {k: [] for k in ("t", "v", "s", "g", "y")}
        self.auto = {}
        self.cnt = {"v": 0, "s": 0, "g": 0}
        self.stack = ExitStack()
        self.dbg = {}

    def op(self, eng, build_fn, waits=(), incs=()):
        my_idx = self.cnt[eng]
        self.cnt[eng] = my_idx + 1
        waits, incs = tuple(self.norm_waits(waits)), tuple(incs)
        a = self.auto[eng]

        def f(e, _w=waits, _i=incs, _k=my_idx, _fn=build_fn, _a=a):
            for sc, val in _w:
                if val > 0:
                    e.wait_ge(sc.h, val)
            if _k > 0:
                e.wait_ge(_a.h, _k)
            inst = _fn(e)
            inst.then_inc(_a.h, 1)
            for sc, k in _i:
                inst.then_inc(sc.h, k)

        self.ops[eng].append(f)

    def raw(self, eng, build_fn, waits=(), incs=()):
        waits, incs = tuple(self.norm_waits(waits)), tuple(incs)

        def f(e, _w=waits, _i=incs, _fn=build_fn):
            for sc, val in _w:
                if val > 0:
                    e.wait_ge(sc.h, val)
            inst = _fn(e)
            if inst is not None:
                for sc, k in _i:
                    inst.then_inc(sc.h, k)

        self.ops[eng].append(f)

    def wait(self, eng, sc, val=None):
        if isinstance(sc, DQ):
            assert val is not None
            if isinstance(val, list):
                for s, v in val:
                    self.wait(eng, s, v)
                return
            sc, val = sc.tok(val)
        v = sc.n if val is None else val
        if v > 0:
            def f(e, _s=sc, _v=v):
                e.wait_ge(_s.h, _v)
            self.ops[eng].append(f)

    def wait_snap(self, eng, snap):
        for s, v in snap:
            self.wait(eng, s, v)

    def mark(self, eng):
        return (self.auto[eng], self.cnt[eng])

    def norm_waits(self, waits):
        out = []
        for w in waits:
            if isinstance(w[0], DQ):
                out.append(w[0].tok(w[1]))
            else:
                out.append(w)
        return out

    def dma(self, eng, q, out, in_, accum=None, waits=(), indirect_out=None,
            indirect_in=None, bounds=None, elem_off=0):
        kw = {}
        if accum is not None:
            kw["accum_op"] = accum
        waits = tuple(self.norm_waits(waits))
        j = q.i
        q.i += 1
        s = q.sems[j % q.k]
        gate = s.n
        s.n += 16
        ind = (indirect_out is not None) or (indirect_in is not None)

        def f(e, _w=waits, _o=out, _i=in_, _kw=kw, _s=s, _g=gate,
              _io=indirect_out, _ii=indirect_in, _b=bounds, _eo=elem_off):
            for sc, val in _w:
                if val > 0:
                    e.wait_ge(sc.h, val)
            if _g > 0:
                e.wait_ge(_s.h, _g)
            if _io is not None or _ii is not None:
                ikw = {}
                if _eo:
                    ikw["element_offset"] = _eo
                if _b is not None:
                    ikw["bounds_check"] = _b
                    ikw["oob_is_err"] = False
                e.indirect_dma_start(
                    out=_o,
                    out_offset=(bass.IndirectOffsetOnAxis(_io, 0) if _io is not None else None),
                    in_=_i,
                    in_offset=(bass.IndirectOffsetOnAxis(_ii, 0) if _ii is not None else None),
                    **ikw).then_inc(_s.h, 16)
            else:
                e.dma_start(out=_o, in_=_i, **_kw).then_inc(_s.h, 16)

        self.ops[eng].append(f)
        return 16 * (j + 1)

    def scratch(self, name, shape, dtype):
        kind = "ExternalOutput" if self.debug else "Internal"
        t = self.nc.dram_tensor(name, list(shape), dtype, kind=kind)
        if self.debug:
            self.dbg[name] = t
        return t


def _bf(x):
    return np.asarray(x, dtype=ml_dtypes.bfloat16)


def make_constants():
    c = {}
    c["ident"] = _bf(np.eye(128, dtype=np.float32))
    c["lstrict"] = _bf(np.tril(np.ones((128, 128), np.float32), -1))
    c["ones_bf"] = _bf(np.ones((1, 128), np.float32))
    c["ones_f"] = np.ones((1, 128), np.float32)
    c["onescol_bf"] = _bf(np.ones((128, 1), np.float32))
    c["iota8"] = np.tile(np.arange(8, dtype=np.float32), (128, 1))
    c["iotac"] = np.arange(128, dtype=np.float32).reshape(128, 1)
    c["st512"] = (np.arange(16, dtype=np.float32) * 512).reshape(1, 16)
    inv = (1.0 / 10000.0 ** (np.arange(32, dtype=np.float32) / 32.0)).astype(np.float32)
    ang = np.arange(T, dtype=np.float32)[:, None] * inv[None, :]
    cosv = np.cos(ang).astype(np.float32)
    sinv = np.sin(ang).astype(np.float32)
    cosx = np.concatenate([cosv, cosv], axis=1)
    sinx = np.concatenate([-sinv, sinv], axis=1)
    c["cosx"] = np.ascontiguousarray(cosx.reshape(NT, 128, 64).transpose(1, 0, 2))
    c["sinx"] = np.ascontiguousarray(sinx.reshape(NT, 128, 64).transpose(1, 0, 2))
    return c


def build(debug=False, stop_after="all", p3_mode=3):
    B = Build(debug=debug)
    nc = B.nc
    E = B.stack.enter_context

    def din(name, shape, dtype=F32):
        return nc.dram_tensor(name, list(shape), dtype, kind="ExternalInput").ap()

    x_in = din("x", (T, D))
    Wq, bq = din("Wq", (D, D)), din("bq", (1, D))
    Wk, bk = din("Wk", (D, D)), din("bk", (1, D))
    Wv, bv = din("Wv", (D, D)), din("bv", (1, D))
    Wg, bg = din("Wg", (D, D)), din("bg", (1, D))
    Wo, bo = din("Wo", (D, D)), din("bo", (1, D))
    rW, rb = din("rW", (128, DC, NE)), din("rb", (1, NE))
    aW1p = din("aW1p", (NE, 128, DC * HID))
    aW2p = din("aW2p", (NE, 128, HC * D))
    ab1p = din("ab1p", (NE, 128, HID // 128))
    ab2p = din("ab2p", (NE, D))
    s1c, s2c = din("s1c", (128, DC)), din("s2c", (128, DC))
    gwT = din("gwT", (128, DC, NE))
    gbr, ebr = din("gbr", (1, NE)), din("ebr", (1, NE))
    fW1p = din("fW1p", (NE, 128, DC * HID))
    fW2p = din("fW2p", (NE, 128, HC * D))
    fb1p = din("fb1p", (NE, 128, HID // 128))
    fb2p = din("fb2p", (NE, D))
    ident_i = din("ident", (128, 128), BF16)
    lstrict_i = din("lstrict", (128, 128), BF16)
    ones_bf_i = din("ones_bf", (1, 128), BF16)
    ones_f_i = din("ones_f", (1, 128))
    onescol_i = din("onescol_bf", (128, 1), BF16)
    iota8_i = din("iota8", (128, NE))
    iotac_i = din("iotac", (128, 1))
    st512_i = din("st512", (1, 16))
    cosx_i = din("cosx", (128, NT, 64))
    sinx_i = din("sinx", (128, NT, 64))
    maskcore = din("maskcore", (NE, 1))

    OUT = nc.dram_tensor("out", [T, D], F32, kind="ExternalOutput").ap()
    OUT_T = OUT.rearrange("(t p) d -> t p d", p=128)
    X_T = x_in.rearrange("(t p) d -> t p d", p=128)

    s_qT = B.scratch("s_qT", (NT, 128, D), BF16)
    s_k = B.scratch("s_k", (NT, 128, D), BF16)
    s_v = B.scratch("s_v", (NT, 128, H * 65), BF16)
    s_g = B.scratch("s_g", (NT, 128, D), F32)
    s_ao = B.scratch("s_ao", (NT, 128, D), BF16)
    s_xnT = B.scratch("s_xnT", (NT, 128, D), BF16)
    s_xm = B.scratch("s_xm", (T, D), BF16)
    s_xn = B.scratch("s_xn", (T, D), BF16)
    s_atok = B.scratch("s_atok", (NSTA * 512, 1), mybir.dt.int32)
    s_ftok = B.scratch("s_ftok", (NSTF * 512, 1), mybir.dt.int32)
    s_ay = B.scratch("s_ay", (NSTA * 512, D), F32)
    s_fy = B.scratch("s_fy", (NSTF * 512, D), F32)
    cc_in = nc.dram_tensor("cc_in", [1, NE], F32)
    cc_out = nc.dram_tensor("cc_out", [NCORES, NE], F32)
    if debug:
        d_xn = B.scratch("d_xn", (T, D), F32)
        d_cnt = B.scratch("d_cnt", (T, NE), F32)
        d_ohf = B.scratch("d_ohf", (T, NE), F32)
        d_h1 = B.scratch("d_h1", (T, D), F32)
        d_rows = B.scratch("d_rows", (1, 128), F32)

    cT = E(nc.sbuf_tensor("cT", [128, DC * T], BF16))
    big1 = E(nc.sbuf_tensor("big1", [128, 16384], F32))
    big2 = E(nc.sbuf_tensor("big2", [128, 16384], F32))
    work1 = E(nc.sbuf_tensor("work1", [128, 8192], F32))
    work2 = E(nc.sbuf_tensor("work2", [128, 2048], F32))
    smalls = E(nc.sbuf_tensor("smalls", [128, 1184], F32))
    consts = E(nc.sbuf_tensor("consts", [128, 24], F32))
    cbf = E(nc.sbuf_tensor("cbf", [128, 400], BF16))
    onesr_bf = E(nc.sbuf_tensor("onesr_bf", [1, 128], BF16))
    onesr_f = E(nc.sbuf_tensor("onesr_f", [1, 128], F32))
    bcol = E(nc.sbuf_tensor("bcol", [128, NE * 32], F32))
    mcol = E(nc.sbuf_tensor("mcol", [NE, 1], F32))
    iotac = E(nc.sbuf_tensor("iotac_sb", [128, 1], F32))
    posA = E(nc.sbuf_tensor("posA", [128, 32], mybir.dt.int32))
    posF = E(nc.sbuf_tensor("posF", [128, 16], mybir.dt.int32))
    idxw = E(nc.sbuf_tensor("idxw", [128, 4], mybir.dt.int32))
    tik = E(nc.sbuf_tensor("tik", [128, 8], mybir.dt.int32))
    sct = E(nc.sbuf_tensor("sct", [128, 8], mybir.dt.int32))
    b1g = E(nc.sbuf_tensor("b1g", [128, 32], F32))
    etb = E(nc.sbuf_tensor("etb", [128, 16], F32))
    lbb = E(nc.sbuf_tensor("lbb", [128, NE], F32))
    ccsb = E(nc.sbuf_tensor("ccsb", [NCORES, NE], F32))


    PF = [E(nc.psum_tensor(f"pf{i}", [128, 512], F32)) for i in range(6)]
    PT = [E(nc.psum_tensor(f"pt{i}", [128, 1024], BF16)) for i in range(2)]

    B.auto["v"] = SC(E(nc.semaphore("auto_v")))
    B.auto["s"] = SC(E(nc.semaphore("auto_s")))
    B.auto["g"] = SC(E(nc.semaphore("auto_g")))
    pe = SC(E(nc.semaphore("pe")))
    wq = DQ([SC(E(nc.semaphore(f"wq{i}"))) for i in range(4)])
    aq = DQ([SC(E(nc.semaphore(f"aq{i}"))) for i in range(8)])
    oq = DQ([SC(E(nc.semaphore(f"oq{i}"))) for i in range(4)])
    iq = DQ([SC(E(nc.semaphore(f"iq{i}"))) for i in range(6)])
    rbA = E(nc.gpsimd.register("rbA"))
    rbF = E(nc.gpsimd.register("rbF"))
    B.raw("g", lambda g: g.reg_mov(rbA, NSTA * 512 - 1) and None)
    B.raw("g", lambda g: g.reg_mov(rbF, NSTF * 512 - 1) and None)
    stq = DQ([SC(E(nc.semaphore(f"stq{i}"))) for i in range(4)])
    cc = SC(E(nc.semaphore("cc")))

    def mm(out_ap, lhsT, rhs, start, stop, waits=(), incs=()):
        def f(t_, _o=out_ap, _l=lhsT, _r=rhs, _s=start, _p=stop):
            return t_.matmul(_o, _l, _r, start=_s, stop=_p, skip_group_check=True)
        B.raw("t", f, waits=waits, incs=incs)

    def pe_inc():
        pe.n += 1
        return (pe, pe.n)

    IDENT = cbf[:, 0:128]
    LSTRICT = cbf[:, 128:256]
    ONESCOL = cbf[:, 256:257]
    rWsb = cbf[:, 260:324]
    gWsb = cbf[:, 324:388]
    OHBF = cbf[:, 388:396]
    OHBF2 = smalls[:, 400:404].bitcast(BF16)

    def transp(out_ap, in_ap, waits=(), incs=()):
        def f(t_, _o=out_ap, _i=in_ap):
            return t_.transpose(_o, _i, IDENT)
        B.raw("t", f, waits=waits, incs=incs)

    class TT:

        def __init__(self):
            self.bank_free = [None, None]

        def run(self, src_tile, dst_fn, in_mark, extra_first_wait=(), scale_col=None):
            for dc in range(DC):
                bk = PT[dc % 2]
                w = []
                if dc == 0 and in_mark is not None:
                    w.append(in_mark)
                if self.bank_free[dc % 2] is not None:
                    w.append(self.bank_free[dc % 2])
                transp(bk[:, 0:128], src_tile[:, dc * 128:(dc + 1) * 128],
                       waits=w, incs=[(pe, 1)])
                m = pe_inc()
                B.wait("v", *m)
                ww = list(extra_first_wait) if dc == 0 else []
                if scale_col is None:
                    B.op("v", lambda v, _o=dst_fn(dc), _i=bk[:, 0:128]:
                         v.tensor_copy(_o, _i), waits=ww)
                else:
                    B.op("v", lambda v, _o=dst_fn(dc), _i=bk[:, 0:128],
                         _s=scale_col(dc):
                         v.tensor_scalar(_o, _i, _s, None, ALU.mult), waits=ww)
                self.bank_free[dc % 2] = B.mark("v")

    tpose = TT()

    B.dma("y", aq, onesr_bf[:], ones_bf_i)
    B.dma("y", aq, onesr_f[:], ones_f_i)
    B.dma("y", aq, mcol[:], maskcore)
    IOTA = consts[:, 0:8]
    S1C = consts[:, 8:16]
    S2C = consts[:, 16:24]
    B.dma("y", aq, IOTA, iota8_i)
    B.dma("y", aq, iotac[:], iotac_i)
    B.dma("y", aq, S1C, s1c)
    B.dma("y", aq, S2C, s2c)
    B.dma("y", aq, IDENT, ident_i)
    B.dma("y", aq, LSTRICT, lstrict_i)
    B.dma("y", aq, ONESCOL, onescol_i)
    BROW = smalls[0:1, 640:1152].bitcast(BF16)
    RRB = smalls[0:1, 1168:1172].bitcast(BF16)
    RGB = smalls[0:1, 1172:1176].bitcast(BF16)
    REB = smalls[0:1, 1176:1180].bitcast(BF16)
    for dst, src_ in ((RRB, rb), (RGB, gbr), (REB, ebr)):
        B.dma("g", wq, dst, src_)

    COS = work1[:, 0:NT * 64]
    SIN = work1[:, NT * 64:2 * NT * 64]
    B.dma("y", aq, COS, cosx_i.rearrange("p t c -> p (t c)"))
    B.dma("y", aq, SIN, sinx_i.rearrange("p t c -> p (t c)"))
    consts_tok = aq.snapshot()
    B.dma("g", wq, rWsb.rearrange("p (c e) -> p c e", e=NE), rW)
    B.dma("g", wq, gWsb.rearrange("p (c e) -> p c e", e=NE), gwT)
    wq_const_tok = wq.snapshot()

    stats = smalls[:, 0:32]
    logitsA = smalls[:, 32:160]
    cntC = smalls[:, 160:288]
    wgA = smalls[:, 288:416]
    ohF = smalls[:, 416:544]
    gcols = smalls[:, 544:608]
    scr8a = smalls[:, 608:616]
    scr8b = smalls[:, 616:624]
    scr8c = smalls[:, 624:632]
    thr_b = smalls[:, 632:640]
    run_row = smalls[0:1, 1152:1160]
    off_row = smalls[0:1, 1160:1168]
    ST512R = smalls[0:1, 288:304]
    krow = smalls[0:1, 304:312]
    prow = smalls[0:1, 312:320]
    lbase = smalls[0:1, 320:328]
    endr = smalls[0:1, 328:336]
    et_row = smalls[0:1, 336:352]
    et_row_tmp = smalls[0:1, 384:400]
    gcA = smalls[:, 352:384]
    B.dma("y", aq, ST512R, st512_i)
    B.op("v", lambda v: v.memset(posA[:], 0))
    zvm = B.mark("v")
    for tgt, nst in ((s_atok, NSTA), (s_ftok, NSTF)):
        cols = nst * 4
        done = 0
        while done < cols:
            cw = min(32, cols - done)
            B.dma("y", aq, tgt.ap()[done * 128:(done + cw) * 128].rearrange(
                "(p c) one -> p (c one)", p=128), posA[:, 0:cw], waits=[zvm])
            done += cw


    xn = big1[:]
    xstage = [work1[:, 2048 + i * 1024:2048 + (i + 1) * 1024] for i in range(2)]
    sq = work1[:, 4096:4096 + 1024]
    x_tok = [None] * NT
    cons_mark = {}
    B.wait_snap("v", consts_tok)
    B.wait_snap("s", consts_tok)
    eps_col = stats[:, 30:31]
    B.op("v", lambda v: v.memset(eps_col, EPS_LN))
    for t in range(NT):
        w = []
        if t >= 2:
            w.extend(cons_mark[t - 2])
        x_tok[t] = B.dma("y", aq, xstage[t % 2], X_T[t], waits=w)
        xt = xstage[t % 2]
        xnt = xn[:, t * 1024:(t + 1) * 1024]
        c_s1 = stats[:, 0:1]
        c_ss = stats[:, 1:2]
        c_m = stats[:, 2:3]
        c_r = stats[:, 3:4]
        B.wait("v", aq, x_tok[t])
        B.wait("s", aq, x_tok[t])
        prev_vm0 = B.mark("v")
        B.op("v", lambda v, _x=xt, _o=c_s1: v.reduce_sum(_o, _x, axis=AX.X))
        B.op("s", lambda s, _x=xt, _a=c_ss: s.activation(
            out=sq[:], in_=_x, func=AF.Square, accum_out=_a), waits=[prev_vm0])
        B.op("v", lambda v, _o=c_m, _i=c_s1: v.tensor_scalar(_o, _i, 1.0 / D, None, ALU.mult))
        sm = B.mark("s")
        B.wait("v", *sm)
        B.op("v", lambda v, _o=c_s1, _m=c_m: v.tensor_mul(_o, _m, _m))
        B.op("v", lambda v, _o=c_ss, _i=c_ss: v.tensor_scalar(_o, _i, 1.0 / D, None, ALU.mult))
        B.op("v", lambda v, _o=c_ss, _a=c_ss, _b=c_s1: v.tensor_sub(_o, _a, _b))
        vm = B.mark("v")
        B.wait("s", *vm)
        B.op("s", lambda s, _o=c_ss, _i=c_ss: s.activation(out=_o, in_=_i, func=AF.Sqrt, bias=eps_col))
        sm = B.mark("s")
        B.wait("v", *sm)
        B.op("v", lambda v, _o=c_r, _i=c_ss: v.reciprocal(_o, _i))
        B.op("v", lambda v, _o=xnt, _x=xt, _m=c_m, _r=c_r:
             v.tensor_scalar(_o, _x, _m, _r, ALU.subtract, ALU.mult))
        cons_mark[t] = [B.mark("v"), B.mark("s")]
    dbg_toks = []
    if debug:
        vm = B.mark("v")
        B.wait("y", *vm)
        dbg_toks.append((aq, B.dma("y", aq, d_xn.ap().rearrange("(t p) d -> p t d", p=128),
              xn.rearrange("p (t d) -> p t d", d=D))))

    if stop_after == "p0":
        return _finish_trunc(B, nc, [aq, wq])
    B.wait_snap("t", consts_tok)

    abf = [work1[:, 5120 + i * 512:5120 + (i + 1) * 512].bitcast(BF16) for i in range(2)]
    xnbf = [work1[:, 6144 + i * 512:6144 + (i + 1) * 512].bitcast(BF16) for i in range(2)]
    tsg = [work1[:, 7168 + i * 512:7168 + (i + 1) * 512].bitcast(BF16) for i in range(2)]
    xnT_spill = {}
    xn_sp = {}
    tpose_done_pe = {}
    for t in range(NT):
        xnt = xn[:, t * 1024:(t + 1) * 1024]
        c_ss = stats[:, 4:5]
        c_r1 = stats[:, 5:6]
        a_t = abf[t % 2]
        xn_t = xnbf[t % 2]
        ts_t = tsg[t % 2]
        prev_vm1 = B.mark("v")
        B.op("s", lambda s, _x=xnt, _a=c_ss: s.activation(
            out=sq[:], in_=_x, func=AF.Square, accum_out=_a), waits=[prev_vm1])
        B.op("s", lambda s, _o=c_ss, _i=c_ss: s.activation(out=_o, in_=_i, func=AF.Sqrt, scale=1.0 / D))
        sm = B.mark("s")
        B.wait("v", *sm)
        B.op("v", lambda v, _o=c_r1, _i=c_ss: v.reciprocal(_o, _i))
        wv = [(pe, tpose_done_pe[t - 2])] if t >= 2 else []
        B.op("v", lambda v, _o=a_t, _x=xnt, _r=c_r1:
             v.tensor_scalar(_o, _x, _r, None, ALU.mult), waits=wv)
        wxn = [(aq, xn_sp[t - 2])] if (t - 2) in xn_sp else []
        B.op("v", lambda v, _o=xn_t, _x=xnt: v.tensor_copy(_o, _x), waits=wxn)
        vm = B.mark("v")
        tpose.run(a_t, lambda dc, _t=t: cT[:, dc * T + _t * 128: dc * T + _t * 128 + 128], vm,
                  scale_col=lambda dc: S1C[:, dc:dc + 1])
        war = [(aq, xnT_spill[t - 2])] if t >= 2 else []
        tpose.run(xn_t, lambda dc, _ts=ts_t: _ts[:, dc * 128:(dc + 1) * 128], None,
                  extra_first_wait=war)
        tpose_done_pe[t] = pe.n
        vm2 = B.mark("v")
        B.wait("y", *vm2)
        xnT_spill[t] = B.dma("y", aq, s_xnT.ap()[t], ts_t)
        xn_sp[t] = B.dma("y", aq, s_xn.ap().rearrange("(t p) d -> t p d", p=128)[t], xn_t,
                         waits=[vm2])
    p1_vmark = B.mark("v")

    if stop_after == "p1":
        return _finish_trunc(B, nc, [aq, wq])
    wbuf = big2[:].bitcast(BF16)
    qkst = [work1[:, 2048 + i * 512:2048 + (i + 1) * 512].bitcast(BF16) for i in range(2)]
    tmp1 = work1[:, 3072:3584]
    tmp2 = work1[:, 3584:4096]
    tmp3 = work1[:, 4096:4608]
    v65 = [work1[:, 4608 + i * 520:4608 + (i + 1) * 520].bitcast(BF16) for i in range(2)]
    qTst = [work1[:, 5648 + i * 512:5648 + (i + 1) * 512].bitcast(BF16) for i in range(2)]
    g32 = [work2[:, i * 1024:(i + 1) * 1024] for i in range(2)]

    p1_aq = aq.snapshot()
    B.wait_snap("v", p1_aq)
    B.wait_snap("s", p1_aq)
    B.op("v", lambda v: v.memset(
        v65[0].rearrange("p (h c) -> p h c", c=65)[:, :, 64:65], 1.0))
    B.op("v", lambda v: v.memset(
        v65[1].rearrange("p (h c) -> p h c", c=65)[:, :, 64:65], 1.0))

    PROJS = [("q", Wq, bq, s_qT), ("k", Wk, bk, s_k), ("v", Wv, bv, s_v), ("g", Wg, bg, s_g)]
    spill_tok = {}
    stage_cons_pe = {}
    bank_cons = {0: p1_vmark, 1: p1_vmark}
    wslot_last_pe = {0: None, 1: None}
    for pi, (pname, Wsrc, brow, sdst) in enumerate(PROJS):
        wslot = wbuf[:, (pi % 2) * 8192:(pi % 2) * 8192 + 8192]
        wwar = [(pe, wslot_last_pe[pi % 2])] if wslot_last_pe[pi % 2] else []
        wtok = B.dma("g", wq, wslot.rearrange("p (c n) -> p c n", n=D),
                     Wsrc.rearrange("(c p) n -> p c n", p=128), waits=wwar)
        btok = B.dma("g", wq, BROW, brow, waits=[(pe, pe.n)])
        B.wait("t", wq, btok)
        B.wait("t", wq, wtok)
        if pi == 0:
            B.wait("t", p1_vmark[0], p1_vmark[1])
        for t in range(NT):
            for nh in range(2):
                gidx = t * 2 + nh
                bank = PF[gidx % 2]
                for dc in range(DC):
                    w = []
                    if dc == 0:
                        w.append(bank_cons[gidx % 2])
                    mm(bank[:], cT[:, dc * T + t * 128: dc * T + t * 128 + 128],
                       wslot[:, dc * 1024 + nh * 512: dc * 1024 + nh * 512 + 512],
                       start=(dc == 0), stop=False, waits=w)
                mm(bank[:], onesr_bf[:], BROW[:, nh * 512:nh * 512 + 512], False, True,
                   incs=[(pe, 1)])
                mmk = pe_inc()
                if pname in ("q", "k"):
                    stg = qkst[t % 2]
                    pv = bank[:].rearrange("p (h c) -> p h c", c=64)
                    cb = COS[:, t * 64:(t + 1) * 64].unsqueeze(1).broadcast_to([128, 8, 64])
                    sb0 = SIN[:, t * 64:t * 64 + 32].unsqueeze(1).broadcast_to([128, 8, 32])
                    sb1 = SIN[:, t * 64 + 32:t * 64 + 64].unsqueeze(1).broadcast_to([128, 8, 32])
                    t1v = tmp1[:].rearrange("p (h c) -> p h c", c=64)
                    t2v = tmp2[:].rearrange("p (h c) -> p h c", c=64)
                    B.wait("v", *mmk)
                    B.op("v", lambda v, _p=pv, _c=cb, _o=t1v: v.tensor_mul(_o, _p, _c))
                    B.op("v", lambda v, _p=pv, _s=sb0, _o=t2v:
                         v.tensor_mul(_o[:, :, 0:32], _p[:, :, 32:64], _s))
                    B.op("v", lambda v, _p=pv, _s=sb1, _o=t2v:
                         v.tensor_mul(_o[:, :, 32:64], _p[:, :, 0:32], _s))
                    bank_cons[gidx % 2] = B.mark("v")
                    B.op("v", lambda v: v.tensor_add(tmp1[:], tmp1[:], tmp2[:]))
                    B.op("v", lambda v: v.tensor_scalar(tmp2[:], tmp1[:], 0.0, None, ALU.min))
                    vm = B.mark("v")
                    B.wait("s", *vm)
                    B.op("s", lambda s: s.activation(out=tmp3[:], in_=tmp2[:], func=AF.Exp))
                    sm = B.mark("s")
                    B.wait("v", *sm)
                    ww = []
                    if pname == "q" and (pi, t - 2) in stage_cons_pe:
                        ww.append((pe, stage_cons_pe[(pi, t - 2)]))
                    if pname == "k" and (pi, t - 2) in spill_tok:
                        ww.append((aq, spill_tok[(pi, t - 2)]))
                    B.op("v", lambda v, _o=stg[:, nh * 512:nh * 512 + 512]:
                         v.scalar_tensor_tensor(_o, tmp1[:], 0.0, tmp3[:], ALU.max, ALU.add),
                         waits=ww)
                elif pname == "v":
                    vst = v65[t % 2]
                    ww = [(aq, spill_tok[(pi, t - 2)])] if (pi, t - 2) in spill_tok else []
                    B.wait("v", *mmk)
                    B.op("v", lambda v,
                         _o=vst.rearrange("p (h c) -> p h c", c=65)[:, nh * 8:nh * 8 + 8, 0:64],
                         _i=bank[:].rearrange("p (h c) -> p h c", c=64):
                         v.tensor_copy(_o, _i), waits=ww)
                    bank_cons[gidx % 2] = B.mark("v")
                elif pname == "g":
                    gst = g32[t % 2]
                    ww = [(aq, spill_tok[(pi, t - 2)])] if (pi, t - 2) in spill_tok else []
                    B.wait("s", *mmk)
                    B.op("s", lambda s, _o=gst[:, nh * 512:nh * 512 + 512], _i=bank[:]:
                         s.activation(out=_o, in_=_i, func=AF.Sigmoid), waits=ww)
                    bank_cons[gidx % 2] = B.mark("s")
            if pname == "q":
                stg = qkst[t % 2]
                qts = qTst[t % 2]
                vm = B.mark("v")
                war = [(aq, spill_tok[(pi, t - 2)])] if (pi, t - 2) in spill_tok else []
                tpose.run(stg, lambda dc, _q=qts: _q[:, dc * 128:(dc + 1) * 128], vm,
                          extra_first_wait=war)
                stage_cons_pe[(pi, t)] = pe.n
                vm2 = B.mark("v")
                B.wait("y", *vm2)
                spill_tok[(pi, t)] = B.dma("y", aq, sdst.ap()[t], qts)
            elif pname == "k":
                vm = B.mark("v")
                B.wait("y", *vm)
                spill_tok[(pi, t)] = B.dma("y", aq, sdst.ap()[t], qkst[t % 2])
            elif pname == "v":
                vm = B.mark("v")
                B.wait("y", *vm)
                spill_tok[(pi, t)] = B.dma("y", aq, sdst.ap()[t], v65[t % 2])
            elif pname == "g":
                sm = B.mark("s")
                B.wait("y", *sm)
                spill_tok[(pi, t)] = B.dma("y", aq, sdst.ap()[t], g32[t % 2])
        wslot_last_pe[pi % 2] = pe.n
    p2_aq = aq.snapshot()
    p2_vmark = B.mark("v")
    p2_smark = B.mark("s")

    if stop_after == "p2":
        return _finish_trunc(B, nc, [aq, wq])
    wo_slot = wbuf[:, 0:8192]
    wo_tok = B.dma("g", wq, wo_slot.rearrange("p (c n) -> p c n", n=D),
                   Wo.rearrange("(c p) n -> p c n", p=128),
                   waits=[(pe, pe.n)])
    wob_tok = B.dma("g", wq, BROW, bo, waits=[(pe, pe.n)])

    off = [0]

    def aw1(ncols):
        ap = work1[:, off[0]:off[0] + ncols]
        off[0] += ncols
        return ap

    qT_b = [aw1(512).bitcast(BF16) for _ in range(2)]
    k_b = [aw1(512).bitcast(BF16) for _ in range(2)]
    v_b = [aw1(520).bitcast(BF16) for _ in range(2)]
    g_b = [aw1(1024) for _ in range(2)]
    szsb = aw1(260).bitcast(BF16)
    s_acc = aw1(520)
    rden = aw1(16)
    ptmp = aw1(256)
    ao_b = [aw1(512).bitcast(BF16) for _ in range(2)]
    assert off[0] <= 8192, off[0]

    def S_ap(h):
        return PF[2 + h // 8][(h % 2) * 64:(h % 2) * 64 + 64,
                              ((h // 2) % 4) * 65:((h // 2) % 4) * 65 + 65]

    def SZ_ap(h):
        p0 = (h % 2) * 64
        c0 = (h // 8) * 260 + ((h // 2) % 4) * 65
        return szsb[p0:p0 + 64, c0:c0 + 65]

    NUMB = [PF[0], PF[1], PF[4], PF[5]]

    def num_ap(h, c):
        return NUMB[h // 4][(c % 2) * 64:(c % 2) * 64 + 64,
                            (h % 4) * 65:(h % 4) * 65 + 65]

    rl_tok = {}
    scan_done = {}

    def issue_reload(t):
        w = []
        if t >= 2 and (t - 2) in scan_done:
            w = [(pe, scan_done[t - 2][0]), scan_done[t - 2][1]]
            if (t - 2) in ao_spill:
                w.append((aq, ao_spill[t - 2]))
        pr = t % 2
        if t < 2:
            w = list(w) + p2_aq
        rl_tok[t] = [
            B.dma("y", aq, qT_b[pr], s_qT.ap()[t], waits=w),
            B.dma("y", aq, k_b[pr], s_k.ap()[t], waits=w),
            B.dma("y", aq, v_b[pr], s_v.ap()[t], waits=w),
            B.dma("y", aq, g_b[pr], s_g.ap()[t], waits=w),
        ]

    ao_spill = {}
    B.op("v", lambda v: v.memset(s_acc[:], 0.0), waits=p2_aq)
    issue_reload(0)
    issue_reload(1)
    B.wait("t", p2_vmark[0], p2_vmark[1])
    B.wait("t", p2_smark[0], p2_smark[1])
    sz_mark = None
    epi_mark = None
    prev_numk = None
    for c in range(NCH):
        t = c // 2
        pr = t % 2
        cp = (c % 2) * 64
        if c % 2 == 0:
            B.wait("t", aq, rl_tok[t][1])
            B.wait("t", aq, rl_tok[t][2])
        for h in range(H):
            w = []
            if h == 0 and sz_mark is not None:
                w.append(sz_mark)
            if h == 0 and prev_numk is not None:
                w.append(prev_numk)
            mm(S_ap(h),
               k_b[pr][cp:cp + 64, h * 64:(h + 1) * 64],
               v_b[pr][cp:cp + 64, h * 65:h * 65 + 65],
               start=True, stop=True, waits=w,
               incs=([(pe, 1)] if h == H - 1 else []))
        supd = pe_inc()
        B.wait("v", *supd)
        B.op("v", lambda v: v.tensor_add(s_acc[:, 0:260], s_acc[:, 0:260], PF[2][:, 0:260]))
        B.op("v", lambda v: v.tensor_add(s_acc[:, 260:520], s_acc[:, 260:520], PF[3][:, 0:260]))
        sz_mark = B.mark("v")
        B.op("v", lambda v: v.tensor_copy(szsb[:, 0:260], s_acc[:, 0:260]))
        B.op("v", lambda v: v.tensor_copy(szsb[:, 260:520], s_acc[:, 260:520]))
        sz_mark2 = B.mark("v")
        B.wait("t", sz_mark2[0], sz_mark2[1])
        if epi_mark is not None:
            B.wait("t", epi_mark[0], epi_mark[1])
        if c % 2 == 0:
            B.wait("t", aq, rl_tok[t][0])
        if p3_mode < 2:
            epi_mark = sz_mark2
            if c % 2 == 1:
                scan_done[t] = (pe.n, sz_mark2)
                ao_spill[t] = rl_tok[t][3]
                if t + 2 < NT:
                    issue_reload(t + 2)
            continue
        order = list(range(0, H, 2)) + list(range(1, H, 2))
        evens_mark = None
        for i, h in enumerate(order):
            w = []
            if i == 8:
                w = [evens_mark]
            mm(num_ap(h, c),
               qT_b[pr][(h % 2) * 64:(h % 2) * 64 + 64,
                        (h // 2) * 128 + cp:(h // 2) * 128 + cp + 64],
               SZ_ap(h), start=True, stop=True, waits=w,
               incs=([(pe, 1)] if i in (7, 15) else []))
            if i == 7:
                evens_mark = pe_inc()
        numk = pe_inc()
        prev_numk = numk
        B.wait("v", *numk)
        if p3_mode < 3:
            epi_mark = (pe, numk[1]) if False else sz_mark2
            B.wait("v", *numk)
            B.op("v", lambda v: v.tensor_copy(ptmp[0:64, 0:256], NUMB[0][0:64, 0:256]))
            epi_mark = B.mark("v")
            if c % 2 == 1:
                scan_done[t] = (pe.n, epi_mark)
                ao_spill[t] = rl_tok[t][3]
                if t + 2 < NT:
                    issue_reload(t + 2)
            continue
        if c % 2 == 0:
            B.wait("v", aq, rl_tok[t][3])
        if c % 2 == 0 and t >= 2 and (t - 2) in ao_spill:
            B.wait("v", aq, ao_spill[t - 2])
        for b4 in range(4):
            bank = NUMB[b4]
            dv = bank[cp:cp + 64, 0:260].rearrange("p (h x) -> p h x", x=65)[:, :, 64:65]
            ov = rden[cp:cp + 64, b4 * 4:(b4 + 1) * 4].unsqueeze(2)
            B.op("v", lambda v, _d=dv, _o=ov: v.tensor_scalar(_o, _d, EPS_DEN, None, ALU.add))
        B.op("v", lambda v, _r=rden[cp:cp + 64, 0:16]: v.reciprocal(_r, _r))
        for b4 in range(4):
            bank = NUMB[b4]
            nv = bank[cp:cp + 64, 0:260].rearrange("p (h x) -> p h x", x=65)[:, :, 0:64]
            rv = rden[cp:cp + 64, b4 * 4:(b4 + 1) * 4].unsqueeze(2).broadcast_to([64, 4, 64])
            pt_ = ptmp[cp:cp + 64, 0:256].rearrange("p (h x) -> p h x", x=64)
            B.op("v", lambda v, _n=nv, _r=rv, _o=pt_: v.tensor_mul(_o, _n, _r))
            gv = g_b[pr][cp:cp + 64, b4 * 256:(b4 + 1) * 256]
            av = ao_b[pr][cp:cp + 64, b4 * 256:(b4 + 1) * 256]
            B.op("v", lambda v, _g=gv, _o=av, _p=ptmp[cp:cp + 64, 0:256]:
                 v.tensor_mul(_o, _p, _g))
        epi_mark = B.mark("v")
        if c % 2 == 1:
            B.wait("y", epi_mark[0], epi_mark[1])
            ao_spill[t] = B.dma("y", aq, s_ao.ap()[t], ao_b[pr])
            scan_done[t] = (pe.n, epi_mark)
            if t + 2 < NT:
                issue_reload(t + 2)
    p3_aq = aq.snapshot()
    p3_vmark = B.mark("v")

    if stop_after == "p3":
        return _finish_trunc(B, nc, [aq, wq])
    aol = [work1[:, i * 512:(i + 1) * 512].bitcast(BF16) for i in range(2)]
    aoT = work1[:, 1024:1536].bitcast(BF16)
    h1t = work1[:, 1536:2560]
    outst = [work1[:, 2560 + i * 1024:2560 + (i + 1) * 1024] for i in range(2)]
    xst4 = [work1[:, 4608 + i * 1024:4608 + (i + 1) * 1024] for i in range(2)]
    sq4 = work1[:, 6656:7680]
    xmst = [work2[:, i * 512:(i + 1) * 512].bitcast(BF16) for i in range(2)]
    B.wait("t", wq, wo_tok)
    B.wait("t", wq, wob_tok)
    B.wait("t", p3_vmark[0], p3_vmark[1])
    ao_tok = [None] * NT
    x4_tok = [None] * NT
    tpose_done_p4 = {}
    out_tok4 = {}
    h1dbg_tok = {}
    xm_sp = {}
    for t in range(2):
        w0 = [(pe, pe.n), p3_vmark] + p3_aq
        ao_tok[t] = B.dma("y", aq, aol[t % 2], s_ao.ap()[t], waits=w0)
        x4_tok[t] = B.dma("y", aq, xst4[t % 2], X_T[t], waits=w0)
    for t in range(NT):
        B.wait("t", aq, ao_tok[t])
        vm_prev = B.mark("v")
        tpose.run(aol[t % 2], lambda dc: aoT[:, dc * 128:(dc + 1) * 128], vm_prev)
        aoT_mark = B.mark("v")
        for nh in range(2):
            bank = PF[nh]
            w = [aoT_mark] if nh == 0 else []
            if nh == 0 and debug and (t - 1) in h1dbg_tok:
                wh1 = [(aq, h1dbg_tok[t - 1])]
            else:
                wh1 = []
            for dc in range(DC):
                ww = list(w) if dc == 0 else []
                mm(bank[:], aoT[:, dc * 128:(dc + 1) * 128],
                   wo_slot[:, dc * 1024 + nh * 512: dc * 1024 + nh * 512 + 512],
                   start=(dc == 0), stop=False, waits=ww)
            mm(bank[:], onesr_bf[:], BROW[:, nh * 512:nh * 512 + 512], False, True,
               incs=[(pe, 1)])
            mk = pe_inc()
            B.wait("v", *mk)
            B.op("v", lambda v, _o=h1t[:, nh * 512:nh * 512 + 512], _p=bank[:],
                 _x=xn[:, t * 1024 + nh * 512: t * 1024 + nh * 512 + 512]:
                 v.tensor_add(_o, _p, _x), waits=wh1)
        B.wait("v", aq, x4_tok[t])
        wout = [(stq, out_tok4[t - 2])] if t >= 2 else []
        B.op("v", lambda v, _o=outst[t % 2], _a=h1t[:], _b=xst4[t % 2]:
             v.tensor_add(_o, _a, _b), waits=wout)
        vm = B.mark("v")
        B.wait("y", *vm)
        out_tok4[t] = B.dma("y", stq, OUT_T[t], outst[t % 2])
        if debug:
            h1dbg_tok[t] = B.dma("y", aq, d_h1.ap().rearrange("(t p) d -> t p d", p=128)[t], h1t[:])
        if t + 2 < NT:
            wpre = [(pe, pe.n), vm]
            ao_tok[t + 2] = B.dma("y", aq, aol[t % 2], s_ao.ap()[t + 2], waits=wpre)
            x4_tok[t + 2] = B.dma("y", aq, xst4[t % 2], X_T[t + 2], waits=wpre)
        c_ss = stats[:, 6:7]
        c_r2 = stats[:, 7:8]
        B.wait("s", *vm)
        B.op("s", lambda s, _a=c_ss: s.activation(out=sq4[:], in_=h1t[:], func=AF.Square, accum_out=_a))
        B.op("s", lambda s, _o=c_ss, _i=c_ss: s.activation(out=_o, in_=_i, func=AF.Sqrt, scale=1.0 / D))
        sm = B.mark("s")
        B.wait("v", *sm)
        B.op("v", lambda v, _o=c_r2, _i=c_ss: v.reciprocal(_o, _i))
        wxm = [(pe, tpose_done_p4[t - 2])] if t >= 2 else []
        if (t - 2) in xm_sp:
            wxm.append((aq, xm_sp[t - 2]))
        B.op("v", lambda v, _o=xmst[t % 2], _r=c_r2:
             v.tensor_scalar(_o, h1t[:], _r, None, ALU.mult), waits=wxm)
        vm2 = B.mark("v")
        xm_sp[t] = B.dma("y", aq, s_xm.ap().rearrange("(t p) d -> t p d", p=128)[t],
                         xmst[t % 2], waits=[vm2])
        tpose.run(xmst[t % 2], lambda dc, _t=t: cT[:, dc * T + _t * 128: dc * T + _t * 128 + 128], vm2,
                  scale_col=lambda dc: S2C[:, dc:dc + 1])
        tpose_done_p4[t] = pe.n
    p4_vmark = B.mark("v")
    p4_out_tok = stq.snapshot()

    if stop_after == "p4":
        return _finish_trunc(B, nc, [aq, wq, stq])
    B.wait_snap("t", wq_const_tok)
    B.wait("t", p4_vmark[0], p4_vmark[1])
    for t in range(NT):
        for dc in range(DC):
            mm(PF[0][:, t * 8:(t + 1) * 8], cT[:, dc * T + t * 128: dc * T + t * 128 + 128],
               rWsb[:, dc * 8:(dc + 1) * 8], start=(dc == 0), stop=False)
        mm(PF[0][:, t * 8:(t + 1) * 8], onesr_bf[:], RRB, False, True,
           incs=([(pe, 1)] if t == NT - 1 else []))
    rmk = pe_inc()
    B.wait("v", *rmk)
    B.op("v", lambda v: v.tensor_copy(logitsA[:], PF[0][:, 0:128]))
    B.op("v", lambda v: v.memset(run_row, 0.0))
    mx = stats[:, 8:9]
    sume = stats[:, 9:10]
    i1c = stats[:, 10:11]
    i2c = stats[:, 11:12]
    rsum = stats[:, 12:13]
    cnt_pe_hist = {}
    for t in range(NT):
        blk = logitsA[:, t * 8:(t + 1) * 8]
        B.op("v", lambda v, _b=blk: v.reduce_max(mx, _b, axis=AX.X))
        B.op("v", lambda v: v.tensor_scalar(mx, mx, -1.0, None, ALU.mult))
        vm = B.mark("v")
        B.wait("s", *vm)
        B.op("s", lambda s, _b=blk: s.activation(out=_b, in_=_b, func=AF.Exp, bias=mx, accum_out=sume))
        sm = B.mark("s")
        B.wait("v", *sm)
        B.op("v", lambda v: v.reciprocal(sume, sume))
        B.op("v", lambda v, _b=blk: v.tensor_scalar(_b, _b, sume, None, ALU.mult))
        m8 = scr8c
        B.op("v", lambda v, _o=m8, _b=blk: v.max(out=_o, in_=_b))
        B.op("v", lambda v, _b=blk, _m=m8: v.tensor_single_scalar(scr8a, _b, _m[:, 0:1], ALU.is_equal))
        B.op("v", lambda v: v.tensor_scalar(scr8a, scr8a, -1000.0, 1000.0, ALU.mult, ALU.add))
        B.op("v", lambda v: v.tensor_add(scr8a, scr8a, IOTA))
        B.op("v", lambda v: v.tensor_reduce(i1c, scr8a, AX.X, ALU.min))
        B.op("v", lambda v, _b=blk, _m=m8: v.tensor_single_scalar(scr8a, _b, _m[:, 1:2], ALU.is_equal))
        B.op("v", lambda v: v.tensor_scalar(scr8a, scr8a, -1000.0, 1000.0, ALU.mult, ALU.add))
        B.op("v", lambda v: v.tensor_add(scr8a, scr8a, IOTA))
        B.op("v", lambda v: v.tensor_single_scalar(scr8b, IOTA, i1c, ALU.is_equal))
        B.op("v", lambda v: v.tensor_scalar(scr8b, scr8b, 2000.0, None, ALU.mult))
        B.op("v", lambda v: v.tensor_add(scr8a, scr8a, scr8b))
        B.op("v", lambda v: v.tensor_reduce(i2c, scr8a, AX.X, ALU.min))
        B.op("v", lambda v, _m=m8: v.tensor_add(rsum, _m[:, 0:1], _m[:, 1:2]))
        B.op("v", lambda v: v.reciprocal(rsum, rsum))
        B.op("v", lambda v, _m=m8, _o=gcols[:, t * 4:t * 4 + 1]:
             v.tensor_scalar(_o, _m[:, 0:1], rsum, None, ALU.mult))
        B.op("v", lambda v, _m=m8, _o=gcols[:, t * 4 + 1:t * 4 + 2]:
             v.tensor_scalar(_o, _m[:, 1:2], rsum, None, ALU.mult))
        B.op("v", lambda v, _o=gcols[:, t * 4 + 2:t * 4 + 3]: v.tensor_copy(_o, i1c))
        B.op("v", lambda v, _o=gcols[:, t * 4 + 3:t * 4 + 4]: v.tensor_copy(_o, i2c))
        B.op("v", lambda v, _b=blk: v.tensor_single_scalar(_b, IOTA, i1c, ALU.is_equal))
        B.op("v", lambda v: v.tensor_single_scalar(scr8b, IOTA, i2c, ALU.is_equal))
        oh_slot = OHBF if t % 2 == 0 else OHBF2
        w_oh = [(pe, cnt_pe_hist[t - 2])] if (t - 2) in cnt_pe_hist else []
        B.op("v", lambda v, _b=blk: v.tensor_add(scr8a, _b, scr8b), waits=w_oh)
        B.op("v", lambda v, _o=oh_slot: v.tensor_copy(_o, scr8a))
        vm2 = B.mark("v")
        B.wait("t", vm2[0], vm2[1])
        mm(PF[1][:, 0:8], LSTRICT, oh_slot, True, True)
        mm(PF[1][:, 16:24], onesr_f[:], run_row, True, True)
        mm(PF[1][0:1, 8:16], ONESCOL, oh_slot, True, True, incs=[(pe, 1)])
        ck = pe_inc()
        cnt_pe_hist[t] = ck[1]
        B.wait("v", *ck)
        B.op("v", lambda v: v.tensor_copy(scr8b, PF[1][:, 16:24]))
        B.op("v", lambda v, _o=cntC[:, t * 8:(t + 1) * 8]: v.tensor_add(_o, PF[1][:, 0:8], scr8b))
        B.op("v", lambda v: v.tensor_add(run_row, run_row, PF[1][0:1, 8:16]))
    vm = B.mark("v")
    B.wait("g", *vm)
    cc_wr = B.dma("g", oq, cc_in.ap(), run_row)
    B.wait("g", oq, cc_wr)

    def do_cc(g):
        g.collective_compute(
            "AllGather", ALU.bypass, replica_groups=[list(range(NCORES))],
            ins=[cc_in.ap()], outs=[cc_out.ap()]).then_inc(cc.h, 1)
    B.ops["g"].append(do_cc)
    cc.n += 1
    B.wait("g", cc, 1)
    cct = B.dma("g", oq, ccsb[:], cc_out.ap())
    B.wait("t", oq, cct)
    B.wait("t", vm[0], vm[1])
    mm(PF[1][0:1, 24:32], mcol[:], ccsb[:], True, True, incs=[(pe, 1)])
    ok = pe_inc()
    B.wait("v", *ok)
    B.op("v", lambda v: v.tensor_scalar(off_row, PF[1][0:1, 24:32], -1.0, float(CAP),
                                        ALU.mult, ALU.add))
    vm = B.mark("v")
    B.wait("t", vm[0], vm[1])
    mm(PF[1][:, 32:40], onesr_f[:], off_row, True, True, incs=[(pe, 1)])
    tk = pe_inc()
    B.wait("v", *tk)
    B.op("v", lambda v: v.tensor_copy(thr_b, PF[1][:, 32:40]))
    B.op("v", lambda v: v.tensor_scalar(krow, off_row, 0.0, None, ALU.max))
    B.op("v", lambda v: v.tensor_tensor(krow, krow, run_row, ALU.min))
    B.op("v", lambda v: v.memset(prow, 0.0))
    for m in range(15):
        B.op("v", lambda v, _m=m: v.tensor_single_scalar(et_row_tmp[:, 0:8], krow,
                                                         float(512 * _m), ALU.is_gt))
        B.op("v", lambda v: v.tensor_add(prow, prow, et_row_tmp[:, 0:8]))
    B.op("v", lambda v: v.tensor_scalar(prow, prow, 512.0, None, ALU.mult))
    B.op("v", lambda v: v.memset(lbase[:, 0:1], 0.0))
    for e in range(1, NE):
        B.op("v", lambda v, _e=e: v.tensor_add(lbase[:, _e:_e + 1], lbase[:, _e - 1:_e],
                                               prow[:, _e - 1:_e]))
    B.op("v", lambda v: v.tensor_add(endr, lbase, prow))
    B.op("v", lambda v: v.memset(et_row[:, 0:16], 0.0))
    for e in range(NE):
        B.op("v", lambda v, _e=e: v.tensor_single_scalar(
            et_row_tmp[:, 0:NSTA], ST512R[:, 0:NSTA], endr[:, _e:_e + 1], ALU.is_ge))
        B.op("v", lambda v: v.tensor_add(et_row[:, 0:NSTA], et_row[:, 0:NSTA],
                                         et_row_tmp[:, 0:NSTA]))
    B.op("v", lambda v: v.tensor_scalar(et_row[:, 0:NSTA], et_row[:, 0:NSTA], 7.0, None, ALU.min))
    vm = B.mark("v")
    B.wait("t", vm[0], vm[1])
    mm(PF[1][:, 40:40 + NSTA], onesr_f[:], et_row[:, 0:NSTA], True, True)
    mm(PF[1][:, 56:64], onesr_f[:], lbase, True, True, incs=[(pe, 1)])
    bk = pe_inc()
    B.wait("v", *bk)
    B.op("v", lambda v: v.tensor_copy(etb[:, 0:NSTA], PF[1][:, 40:40 + NSTA]))
    B.op("v", lambda v: v.tensor_copy(lbb[:], PF[1][:, 56:64]))
    sc_toks_a = []
    for t in range(NT):
        cntt = cntC[:, t * 8:(t + 1) * 8]
        g1 = gcols[:, t * 4:t * 4 + 1]
        g2 = gcols[:, t * 4 + 1:t * 4 + 2]
        i1s = gcols[:, t * 4 + 2:t * 4 + 3]
        i2s = gcols[:, t * 4 + 3:t * 4 + 4]
        B.op("v", lambda v, _i=i1s: v.tensor_single_scalar(scr8a, IOTA, _i, ALU.is_equal))
        B.op("v", lambda v, _i=i2s: v.tensor_single_scalar(scr8b, IOTA, _i, ALU.is_equal))
        B.op("v", lambda v, _c=cntt: v.tensor_tensor(scr8c, _c, thr_b, ALU.is_lt))
        B.op("v", lambda v: v.tensor_mul(scr8c, scr8c, scr8a))
        B.op("v", lambda v, _o=stats[:, 14:15]: v.reduce_sum(_o, scr8c, axis=AX.X))
        B.op("v", lambda v, _c=cntt: v.tensor_add(scr8c, _c, scr8a))
        B.op("v", lambda v: v.tensor_tensor(scr8c, scr8c, thr_b, ALU.is_lt))
        B.op("v", lambda v: v.tensor_mul(scr8c, scr8c, scr8b))
        B.op("v", lambda v, _o=stats[:, 15:16]: v.reduce_sum(_o, scr8c, axis=AX.X))
        B.op("v", lambda v, _c=cntt: v.tensor_add(scr8c, lbb[:], _c))
        B.op("v", lambda v: v.tensor_mul(scr8a, scr8a, scr8c))
        B.op("v", lambda v, _o=stats[:, 16:17]: v.reduce_sum(_o, scr8a, axis=AX.X))
        B.op("v", lambda v: v.tensor_mul(scr8b, scr8b, scr8c))
        B.op("v", lambda v, _o=stats[:, 17:18]: v.reduce_sum(_o, scr8b, axis=AX.X))
        B.op("v", lambda v, _g=g1, _o=gcA[:, 2 * t:2 * t + 1]:
             v.tensor_mul(_o, _g, stats[:, 14:15]))
        B.op("v", lambda v, _g=g2, _o=gcA[:, 2 * t + 1:2 * t + 2]:
             v.tensor_mul(_o, _g, stats[:, 15:16]))
        B.op("v", lambda v: v.tensor_mul(stats[:, 18:19], stats[:, 16:17], stats[:, 14:15]))
        B.op("v", lambda v: v.tensor_mul(stats[:, 19:20], stats[:, 17:18], stats[:, 15:16]))
        B.op("v", lambda v, _o=posA[:, 2 * t:2 * t + 1]: v.tensor_copy(_o, stats[:, 18:19]))
        B.op("v", lambda v, _o=posA[:, 2 * t + 1:2 * t + 2]: v.tensor_copy(_o, stats[:, 19:20]))
        B.op("v", lambda v: v.tensor_scalar(stats[:, 20:21], stats[:, 14:15], -1.0, 1.0, ALU.mult, ALU.add))
        B.op("v", lambda v: v.scalar_tensor_tensor(stats[:, 20:21], stats[:, 20:21], 1e7,
                                                   stats[:, 16:17], ALU.mult, ALU.add))
        B.op("v", lambda v: v.tensor_scalar(stats[:, 21:22], stats[:, 15:16], -1.0, 1.0, ALU.mult, ALU.add))
        B.op("v", lambda v: v.scalar_tensor_tensor(stats[:, 21:22], stats[:, 21:22], 1e7,
                                                   stats[:, 17:18], ALU.mult, ALU.add))
        c0 = 4 * (t % 2)
        wsc = ([(iq, sc_toks_a[2 * (t - 2)]), (iq, sc_toks_a[2 * (t - 2) + 1])]
               if t >= 2 else [])
        B.op("v", lambda v, _o=sct[:, c0:c0 + 1]: v.tensor_copy(_o, stats[:, 20:21]),
             waits=wsc)
        B.op("v", lambda v, _t=t:
             v.tensor_scalar(stats[:, 22:23], iotac[:], float(128 * _t), None, ALU.add))
        B.op("v", lambda v, _o=sct[:, c0 + 1:c0 + 2]: v.tensor_copy(_o, stats[:, 22:23]))
        B.op("v", lambda v, _o=sct[:, c0 + 2:c0 + 3]: v.tensor_copy(_o, stats[:, 21:22]))
        vmk = B.mark("v")
        sc_toks_a.append(B.dma("g", iq, s_atok.ap(), sct[:, c0 + 1:c0 + 2],
                               indirect_out=sct[:, c0:c0 + 1], bounds=rbA,
                               waits=[vmk]))
        sc_toks_a.append(B.dma("g", iq, s_atok.ap(), sct[:, c0 + 1:c0 + 2],
                               indirect_out=sct[:, c0 + 2:c0 + 3], bounds=rbA))
    if debug:
        vmD = B.mark("v")
        B.wait("y", *vmD)
        B.dma("y", aq, d_cnt.ap().rearrange("(t p) e -> p t e", p=128),
              cntC[:].rearrange("p (t e) -> p t e", e=NE))
        B.dma("y", aq, d_rows.ap()[:, 0:64], smalls[0:1, 288:352])
    p5_vmark = B.mark("v")

    if stop_after == "p5":
        return _finish_trunc(B, nc, [aq, wq, stq, oq])
    W1sb = big1[:].bitcast(BF16)
    W2sb = big2[:].bitcast(BF16)
    hTs = work1[:].bitcast(BF16)
    xg = work2[:].bitcast(BF16)
    xgT = cT[:, 0:4096]
    yst2 = [cT[:, 4096 + i * 2048:4096 + (i + 1) * 2048].bitcast(F32) for i in range(2)]
    Ycmb = [cT[:, 8192 + i * 2048:8192 + (i + 1) * 2048].bitcast(F32) for i in range(2)]

    def sparse_moe(NST, sw1, sw2, b1src, b2src, s_x, s_yv, stok, act_fn,
                   sc_toks, first_waits, scale_cols=None, drip=None):
        st_store = {}
        xfer = {}
        yev = {0: None, 1: None}
        ph2_vmark = {"m": None}

        def part_a(st):
            w_idx = [(iq, tk) for tk in xfer[st - 1]["wall"]] if st >= 1 else []
            B.op("v", lambda v, _st=st: v.scalar_tensor_tensor(
                stats[:, 24:25], etb[:, _st:_st + 1], 128.0, iotac[:], ALU.mult, ALU.add),
                waits=w_idx)
            B.op("v", lambda v: v.tensor_copy(idxw[:, 0:1], stats[:, 24:25]))
            B.op("v", lambda v, _st=st: v.tensor_copy(idxw[:, 1:2], etb[:, _st:_st + 1]))
            idx_vm = B.mark("v")
            sl = 4 * (st % 2)
            if st == 0:
                wtk = list(sc_toks)
            elif st >= 2:
                wtk = [(iq, tk) for tk in xfer[st - 2]["xg"]]
            else:
                wtk = []
            t_ti = B.dma("g", iq, tik[:, sl:sl + 4],
                         stok.ap()[st * 512:(st + 1) * 512].rearrange(
                             "(p j) one -> p (j one)", p=128),
                         waits=wtk)
            xg_toks = []
            for jj in range(4):
                wxg = [(iq, t_ti)]
                if st >= 1:
                    wxg.append((pe, xfer[st - 1]["tp"]))
                else:
                    wxg.extend(first_waits.get("xg", []))
                xg_toks.append(B.dma("g", iq, xg[:, jj * 1024:(jj + 1) * 1024], s_x.ap(),
                                     indirect_in=tik[:, sl + jj:sl + jj + 1], waits=wxg))
            t_w1q = []
            for q in range(4):
                if st == 0:
                    wwq = [idx_vm] + first_waits.get("w1", [])
                else:
                    wwq = [idx_vm, (pe, pe.n + 4 * (q + 1))]
                t_w1q.append(B.dma("g", iq, W1sb[:, q * 8192:(q + 1) * 8192],
                                   sw1, indirect_in=idxw[:, 0:1], waits=wwq,
                                   elem_off=q * 8192))
            xfer[st] = dict(xg=xg_toks, w1=t_w1q, idx_vm=idx_vm, wall=list(t_w1q))

        def part_b(st):
            idx_vm = xfer[st]["idx_vm"]
            gate = first_waits.get("w2", []) if st == 0 else [(pe, pe.n)]
            wb = [idx_vm] + gate
            t_b1 = B.dma("g", iq, b1g[:], b1src, indirect_in=idxw[:, 0:1], waits=wb)
            t_b2 = B.dma("g", iq, smalls[0:2, 640:1152].bitcast(BF16), b2src,
                         indirect_in=idxw[0:2, 1:2], waits=wb)
            t_w2 = [B.dma("g", iq, W2sb[:, part * 8192:(part + 1) * 8192],
                          sw2, indirect_in=idxw[:, 0:1], waits=list(wb),
                          elem_off=part * 8192) for part in range(4)]
            xfer[st]["b1"], xfer[st]["b2"], xfer[st]["w2"] = t_b1, t_b2, t_w2
            xfer[st]["wall"] += t_w2 + [t_b1, t_b2]

        def transposes(st):
            for jj in range(4):
                tpose.run(xg[:, jj * 1024:(jj + 1) * 1024],
                          lambda dc, _j=jj: xgT[:, dc * 512 + _j * 128:
                                               dc * 512 + _j * 128 + 128],
                          (iq, xfer[st]["xg"][jj]),
                          scale_col=(None if scale_cols is None else
                                     (lambda dc: scale_cols[:, dc:dc + 1])))
            xfer[st]["tp"] = pe.n
            xfer[st]["tp_vm"] = B.mark("v")

        part_a(0)
        part_b(0)
        transposes(0)
        for st in range(NST):
            if st + 1 < NST:
                part_a(st + 1)
            t_w1q = xfer[st]["w1"]
            tp_vm = xfer[st]["tp_vm"]
            smk0 = B.mark("s")
            gel_mark = {0: smk0, 1: smk0}
            B.wait("t", iq, t_w1q[0])
            B.wait("t", tp_vm[0], tp_vm[1])
            for hcg in range(16):
                if hcg % 4 == 0 and hcg > 0:
                    B.wait("t", iq, t_w1q[hcg // 4])
                pr2 = hcg % 2
                banks = (PF[0], PF[1]) if pr2 == 0 else (PF[2], PF[3])
                for dc in range(DC):
                    for j in range(2):
                        hc = hcg * 2 + j
                        h, lh = hc // 8, hc % 8
                        w = []
                        if dc == 0 and j == 0:
                            w.append(gel_mark[pr2])
                            if hcg == 1 and ph2_vmark["m"] is not None:
                                w.append(ph2_vmark["m"])
                        mm(banks[j][:],
                           W1sb[:, h * 8192 + dc * 1024 + lh * 128:
                                h * 8192 + dc * 1024 + (lh + 1) * 128],
                           xgT[:, dc * 512: dc * 512 + 512],
                           start=(dc == 0), stop=(dc == DC - 1), waits=w,
                           incs=([(pe, 1)] if (dc == DC - 1 and j == 1) else []))
                mk = pe_inc()
                B.wait("s", *mk)
                if hcg == 0:
                    B.wait("s", iq, xfer[st]["b1"])
                for j in range(2):
                    hc = hcg * 2 + j
                    B.op("s", lambda s, _o=hTs[:, hc * 512:(hc + 1) * 512],
                         _p=banks[j][:], _b=b1g[:, hc:hc + 1], _f=act_fn:
                         s.activation(out=_o, in_=_p, func=_f, bias=_b))
                gel_mark[pr2] = B.mark("s")
            if st + 1 < NST:
                transposes(st + 1)
            smk = B.mark("s")
            B.wait("t", smk[0], smk[1])
            for _tk in xfer[st]["w2"]:
                B.wait("t", iq, _tk)
            B.wait("t", iq, xfer[st]["b2"])
            for tt in range(4):
                ybanks = [PF[4][:], PF[5][:]] if tt % 2 == 0 else [PF[2][:], PF[3][:]]
                for nh in range(2):
                    bank = ybanks[nh]
                    for hc in range(HC):
                        w = []
                        if hc == 0 and nh == 0 and yev[tt % 2] is not None:
                            w.append(yev[tt % 2])
                        mm(bank,
                           hTs[:, hc * 512 + tt * 128: hc * 512 + tt * 128 + 128],
                           W2sb[:, hc * 1024 + nh * 512: hc * 1024 + nh * 512 + 512],
                           start=(hc == 0), stop=False, waits=w)
                    mm(bank, onesr_bf[:], BROW[:, nh * 512:nh * 512 + 512],
                       False, True, incs=([(pe, 1)] if nh == 1 else []))
                yk = pe_inc()
                B.wait("v", *yk)
                ww = []
                prevk = st_store.get((st, tt - 2)) or st_store.get((st - 1, tt + 2))
                if prevk is not None:
                    ww.append((aq, prevk))
                B.op("v", lambda v, _o=yst2[tt % 2][:, 0:512], _p=ybanks[0]:
                     v.tensor_copy(_o, _p), waits=ww)
                B.op("v", lambda v, _o=yst2[tt % 2][:, 512:1024], _p=ybanks[1]:
                     v.tensor_copy(_o, _p))
                yev[tt % 2] = B.mark("v")
                st_store[(st, tt)] = B.dma(
                    "y", aq,
                    s_yv.ap()[st * 512:(st + 1) * 512].rearrange("(p j) d -> j p d", j=4)[tt],
                    yst2[tt % 2][:, 0:1024], waits=[yev[tt % 2]])
            ph2_vmark["m"] = B.mark("v")
            if st + 1 < NST:
                part_b(st + 1)
            if drip is not None:
                drip(st)
        return aq.snapshot()

    if debug:
        dbg_toks.append((aq, h1dbg_tok[NT - 1]))
        for tok in dbg_toks:
            B.wait("s", aq, tok[1])
            B.wait("v", aq, tok[1])
    B.wait_snap("g", p4_out_tok)
    p4_aq_snap = aq.snapshot()
    B.wait_snap("g", p4_aq_snap)
    b1sa = ab1p.rearrange("e p c -> (e p) c")
    sw1a = aW1p.rearrange("e p c -> (e p) c")
    sw2a = aW2p.rearrange("e p c -> (e p) c")
    ay_snap = sparse_moe(NSTA, sw1a, sw2a, b1sa, ab2p, s_xm, s_ay, s_atok,
                         AF.Gelu,
                         iq.snapshot(),
                         {"w1": [p4_vmark] + [aq.tok(tk[1]) for tk in dbg_toks],
                          "w2": [(pe, pe.n)],
                          "xg": [(pe, pe.n)]}, scale_cols=S2C)
    moe_pe = (pe, pe.n)

    xst7 = [work2[:, i * 512:(i + 1) * 512].bitcast(BF16) for i in range(2)]
    rt_tok = {}
    rt_pe = {}
    for t in range(2):
        rt_tok[t] = B.dma("y", aq, xst7[t % 2], s_xnT.ap()[t], waits=[moe_pe])
    for t in range(NT):
        B.wait("t", aq, rt_tok[t])
        for dc in range(DC):
            mm(PF[0][:, t * 8:(t + 1) * 8], xst7[t % 2][:, dc * 128:(dc + 1) * 128],
               gWsb[:, dc * 8:(dc + 1) * 8], start=(dc == 0), stop=False)
        mm(PF[0][:, t * 8:(t + 1) * 8], onesr_bf[:], RGB, False, False)
        mm(PF[0][:, t * 8:(t + 1) * 8], onesr_bf[:], REB, False, True,
           incs=[(pe, 1)])
        rt_pe[t] = pe_inc()[1]
        if t + 2 < NT:
            rt_tok[t + 2] = B.dma("y", aq, xst7[t % 2], s_xnT.ap()[t + 2],
                                  waits=[(pe, rt_pe[t])])
    gk = (pe, pe.n)
    B.wait("v", *gk)
    B.op("v", lambda v: v.tensor_copy(logitsA[:], PF[0][:, 0:128]))
    B.op("v", lambda v: v.memset(run_row, 0.0))
    cnt_pe_hist2 = {}
    for t in range(NT):
        blk = logitsA[:, t * 8:(t + 1) * 8]
        B.op("v", lambda v, _b=blk: v.reduce_max(mx, _b, axis=AX.X))
        B.op("v", lambda v, _b=blk: v.tensor_single_scalar(scr8a, _b, mx, ALU.is_equal))
        B.op("v", lambda v: v.tensor_scalar(scr8a, scr8a, -1000.0, 1000.0, ALU.mult, ALU.add))
        B.op("v", lambda v: v.tensor_add(scr8a, scr8a, IOTA))
        B.op("v", lambda v: v.tensor_reduce(i1c, scr8a, AX.X, ALU.min))
        B.op("v", lambda v, _o=ohF[:, t * 8:(t + 1) * 8]:
             v.tensor_single_scalar(_o, IOTA, i1c, ALU.is_equal))
        oh_slot = OHBF if t % 2 == 0 else OHBF2
        w_oh = [(pe, cnt_pe_hist2[t - 2])] if (t - 2) in cnt_pe_hist2 else []
        B.op("v", lambda v, _b=ohF[:, t * 8:(t + 1) * 8], _o=oh_slot:
             v.tensor_copy(_o, _b), waits=w_oh)
        vm2 = B.mark("v")
        B.wait("t", vm2[0], vm2[1])
        mm(PF[1][:, 0:8], LSTRICT, oh_slot, True, True)
        mm(PF[1][:, 16:24], onesr_f[:], run_row, True, True)
        mm(PF[1][0:1, 8:16], ONESCOL, oh_slot, True, True, incs=[(pe, 1)])
        ck = pe_inc()
        cnt_pe_hist2[t] = ck[1]
        B.wait("v", *ck)
        B.op("v", lambda v: v.tensor_copy(scr8b, PF[1][:, 16:24]))
        B.op("v", lambda v, _o=cntC[:, t * 8:(t + 1) * 8]: v.tensor_add(_o, PF[1][:, 0:8], scr8b))
        B.op("v", lambda v: v.tensor_add(run_row, run_row, PF[1][0:1, 8:16]))
    p7_vmark = B.mark("v")
    if debug:
        B.wait("y", *p7_vmark)
        B.dma("y", aq, d_ohf.ap().rearrange("(t p) e -> p t e", p=128),
              ohF[:].rearrange("p (t e) -> p t e", e=NE))
    B.op("v", lambda v: v.tensor_copy(krow, run_row))
    B.op("v", lambda v: v.memset(prow, 0.0))
    for m in range(11):
        B.op("v", lambda v, _m=m: v.tensor_single_scalar(et_row_tmp[:, 0:8], krow,
                                                         float(512 * _m), ALU.is_gt))
        B.op("v", lambda v: v.tensor_add(prow, prow, et_row_tmp[:, 0:8]))
    B.op("v", lambda v: v.tensor_scalar(prow, prow, 512.0, None, ALU.mult))
    B.op("v", lambda v: v.memset(lbase[:, 0:1], 0.0))
    for e in range(1, NE):
        B.op("v", lambda v, _e=e: v.tensor_add(lbase[:, _e:_e + 1], lbase[:, _e - 1:_e],
                                               prow[:, _e - 1:_e]))
    B.op("v", lambda v: v.tensor_add(endr, lbase, prow))
    B.op("v", lambda v: v.memset(et_row[:, 0:16], 0.0))
    for e in range(NE):
        B.op("v", lambda v, _e=e: v.tensor_single_scalar(et_row_tmp[:, 0:NSTF],
             ST512R[:, 0:NSTF], endr[:, _e:_e + 1], ALU.is_ge))
        B.op("v", lambda v: v.tensor_add(et_row[:, 0:NSTF], et_row[:, 0:NSTF],
                                         et_row_tmp[:, 0:NSTF]))
    B.op("v", lambda v: v.tensor_scalar(et_row[:, 0:NSTF], et_row[:, 0:NSTF], 7.0, None, ALU.min))
    vm = B.mark("v")
    B.wait("t", vm[0], vm[1])
    mm(PF[1][:, 40:40 + NSTF], onesr_f[:], et_row[:, 0:NSTF], True, True)
    mm(PF[1][:, 56:64], onesr_f[:], lbase, True, True, incs=[(pe, 1)])
    bk = pe_inc()
    B.wait("v", *bk)
    B.op("v", lambda v: v.tensor_copy(etb[:, 0:NSTF], PF[1][:, 40:40 + NSTF]))
    B.op("v", lambda v: v.tensor_copy(lbb[:], PF[1][:, 56:64]))
    sc_toks_f = []
    for t in range(NT):
        cntt = cntC[:, t * 8:(t + 1) * 8]
        oht = ohF[:, t * 8:(t + 1) * 8]
        B.op("v", lambda v, _c=cntt: v.tensor_add(scr8c, lbb[:], _c))
        B.op("v", lambda v, _oh=oht: v.tensor_mul(scr8c, scr8c, _oh))
        B.op("v", lambda v, _o=stats[:, 16:17]: v.reduce_sum(_o, scr8c, axis=AX.X))
        B.op("v", lambda v, _o=posF[:, t:t + 1]: v.tensor_copy(_o, stats[:, 16:17]))
        c0 = 2 * (t % 4)
        wsc = [(iq, sc_toks_f[t - 4])] if t >= 4 else []
        B.op("v", lambda v, _o=sct[:, c0:c0 + 1]: v.tensor_copy(_o, stats[:, 16:17]),
             waits=wsc)
        B.op("v", lambda v, _t=t:
             v.tensor_scalar(stats[:, 22:23], iotac[:], float(128 * _t), None, ALU.add))
        B.op("v", lambda v, _o=sct[:, c0 + 1:c0 + 2]: v.tensor_copy(_o, stats[:, 22:23]))
        vmk = B.mark("v")
        sc_toks_f.append(B.dma("g", iq, s_ftok.ap(), sct[:, c0 + 1:c0 + 2],
                               indirect_out=sct[:, c0:c0 + 1], bounds=rbF,
                               waits=[vmk]))
    b1sf = fb1p.rearrange("e p c -> (e p) c")
    B.wait_snap("g", ay_snap)
    cmb_tok = {}
    cmb_pe0 = pe.n
    cmb_state = {"next": 0}

    def drip_attn_combine(st):
        goal = min(NT, (NT * (st + 1) + NSTF - 1) // NSTF)
        while cmb_state["next"] < goal:
            t = cmb_state["next"]
            cmb_state["next"] = t + 1
            wY = [(pe, cmb_pe0)] if t == 0 else [(oq, cmb_tok[t - 1])]
            ty1 = B.dma("g", iq, Ycmb[0][:, 0:1024], s_ay.ap(),
                        indirect_in=posA[:, 2 * t:2 * t + 1], waits=wY)
            ty2 = B.dma("g", iq, Ycmb[1][:, 0:1024], s_ay.ap(),
                        indirect_in=posA[:, 2 * t + 1:2 * t + 2], waits=[])
            B.op("v", lambda v, _t=t, _g=gcA[:, 2 * t:2 * t + 1]:
                 v.tensor_scalar(Ycmb[0][:, 0:1024], Ycmb[0][:, 0:1024], _g, None,
                                 ALU.mult),
                 waits=[(iq, ty1), (iq, ty2)])
            B.op("v", lambda v, _t=t, _g=gcA[:, 2 * t + 1:2 * t + 2]:
                 v.scalar_tensor_tensor(Ycmb[0][:, 0:1024], Ycmb[1][:, 0:1024], _g,
                                        Ycmb[0][:, 0:1024], ALU.mult, ALU.add))
            cvm = B.mark("v")
            cmb_tok[t] = B.dma("g", oq, OUT_T[t], Ycmb[0][:, 0:1024], accum=ALU.add,
                               waits=[cvm])

    sw1f = fW1p.rearrange("e p c -> (e p) c")
    sw2f = fW2p.rearrange("e p c -> (e p) c")
    fy_snap = sparse_moe(NSTF, sw1f, sw2f, b1sf, fb2p, s_xn, s_fy, s_ftok,
                         AF.Relu,
                         iq.snapshot(),
                         {"w1": [(pe, pe.n)], "w2": [(pe, pe.n)],
                          "xg": [(pe, pe.n)]}, drip=drip_attn_combine)
    B.wait_snap("g", fy_snap)
    for t in range(NT):
        wY = [(pe, pe.n), (oq, cmb_tok[NT - 1])] if t == 0 else []
        if t >= 1:
            wY.append((oq, cmb_tok[("f", t - 1)]))
        ty1 = B.dma("g", iq, Ycmb[0][:, 0:1024], s_fy.ap(),
                    indirect_in=posF[:, t:t + 1], waits=wY)
        cmb_tok[("f", t)] = B.dma("g", oq, OUT_T[t], Ycmb[0][:, 0:1024], accum=ALU.add,
                                  waits=[(iq, ty1)])

    B.wait_snap("y", stq.snapshot())
    B.wait_snap("y", oq.snapshot())
    B.wait_snap("g", oq.snapshot())

    return _emit(B, nc)


def _finish_trunc(B, nc, qs):
    for q in qs:
        B.wait_snap("y", q.snapshot())
        B.wait_snap("g", q.snapshot())
    return _emit(B, nc)


def _emit(B, nc):
    with B.stack:
        with nc.Block() as block:
            @block.tensor
            def _(t_):
                for f in B.ops["t"]:
                    f(t_)

            @block.vector
            def _(v):
                for f in B.ops["v"]:
                    f(v)

            @block.scalar
            def _(s):
                for f in B.ops["s"]:
                    f(s)

            @block.gpsimd
            def _(g):
                for f in B.ops["g"]:
                    f(g)

            @block.sync
            def _(y):
                for f in B.ops["y"]:
                    f(y)
    return B


def kernel(**inputs):
    c = make_constants()
    B = build(debug=False)
    xfull = np.ascontiguousarray(np.asarray(inputs["L0_attn_x"], np.float32))

    def row(v):
        return np.ascontiguousarray(np.asarray(v, np.float32).reshape(1, -1))

    def arr(v):
        return np.ascontiguousarray(np.asarray(v, np.float32))

    shared = dict(
        Wq=arr(inputs["L0_attn_Wq"]), bq=row(inputs["L0_attn_bq"]),
        Wk=arr(inputs["L0_attn_Wk"]), bk=row(inputs["L0_attn_bk"]),
        Wv=arr(inputs["L0_attn_Wv"]), bv=row(inputs["L0_attn_bv"]),
        Wg=arr(inputs["L0_attn_Wg"]), bg=row(inputs["L0_attn_bg"]),
        Wo=arr(inputs["L0_attn_Wo"]), bo=row(inputs["L0_attn_bo"]),
        rW=np.ascontiguousarray(np.asarray(inputs["L0_attn_router_W"], np.float32)
                                .reshape(DC, 128, NE).transpose(1, 0, 2)),
        rb=row(inputs["L0_attn_router_b"]),
        aW1p=np.ascontiguousarray(np.asarray(inputs["L0_attn_W1"], np.float32)
                                  .reshape(NE, DC, 128, 4, HID // 4)
                                  .transpose(0, 2, 3, 1, 4)
                                  .reshape(NE, 128, DC * HID)),
        aW2p=np.ascontiguousarray(np.asarray(inputs["L0_attn_W2"], np.float32)
                                  .reshape(NE, HC, 128, D).transpose(0, 2, 1, 3)
                                  .reshape(NE, 128, HC * D)),
        ab1p=np.ascontiguousarray(np.asarray(inputs["L0_attn_b1"], np.float32)
                                  .reshape(NE, HID // 128, 128).transpose(0, 2, 1)),
        ab2p=arr(inputs["L0_attn_b2"]),
        s1c=np.ascontiguousarray(np.asarray(inputs["L0_attn_norm1_scale"], np.float32).reshape(DC, 128).T),
        s2c=np.ascontiguousarray(np.asarray(inputs["L0_attn_norm2_scale"], np.float32).reshape(DC, 128).T),
        gwT=np.ascontiguousarray(np.asarray(inputs["L0_ffn_gate_w"], np.float32).T
                                 .reshape(DC, 128, NE).transpose(1, 0, 2)),
        gbr=row(inputs["L0_ffn_gate_b"]), ebr=row(inputs["L0_ffn_expert_bias"]),
        fW1p=np.ascontiguousarray(np.asarray(inputs["L0_ffn_w1"], np.float32)
                                  .reshape(NE, DC, 128, 4, HID // 4)
                                  .transpose(0, 2, 3, 1, 4)
                                  .reshape(NE, 128, DC * HID)),
        fW2p=np.ascontiguousarray(np.asarray(inputs["L0_ffn_w2"], np.float32)
                                  .reshape(NE, HC, 128, D).transpose(0, 2, 1, 3)
                                  .reshape(NE, 128, HC * D)),
        fb1p=np.ascontiguousarray(np.asarray(inputs["L0_ffn_b1"], np.float32)
                                  .reshape(NE, HID // 128, 128).transpose(0, 2, 1)),
        fb2p=arr(inputs["L0_ffn_b2"]),
        ident=c["ident"], lstrict=c["lstrict"], ones_bf=c["ones_bf"], ones_f=c["ones_f"],
        onescol_bf=c["onescol_bf"], iota8=c["iota8"], iotac=c["iotac"],
        st512=c["st512"], cosx=c["cosx"], sinx=c["sinx"],
    )
    in_maps = []
    for i in range(NCORES):
        m = dict(shared)
        m["x"] = np.ascontiguousarray(xfull[i])
        mc = np.zeros((NE, 1), np.float32)
        mc[:i] = 1.0
        m["maskcore"] = mc
        in_maps.append(m)
    res = run_bass_kernel_spmd(B.nc, in_maps, core_ids=list(range(NCORES)))
    return np.stack([res.results[i]["out"] for i in range(NCORES)], axis=0)
